# version 14
# speedup vs baseline: 1.2447x; 1.0032x over previous
from contextlib import ExitStack

import numpy as np
import ml_dtypes
import concourse.bass as bass
import concourse.mybir as mybir
from concourse.bass_utils import run_bass_kernel_spmd

F32 = mybir.dt.float32
BF16 = mybir.dt.bfloat16
ACT = mybir.ActivationFunctionType
ALU = mybir.AluOpType
AX = mybir.AxisListType

D = 1024
T_OWN = 2048
T_PREF = 6144
NT = T_OWN // 128
SCL = 512
NSC_PRE = T_PREF // SCL
NSC_OWN = T_OWN // SCL
NSC = NSC_PRE + NSC_OWN
XTW = 513
EPS = 1e-5

N_PHASES = 4


def _ap(t, off, dims):
    return bass.AP(t, off, dims)



def _clear_sems(nc, sems):
    with nc.Block() as blk:
        @blk.gpsimd
        def _(gp):
            for s in sems.values():
                gp.sem_clear(s)


def emit_phase_a(nc, es, g):
    x_ext, wwT, wrT = g["x_ext"], g["wwT"], g["wrT"]
    diffsT, causalT = g["diffsT"], g["causalT"]
    stream, ident_b, scal = g["stream"], g["ident_b"], g["scal"]

    wwT_sb = es.enter_context(nc.sbuf_tensor("wwT_sb", [128, 8 * D], BF16))
    wrT_sb = es.enter_context(nc.sbuf_tensor("wrT_sb", [128, 8 * D], BF16))
    state_bf = es.enter_context(nc.sbuf_tensor("state_bf", [128, 8 * D], BF16))
    xT = es.enter_context(nc.sbuf_tensor("xT", [128, 8 * XTW], BF16))
    xbf = es.enter_context(nc.sbuf_tensor("xbf", [128, 2 * 4 * D], BF16))
    wkbf = es.enter_context(nc.sbuf_tensor("wkbf", [128, 2 * 4 * D], BF16))
    xf32 = es.enter_context(nc.sbuf_tensor("xf32", [128, 2 * D], F32))
    vgw = es.enter_context(nc.sbuf_tensor("vgw", [128, 4 * D], BF16))
    vtm = es.enter_context(nc.sbuf_tensor("vtm", [128, 4 * D], BF16))
    mt = es.enter_context(nc.sbuf_tensor("mt", [128, 4 * SCL], F32))
    smt = es.enter_context(nc.sbuf_tensor("smt", [128, 4 * SCL], BF16))
    inter_sb = es.enter_context(nc.sbuf_tensor("inter_sb", [128, D], F32))
    reads_sb = es.enter_context(nc.sbuf_tensor("reads_sb", [128, D], BF16))
    readsT_sb = es.enter_context(nc.sbuf_tensor("readsT_sb", [128, D], BF16))
    epi = es.enter_context(nc.sbuf_tensor("epi", [128, 4 * D], F32))
    lnst = es.enter_context(nc.sbuf_tensor("lnst", [128, 16], F32))

    pv = es.enter_context(nc.psum_tensor("pv", [128, 1024], F32))
    pio_a = es.enter_context(nc.psum_tensor("pio_a", [128, 1024], F32))
    pio_b = es.enter_context(nc.psum_tensor("pio_b", [128, 1024], F32))
    pS = es.enter_context(nc.psum_tensor("pS", [128, 512], F32))
    ptr = es.enter_context(nc.psum_tensor("ptr", [128, 1024], BF16))

    sems = {}
    for n in ["LD0", "LD1", "XF0", "XF1", "MT", "TTR", "TEV", "VMM", "VEV",
              "SMM", "SMK", "IMM", "NMM", "REV", "RCB", "RTT", "RTE", "ROM",
              "EPS_", "EPV", "LNV", "SQS", "VAE", "SQT", "AMM", "AUP"]:
        sems[n] = es.enter_context(nc.semaphore("a_" + n))

    with nc.Block() as blk:
        @blk.gpsimd
        def _(gp):
            gp.dma_start(out=wwT_sb[:, :].rearrange("p (b e) -> p b e", b=8),
                         in_=wwT.rearrange("(b p) e -> p b e", p=128)).then_inc(sems["LD0"], 16)
            gp.dma_start(out=wrT_sb[:, :].rearrange("p (b e) -> p b e", b=8),
                         in_=wrT.rearrange("(b p) e -> p b e", p=128)).then_inc(sems["LD0"], 16)
            gp.dma_start(out=smt[:, :].rearrange("k (kt p) -> k kt p", kt=4),
                         in_=causalT.rearrange("(kt k) p -> k kt p", k=128)).then_inc(sems["LD0"], 16)

        @blk.sync
        def _(sy):
            sy.dma_start(out=mt[:, :].rearrange("k (kt p) -> k kt p", kt=4),
                         in_=diffsT.rearrange("(kt k) p -> k kt p", k=128)).then_inc(sems["MT"], 16)

        @blk.scalar
        def _(s):
            s.wait_ge(sems["LD0"], 48)
            s.wait_ge(sems["MT"], 16)
            for kt in range(4):
                s.activation(mt[:, kt * SCL:(kt + 1) * SCL],
                             mt[:, kt * SCL:(kt + 1) * SCL], ACT.Exp,
                             scale=scal[:, 2:3]).then_inc(sems["TEV"], 1)

        @blk.vector
        def _(v):
            v.memset(state_bf[:, :], 0.0)
            v.memset(xT[:, :], 0.0)
            v.wait_ge(sems["TEV"], 4)
            for kt in range(4):
                v.tensor_mul(mt[:, kt * SCL:(kt + 1) * SCL],
                             mt[:, kt * SCL:(kt + 1) * SCL],
                             smt[:, kt * SCL:(kt + 1) * SCL])

    Z = dict(LD0=48, LD1=0, XF0=0, XF1=0, TTR=0, TEV=4, VMM=0, VEV=0, SMM=0,
             SMK=0, IMM=0, NMM=0, REV=0, RCB=0, RTT=0, RTE=0, ROM=0,
             EPS_=0, EPV=0, LNV=0, SQS=0, VAE=0, SQT=0, AMM=0, AUP=0)
    bases = []
    for sc in range(NSC):
        bases.append(dict(Z))
        own = sc >= NSC_PRE
        Z["LD0" if sc % 2 == 0 else "LD1"] += 128
        Z["TTR"] += 4
        Z["TEV"] += 4
        Z["VMM"] += 4
        Z["VEV"] += 4
        Z["AMM"] += 8
        Z["AUP"] += 8
        if own:
            Z["XF0"] += 32
            Z["XF1"] += 32
            for nm in ["SMM", "SMK", "IMM", "NMM", "REV", "RCB", "RTT",
                       "RTE", "ROM", "EPS_", "EPV", "LNV", "SQS", "VAE", "SQT"]:
                Z[nm] += 4

    with nc.Block() as blk:

        @blk.gpsimd
        def _(gp):
            for sc in range(NSC):
                base = bases[sc]
                po = (sc % 2) * 4 * D
                if sc >= 2:
                    gp.wait_ge(sems["AMM"], bases[sc - 1]["AMM"])
                for t in range(4):
                    r = 1 + sc * SCL + t * 128
                    gp.dma_start(out=xbf[:, po + t * D:po + (t + 1) * D],
                                 in_=x_ext[r:r + 128, :]).then_inc(
                                     sems["LD0" if sc % 2 == 0 else "LD1"], 16)
                    gp.dma_start(out=wkbf[:, po + t * D:po + (t + 1) * D],
                                 in_=x_ext[r - 1:r + 127, :]).then_inc(
                                     sems["LD0" if sc % 2 == 0 else "LD1"], 16)

        @blk.sync
        def _(sy):
            for sc in range(NSC):
                base = bases[sc]
                if sc < NSC_PRE:
                    continue
                for t in range(4):
                    r = 1 + sc * SCL + t * 128
                    gi = (sc - NSC_PRE) * 4 + t
                    if gi >= 2:
                        sy.wait_ge(sems["EPV"], gi - 1)
                    sy.dma_start(out=xf32[:, (t % 2) * D:(t % 2 + 1) * D],
                                 in_=x_ext[r:r + 128, :]).then_inc(
                                     sems["XF0" if t % 2 == 0 else "XF1"], 16)

        @blk.tensor
        def _(tn):
            for sc in range(NSC):
                base = bases[sc]
                own = sc >= NSC_PRE
                po = (sc % 2) * 4 * D
                ldk = "LD0" if sc % 2 == 0 else "LD1"
                tn.wait_ge(sems[ldk], base[ldk] + 128)
                tn.wait_ge(sems["AUP"], base["AUP"])
                tn.wait_ge(sems["RTE"], base["RTE"])
                for t in range(4):
                    tn.wait_ge(sems["TEV"], base["TEV"] + t)
                    for jb in range(8):
                        ins = tn.transpose(ptr[:, jb * 128:(jb + 1) * 128],
                                           xbf[:, po + t * D + jb * 128:po + t * D + (jb + 1) * 128],
                                           ident_b[:, :])
                        if jb == 7:
                            ins.then_inc(sems["TTR"], 1)
                    tn.wait_ge(sems["TEV"], base["TEV"] + t + 1)
                    tn.wait_ge(sems["VEV"], base["VEV"] + t)
                    for hh in range(2):
                        for jb in range(8):
                            ins = tn.matmul(pv[:, hh * 512:(hh + 1) * 512],
                                            xT[:, jb * XTW + 1 + t * 128: jb * XTW + 1 + (t + 1) * 128],
                                            wwT_sb[:, jb * D + hh * 512: jb * D + (hh + 1) * 512],
                                            start=(jb == 0), stop=(jb == 7))
                    ins.then_inc(sems["VMM"], 1)
                if own:
                    for kt in range(4):
                        tn.wait_ge(sems["SMK"], base["SMK"] + kt)
                        n = 512 - kt * 128
                        for jb in range(8):
                            ins = tn.matmul(pS[:, 0:n],
                                            xT[:, jb * XTW + kt * 128: jb * XTW + (kt + 1) * 128],
                                            xT[:, jb * XTW + 1 + kt * 128: jb * XTW + 513],
                                            start=(jb == 0), stop=(jb == 7))
                        ins.then_inc(sems["SMM"], 1)
                    tn.wait_ge(sems["VEV"], base["VEV"] + 4)
                    tn.wait_ge(sems["AUP"], base["AUP"])
                    for pt in range(4):
                        tn.wait_ge(sems["REV"], base["REV"] + pt)
                        for hh in range(2):
                            for jb in range(8):
                                ins = tn.matmul(pio_a[:, hh * 512:(hh + 1) * 512],
                                                xT[:, jb * XTW + 1 + pt * 128: jb * XTW + 1 + (pt + 1) * 128],
                                                state_bf[:, jb * D + hh * 512: jb * D + (hh + 1) * 512],
                                                start=(jb == 0), stop=(jb == 7))
                        ins.then_inc(sems["IMM"], 1)
                        tn.wait_ge(sems["SMK"], base["SMK"] + pt + 1)
                        tn.wait_ge(sems["RCB"], base["RCB"] + pt)
                        for hh in range(2):
                            for kt in range(pt + 1):
                                ins = tn.matmul(pio_b[:, hh * 512:(hh + 1) * 512],
                                                smt[:, kt * SCL + pt * 128: kt * SCL + (pt + 1) * 128],
                                                vtm[:, kt * D + hh * 512: kt * D + (hh + 1) * 512],
                                                start=(kt == 0), stop=(kt == pt))
                        ins.then_inc(sems["NMM"], 1)
                        tn.wait_ge(sems["RCB"], base["RCB"] + pt + 1)
                        tn.wait_ge(sems["RTE"], base["RTE"] + pt)
                        for jb in range(8):
                            ins = tn.transpose(ptr[:, jb * 128:(jb + 1) * 128],
                                               reads_sb[:, jb * 128:(jb + 1) * 128],
                                               ident_b[:, :])
                        ins.then_inc(sems["RTT"], 1)
                        tn.wait_ge(sems["RTE"], base["RTE"] + pt + 1)
                        tn.wait_ge(sems["EPS_"], base["EPS_"] + pt)
                        for hh in range(2):
                            for jb in range(8):
                                ins = tn.matmul(pv[:, hh * 512:(hh + 1) * 512],
                                                readsT_sb[:, jb * 128:(jb + 1) * 128],
                                                wrT_sb[:, jb * D + hh * 512: jb * D + (hh + 1) * 512],
                                                start=(jb == 0), stop=(jb == 7))
                        ins.then_inc(sems["ROM"], 1)
                for et in range(8):
                    pA = pv if et % 2 == 0 else pio_a
                    if et == 0:
                        if own:
                            tn.wait_ge(sems["EPS_"], base["EPS_"] + 4)
                        else:
                            tn.wait_ge(sems["VEV"], base["VEV"] + 4)
                    if et == 1:
                        if own:
                            tn.wait_ge(sems["REV"], base["REV"] + 4)
                        else:
                            tn.wait_ge(sems["AUP"], base["AUP"])
                    if et >= 2:
                        tn.wait_ge(sems["AUP"], base["AUP"] + et - 1)
                    for hh in range(2):
                        for t in range(4):
                            ins = tn.matmul(pA[:, hh * 512:(hh + 1) * 512],
                                            wkbf[:, po + t * D + et * 128: po + t * D + (et + 1) * 128],
                                            vgw[:, t * D + hh * 512: t * D + (hh + 1) * 512],
                                            start=(t == 0), stop=(t == 3))
                    ins.then_inc(sems["AMM"], 1)

        @blk.scalar
        def _(s):
            for sc in range(NSC):
                base = bases[sc]
                own = sc >= NSC_PRE
                if sc > 0:
                    if sc - 1 >= NSC_PRE:
                        s.wait_ge(sems["IMM"], bases[sc - 1]["IMM"] + 4)
                    else:
                        s.wait_ge(sems["VMM"], bases[sc - 1]["VMM"] + 4)
                    s.copy(_ap(xT, 0, [[8 * XTW, 128], [XTW, 8], [1, 1]]),
                           _ap(xT, 512, [[8 * XTW, 128], [XTW, 8], [1, 1]]))
                for t in range(4):
                    s.wait_ge(sems["TTR"], base["TTR"] + t + 1)
                    ins = s.activation(
                        _ap(xT, 1 + t * 128, [[8 * XTW, 128], [XTW, 8], [1, 128]]),
                        _ap(ptr, 0, [[1024, 128], [128, 8], [1, 128]]),
                        ACT.Copy)
                    ins.then_inc(sems["TEV"], 1)
                    s.wait_ge(sems["VMM"], base["VMM"] + t + 1)
                    ins = s.activation(vgw[:, t * D:(t + 1) * D], pv[:, :], ACT.Copy,
                                       scale=scal[:, 9 + t:10 + t])
                    if own:
                        ins = s.activation(vtm[:, t * D:(t + 1) * D], pv[:, :], ACT.Copy)
                    ins.then_inc(sems["VEV"], 1)
                if own:
                    for pt in range(5):
                        if pt > 0:
                            eq = ((pt - 1) % 2) * 2 * D
                            q = pt - 1
                            s.wait_ge(sems["LNV"], base["LNV"] + pt)
                            s.activation(epi[:, eq:eq + D], epi[:, eq + D:eq + 2 * D],
                                         ACT.Square, bias=lnst[:, 12 + q:13 + q],
                                         accum_out=lnst[:, 0 + q:1 + q]).then_inc(sems["SQS"], 1)
                            s.wait_ge(sems["VAE"], base["VAE"] + pt)
                            s.activation(lnst[:, 8 + q:9 + q], lnst[:, 4 + q:5 + q],
                                         ACT.Sqrt).then_inc(sems["SQT"], 1)
                        if pt == 4:
                            break
                        s.wait_ge(sems["IMM"], base["IMM"] + pt + 1)
                        s.wait_ge(sems["RCB"], base["RCB"] + pt)
                        s.activation(inter_sb[:, :], pio_a[:, :], ACT.Copy,
                                     scale=scal[:, 5 + pt:6 + pt]).then_inc(sems["REV"], 1)
                        s.wait_ge(sems["RTT"], base["RTT"] + pt + 1)
                        if pt > 0:
                            s.wait_ge(sems["ROM"], base["ROM"] + pt)
                        s.copy(readsT_sb[:, :], ptr[:, :]).then_inc(sems["RTE"], 1)
                        s.wait_ge(sems["ROM"], base["ROM"] + pt + 1)
                        if pt >= 2:
                            s.wait_ge(sems["EPV"], base["EPV"] + pt - 1)
                        ep = (pt % 2) * 2 * D
                        s.activation(epi[:, ep:ep + D], pv[:, :], ACT.Copy,
                                     scale=scal[:, 3:4]).then_inc(sems["EPS_"], 1)

        @blk.vector
        def _(v):
            for sc in range(NSC):
                base = bases[sc]
                own = sc >= NSC_PRE
                if own:
                    for kt in range(4):
                        n = 512 - kt * 128
                        v.wait_ge(sems["SMM"], base["SMM"] + kt + 1)
                        v.tensor_mul(smt[:, kt * SCL + kt * 128: kt * SCL + 512],
                                     pS[:, 0:n],
                                     mt[:, kt * SCL + kt * 128: kt * SCL + 512]
                                     ).then_inc(sems["SMK"], 1)
                    for pt in range(4):
                        ep = (pt % 2) * 2 * D
                        v.wait_ge(sems["NMM"], base["NMM"] + pt + 1)
                        v.wait_ge(sems["REV"], base["REV"] + pt + 1)
                        if pt > 0:
                            v.wait_ge(sems["RTT"], base["RTT"] + pt)
                        v.tensor_add(reads_sb[:, :], inter_sb[:, :],
                                     pio_b[:, :]).then_inc(sems["RCB"], 1)
                        gi = (sc - NSC_PRE) * 4 + pt
                        v.wait_ge(sems["EPS_"], base["EPS_"] + pt + 1)
                        v.wait_ge(sems["XF0" if pt % 2 == 0 else "XF1"],
                                  base["XF0" if pt % 2 == 0 else "XF1"] + 16 * (pt // 2 + 1))
                        v.scalar_tensor_tensor(epi[:, ep + D:ep + 2 * D],
                                               xf32[:, (pt % 2) * D:(pt % 2 + 1) * D], 2.0,
                                               epi[:, ep:ep + D], ALU.mult, ALU.add)
                        v.drain()
                        v.reduce_sum(lnst[:, 8 + pt:9 + pt], epi[:, ep + D:ep + 2 * D], AX.X)
                        v.drain()
                        v.tensor_scalar(lnst[:, 12 + pt:13 + pt], lnst[:, 8 + pt:9 + pt],
                                        -1.0 / D, None, ALU.mult).then_inc(sems["LNV"], 1)
                        v.wait_ge(sems["SQS"], base["SQS"] + pt + 1)
                        v.tensor_scalar(lnst[:, 4 + pt:5 + pt], lnst[:, 0 + pt:1 + pt],
                                        1.0 / D, EPS, ALU.mult, ALU.add).then_inc(sems["VAE"], 1)
                        v.wait_ge(sems["SQT"], base["SQT"] + pt + 1)
                        v.reciprocal(lnst[:, 0 + pt:1 + pt], lnst[:, 8 + pt:9 + pt])
                        v.drain()
                        v.tensor_mul(lnst[:, 4 + pt:5 + pt], lnst[:, 12 + pt:13 + pt],
                                     lnst[:, 0 + pt:1 + pt])
                        v.drain()
                        oc = gi * D
                        v.tensor_scalar(stream[:, oc:oc + D], epi[:, ep + D:ep + 2 * D],
                                        lnst[:, 0 + pt:1 + pt], lnst[:, 4 + pt:5 + pt],
                                        ALU.mult, ALU.add).then_inc(sems["EPV"], 1)
                for et in range(8):
                    pA = pv if et % 2 == 0 else pio_a
                    v.wait_ge(sems["AMM"], base["AMM"] + et + 1)
                    v.scalar_tensor_tensor(state_bf[:, et * D:(et + 1) * D],
                                           state_bf[:, et * D:(et + 1) * D],
                                           scal[:, 4:5], pA[:, :],
                                           ALU.mult, ALU.add).then_inc(sems["AUP"], 1)
    _clear_sems(nc, sems)



def emit_phase_b(nc, es, g):
    gate_io, gammacol, betacol = g["gate_io"], g["gammacol"], g["betacol"]
    w1_io, w2_io = g["w1_io"], g["w2_io"]
    stream, ident_b, onesr_f = g["stream"], g["ident_b"], g["onesr_f"]

    x2T = es.enter_context(nc.sbuf_tensor("x2T", [128, 8 * T_OWN], BF16))
    hT = es.enter_context(nc.sbuf_tensor("hT", [128, 16 * 1024], BF16))
    w1p = es.enter_context(nc.sbuf_tensor("w1p", [128, 2 * 4096], BF16))
    w2sb = es.enter_context(nc.sbuf_tensor("w2sb", [128, 16 * 1024], BF16))
    acc = es.enter_context(nc.sbuf_tensor("acc", [128, 8 * 1024], F32))
    comb = es.enter_context(nc.sbuf_tensor("comb", [128, 16 * 8], F32))
    gsc = es.enter_context(nc.sbuf_tensor("gsc", [128, 16 * 16], F32))
    gv8 = es.enter_context(nc.sbuf_tensor("gv8", [128, 16 * 24], F32))
    gate_sb = es.enter_context(nc.sbuf_tensor("gate_sb", [128, 64], F32))
    gam_sb = es.enter_context(nc.sbuf_tensor("gam_sb", [128, 8], F32))
    bet_sb = es.enter_context(nc.sbuf_tensor("bet_sb", [128, 8], F32))
    g2 = es.enter_context(nc.sbuf_tensor("g2", [128, 64], BF16))
    ceb = es.enter_context(nc.sbuf_tensor("ceb", [128, 8], F32))
    beb = es.enter_context(nc.sbuf_tensor("beb", [128, 8], F32))
    cb_row = es.enter_context(nc.sbuf_tensor("cb_row", [1, 16], F32))
    sqdump = es.enter_context(nc.sbuf_tensor("sqdump", [128, D], F32))
    epiB = es.enter_context(nc.sbuf_tensor("epiB", [128, 4 * D], F32))
    lnstB = es.enter_context(nc.sbuf_tensor("lnstB", [128, 32], F32))

    psB_init = ExitStack()
    pG = psB_init.enter_context(nc.psum_tensor("pG", [128, 8], F32))
    pT = psB_init.enter_context(nc.psum_tensor("pT", [1, 16], F32))
    ptrB = psB_init.enter_context(nc.psum_tensor("ptrB", [128, 1024], BF16))

    sems = {}
    for n in ["LD", "TT", "TE", "G2V", "GT0", "GS0", "GBC", "GT", "GA", "GB",
              "GC_", "GD", "GE", "GF", "CMB", "W1P0", "W1P1", "W2L", "HMM",
              "HEV", "HPC", "YMM", "YEV", "BLNV", "BSQS", "BVAE", "BSQT", "BEPV"]:
        sems[n] = es.enter_context(nc.semaphore("b_" + n))

    with nc.Block() as blk:
        @blk.gpsimd
        def _(gp):
            gp.dma_start(out=gate_sb[:, :].rearrange("p (cb e) -> p cb e", cb=8),
                         in_=gate_io.rearrange("(cb p) e -> p cb e", p=128)).then_inc(sems["LD"], 16)
            gp.dma_start(out=gam_sb[:, :], in_=gammacol).then_inc(sems["LD"], 16)
            gp.dma_start(out=bet_sb[:, :], in_=betacol).then_inc(sems["LD"], 16)

        @blk.tensor
        def _(tn):
            for i in range(NT):
                tn.wait_ge(sems["TE"], i)
                for jb in range(8):
                    ins = tn.transpose(ptrB[:, jb * 128:(jb + 1) * 128],
                                       stream[:, i * D + jb * 128:i * D + (jb + 1) * 128],
                                       ident_b[:, :])
                ins.then_inc(sems["TT"], 1)
            tn.wait_ge(sems["LD"], 48)
            for cb in range(8):
                tn.matmul(pT[0:1, 0:8], gam_sb[:, cb:cb + 1],
                          gate_sb[:, cb * 8:(cb + 1) * 8],
                          start=(cb == 0), stop=(cb == 7))
            for cb in range(8):
                ins = tn.matmul(pT[0:1, 8:16], bet_sb[:, cb:cb + 1],
                                gate_sb[:, cb * 8:(cb + 1) * 8],
                                start=(cb == 0), stop=(cb == 7))
            ins.then_inc(sems["GT0"], 1)
            tn.wait_ge(sems["GS0"], 1)
            tn.matmul(pG[:, 0:8], onesr_f[0:1, :], cb_row[0:1, 0:8],
                      start=True, stop=True).then_inc(sems["GT0"], 1)
            tn.wait_ge(sems["GBC"], 1)
            tn.matmul(pG[:, 0:8], onesr_f[0:1, :], cb_row[0:1, 8:16],
                      start=True, stop=True).then_inc(sems["GT0"], 1)
            tn.wait_ge(sems["G2V"], 1)
            tn.wait_ge(sems["GBC"], 2)
            for i in range(NT):
                if i > 0:
                    tn.wait_ge(sems["GE"], i)
                tn.wait_ge(sems["TE"], i + 1)
                for cb in range(8):
                    ins = tn.matmul(pG[:, 0:8],
                                    x2T[:, cb * T_OWN + i * 128: cb * T_OWN + (i + 1) * 128],
                                    g2[:, cb * 8:(cb + 1) * 8],
                                    start=(cb == 0), stop=(cb == 7))
                ins.then_inc(sems["GT"], 1)

        @blk.scalar
        def _(s):
            for i in range(NT):
                s.wait_ge(sems["TT"], i + 1)
                s.activation(
                    _ap(x2T, i * 128, [[8 * T_OWN, 128], [T_OWN, 8], [1, 128]]),
                    _ap(ptrB, 0, [[1024, 128], [128, 8], [1, 128]]),
                    ACT.Copy).then_inc(sems["TE"], 1)
            s.wait_ge(sems["GT0"], 1)
            s.copy(cb_row[0:1, 0:16], pT[0:1, 0:16]).then_inc(sems["GS0"], 1)
            for i in range(NT):
                gc = i * 16
                s.wait_ge(sems["GA"], i + 1)
                s.drain()
                s.activation(sqdump[:, :], stream[:, i * D:(i + 1) * D],
                             ACT.Square, bias=gsc[:, gc + 0:gc + 1],
                             accum_out=gsc[:, gc + 1:gc + 2]).then_inc(sems["GB"], 1)
                s.wait_ge(sems["GC_"], i + 1)
                s.activation(gsc[:, gc + 3:gc + 4], gsc[:, gc + 2:gc + 3],
                             ACT.Sqrt).then_inc(sems["GD"], 1)
                s.wait_ge(sems["GE"], i + 1)
                s.activation(gv8[:, i * 24 + 16:i * 24 + 24],
                             gv8[:, i * 24 + 8:i * 24 + 16], ACT.Exp,
                             bias=gsc[:, gc + 7:gc + 8],
                             accum_out=gsc[:, gc + 8:gc + 9]).then_inc(sems["GF"], 1)

        @blk.vector
        def _(v):
            v.wait_ge(sems["LD"], 48)
            for cb in range(8):
                ins = v.tensor_scalar(g2[:, cb * 8:(cb + 1) * 8], gate_sb[:, cb * 8:(cb + 1) * 8],
                                      gam_sb[:, cb:cb + 1], None, ALU.mult)
            ins.then_inc(sems["G2V"], 1)
            v.wait_ge(sems["GT0"], 2)
            v.tensor_copy(ceb[:, :], pG[:, 0:8]).then_inc(sems["GBC"], 1)
            v.wait_ge(sems["GT0"], 3)
            v.tensor_copy(beb[:, :], pG[:, 0:8]).then_inc(sems["GBC"], 1)
            for i in range(NT):
                gc, gvv = i * 16, i * 24
                v.reduce_sum(gsc[:, gc + 15:gc + 16], stream[:, i * D:(i + 1) * D], AX.X)
                v.drain()
                v.tensor_scalar(gsc[:, gc + 0:gc + 1], gsc[:, gc + 15:gc + 16],
                                -1.0 / D, None, ALU.mult).then_inc(sems["GA"], 1)
                v.wait_ge(sems["GB"], i + 1)
                v.tensor_scalar(gsc[:, gc + 2:gc + 3], gsc[:, gc + 1:gc + 2],
                                1.0 / D, EPS, ALU.mult, ALU.add).then_inc(sems["GC_"], 1)
                v.wait_ge(sems["GD"], i + 1)
                v.reciprocal(gsc[:, gc + 4:gc + 5], gsc[:, gc + 3:gc + 4])
                v.drain()
                v.tensor_mul(gsc[:, gc + 5:gc + 6], gsc[:, gc + 0:gc + 1],
                             gsc[:, gc + 4:gc + 5])
                v.drain()
                v.tensor_scalar(gv8[:, gvv:gvv + 8], ceb[:, :],
                                gsc[:, gc + 5:gc + 6], None, ALU.mult)
                v.drain()
                v.tensor_add(gv8[:, gvv:gvv + 8], gv8[:, gvv:gvv + 8], beb[:, :])
                v.drain()
                v.wait_ge(sems["GT"], i + 1)
                v.scalar_tensor_tensor(gv8[:, gvv + 8:gvv + 16], pG[:, 0:8],
                                       gsc[:, gc + 4:gc + 5], gv8[:, gvv:gvv + 8],
                                       ALU.mult, ALU.add)
                v.drain()
                v.reduce_max(gsc[:, gc + 6:gc + 7], gv8[:, gvv + 8:gvv + 16], AX.X)
                v.drain()
                v.tensor_scalar(gsc[:, gc + 7:gc + 8], gsc[:, gc + 6:gc + 7],
                                -1.0, None, ALU.mult).then_inc(sems["GE"], 1)
                v.wait_ge(sems["GF"], i + 1)
                v.reciprocal(gsc[:, gc + 9:gc + 10], gsc[:, gc + 8:gc + 9])
                v.drain()
                v.tensor_scalar(gv8[:, gvv + 16:gvv + 24], gv8[:, gvv + 16:gvv + 24],
                                gsc[:, gc + 9:gc + 10], None, ALU.mult)
                v.drain()
                v.reduce_max(gsc[:, gc + 10:gc + 11], gv8[:, gvv + 16:gvv + 24], AX.X)
                v.drain()
                v.tensor_scalar(gv8[:, gvv:gvv + 8], gv8[:, gvv + 16:gvv + 24],
                                gsc[:, gc + 10:gc + 11], None, ALU.is_lt)
                v.drain()
                v.tensor_mul(gv8[:, gvv:gvv + 8], gv8[:, gvv:gvv + 8],
                             gv8[:, gvv + 16:gvv + 24])
                v.drain()
                v.reduce_max(gsc[:, gc + 11:gc + 12], gv8[:, gvv:gvv + 8], AX.X)
                v.drain()
                v.tensor_scalar(gv8[:, gvv:gvv + 8], gv8[:, gvv + 16:gvv + 24],
                                gsc[:, gc + 11:gc + 12], None, ALU.is_ge)
                v.drain()
                v.tensor_mul(comb[:, i * 8:(i + 1) * 8], gv8[:, gvv:gvv + 8],
                             gv8[:, gvv + 16:gvv + 24]).then_inc(sems["CMB"], 1)

    psB_init.close()
    pH = es.enter_context(nc.psum_tensor("pH", [128, 2048], F32))
    pY = es.enter_context(nc.psum_tensor("pY", [128, 2048], F32))
    C = dict(LD=48, W1P0=0, W1P1=0, W2L=0, HMM=0, HEV=0, HPC=0, YMM=0,
             YEV=0, BLNV=0, BSQS=0, BVAE=0, BSQT=0, BEPV=0)

    GB_T = min(1024, T_OWN)
    NGRP = T_OWN // GB_T
    NPC = GB_T // 512
    NPTL = GB_T // 128
    NHG = 16 * NPC
    NYG = NPTL * 2

    C = dict(W1P0=0, W1P1=0, W2L=0, HMM=0, HEV=0, HPC=0, YMM=0,
             YEV=0, BLNV=0, BSQS=0, BVAE=0, BSQT=0, BEPV=0)

    for grp in range(NGRP):
        base = dict(C)
        with nc.Block() as blk:

            @blk.gpsimd
            def _(gp):
                for e in range(8):
                    if e > 0:
                        gp.wait_ge(sems["YMM"], base["YMM"] + NPTL * e)
                    gp.dma_start(
                        out=w2sb[:, :].rearrange("p (fb c) -> p fb c", fb=16),
                        in_=w2_io[e].rearrange("(fb p) c -> p fb c", p=128)
                    ).then_inc(sems["W2L"], 16)
                    for k in range(4):
                        gidx = 4 * e + k
                        if gidx >= 2:
                            pe, pk = (gidx - 2) // 4, (gidx - 2) % 4
                            gp.wait_ge(sems["HMM"],
                                       base["HMM"] + 16 * pe + 4 * (pk + 1))
                        buf = gidx % 2
                        gp.dma_start(
                            out=w1p[:, buf * 4096:(buf + 1) * 4096].rearrange(
                                "p (cb f) -> p cb f", cb=8),
                            in_=w1_io[e, :, 512 * k:512 * (k + 1)].rearrange(
                                "(cb p) f -> p cb f", p=128)
                        ).then_inc(sems["W1P0" if buf == 0 else "W1P1"], 16)

            @blk.tensor
            def _(tn):
                for e in range(8):
                    tn.wait_ge(sems["W2L"], base["W2L"] + 16 * (e + 1))
                    for k in range(4):
                        gidx = 4 * e + k
                        buf = gidx % 2
                        tn.wait_ge(sems["W1P0" if buf == 0 else "W1P1"],
                                   base["W1P0" if buf == 0 else "W1P1"] + 16 * (gidx // 2 + 1))
                        for ft2 in range(4):
                            ft = 4 * k + ft2
                            if 16 * e + ft >= 2:
                                tn.wait_ge(sems["HEV"], base["HEV"] + 16 * e + ft - 1)
                            ph = pH[:, (ft % 2) * 1024:(ft % 2 + 1) * 1024]
                            for cb in range(8):
                                for pc in range(NPC):
                                    ins = tn.matmul(
                                        ph[:, 512 * pc:512 * (pc + 1)],
                                        w1p[:, buf * 4096 + cb * 512 + 128 * ft2:
                                            buf * 4096 + cb * 512 + 128 * (ft2 + 1)],
                                        x2T[:, cb * T_OWN + GB_T * grp + 512 * pc:
                                            cb * T_OWN + GB_T * grp + 512 * (pc + 1)],
                                        start=(cb == 0), stop=(cb == 7),
                                        skip_group_check=True)
                            ins.then_inc(sems["HMM"], 1)
                    tn.wait_ge(sems["HEV"], base["HEV"] + 16 * (e + 1))
                    for ptl in range(NPTL):
                        gptl = NPTL * e + ptl
                        if gptl >= 2:
                            tn.wait_ge(sems["YEV"], base["YEV"] + 2 * (gptl - 1))
                        po = (gptl % 2) * 1024
                        for fk in range(16):
                            for ch in range(2):
                                ins = tn.matmul(
                                    pY[:, po + 512 * ch:po + 512 * (ch + 1)],
                                    hT[:, GB_T * fk + 128 * ptl: GB_T * fk + 128 * (ptl + 1)],
                                    w2sb[:, 1024 * fk + 512 * ch: 1024 * fk + 512 * (ch + 1)],
                                    start=(fk == 0), stop=(fk == 15),
                                    skip_group_check=True)
                        ins.then_inc(sems["YMM"], 1)

            @blk.scalar
            def _(s):
                for e in range(8):
                    if e > 0:
                        s.wait_ge(sems["YMM"], base["YMM"] + NPTL * e)
                    for ft in range(16):
                        s.wait_ge(sems["HMM"], base["HMM"] + 16 * e + ft + 1)
                        ph = pH[:, (ft % 2) * 1024:(ft % 2) * 1024 + GB_T]
                        s.activation(hT[:, GB_T * ft: GB_T * (ft + 1)],
                                     ph, ACT.Gelu).then_inc(sems["HEV"], 1)
                for i in range(NPTL):
                    s.wait_ge(sems["BLNV"], base["BLNV"] + i + 1)
                    s.drain()
                    eq = (i % 2) * 2 * D
                    s.activation(epiB[:, eq + D:eq + 2 * D], epiB[:, eq:eq + D],
                                 ACT.Square, bias=lnstB[:, 4 * i:4 * i + 1],
                                 accum_out=lnstB[:, 4 * i + 1:4 * i + 2]).then_inc(sems["BSQS"], 1)
                    s.wait_ge(sems["BVAE"], base["BVAE"] + i + 1)
                    s.activation(lnstB[:, 4 * i + 3:4 * i + 4],
                                 lnstB[:, 4 * i + 2:4 * i + 3],
                                 ACT.Sqrt).then_inc(sems["BSQT"], 1)

            @blk.vector
            def _(v):
                for e in range(8):
                    for ptl in range(NPTL):
                        gptl = NPTL * e + ptl
                        v.wait_ge(sems["YMM"], base["YMM"] + gptl + 1)
                        gi = NPTL * grp + ptl
                        cc = comb[:, gi * 8 + e: gi * 8 + e + 1]
                        po = (gptl % 2) * 1024
                        for ch in range(2):
                            py = pY[:, po + 512 * ch:po + 512 * (ch + 1)]
                            a = acc[:, 1024 * ptl + 512 * ch: 1024 * ptl + 512 * (ch + 1)]
                            if e == 0:
                                ins = v.tensor_scalar(a, py, cc, None, ALU.mult)
                            else:
                                ins = v.scalar_tensor_tensor(a, py, cc, a, ALU.mult, ALU.add)
                            ins.then_inc(sems["YEV"], 1)
                for i in range(NPTL):
                    gi = NPTL * grp + i
                    ep = (i % 2) * 2 * D
                    if i >= 2:
                        v.wait_ge(sems["BSQS"], base["BSQS"] + i - 1)
                    v.tensor_add(epiB[:, ep:ep + D], stream[:, gi * D:(gi + 1) * D],
                                 acc[:, i * D:(i + 1) * D])
                    v.drain()
                    v.reduce_sum(lnstB[:, 4 * i + 1:4 * i + 2], epiB[:, ep:ep + D], AX.X)
                    v.drain()
                    v.tensor_scalar(lnstB[:, 4 * i:4 * i + 1], lnstB[:, 4 * i + 1:4 * i + 2],
                                    -1.0 / D, None, ALU.mult).then_inc(sems["BLNV"], 1)
                    v.wait_ge(sems["BSQS"], base["BSQS"] + i + 1)
                    v.tensor_scalar(lnstB[:, 4 * i + 2:4 * i + 3],
                                    lnstB[:, 4 * i + 1:4 * i + 2],
                                    1.0 / D, EPS, ALU.mult, ALU.add).then_inc(sems["BVAE"], 1)
                    v.wait_ge(sems["BSQT"], base["BSQT"] + i + 1)
                    v.reciprocal(lnstB[:, 4 * i + 1:4 * i + 2], lnstB[:, 4 * i + 3:4 * i + 4])
                    v.drain()
                    v.tensor_mul(lnstB[:, 4 * i + 2:4 * i + 3], lnstB[:, 4 * i:4 * i + 1],
                                 lnstB[:, 4 * i + 1:4 * i + 2])
                    v.drain()
                    v.tensor_scalar(stream[:, gi * D:(gi + 1) * D], epiB[:, ep:ep + D],
                                    lnstB[:, 4 * i + 1:4 * i + 2],
                                    lnstB[:, 4 * i + 2:4 * i + 3],
                                    ALU.mult, ALU.add).then_inc(sems["BEPV"], 1)

        C["W2L"] += 16 * 8
        C["W1P0"] += 16 * 16
        C["W1P1"] += 16 * 16
        C["HMM"] += 16 * 8
        C["HEV"] += 16 * 8
        C["YMM"] += NPTL * 8
        C["YEV"] += NYG * 8
        C["BLNV"] += NPTL
        C["BSQS"] += NPTL
        C["BVAE"] += NPTL
        C["BSQT"] += NPTL
        C["BEPV"] += NPTL
    _clear_sems(nc, sems)



SCALE_C = 0.03125
SEC = 2048 * 1024
CCIN_N = 4 * SEC


def emit_phase_c(nc, es, g):
    u_io, v_io, u_nat, v_nat = g["u_io"], g["v_io"], g["u_nat"], g["v_nat"]
    stream, ident_b = g["stream"], g["ident_b"]
    cc_in, cc_out = g["cc_in"], g["cc_out"]
    S_ALL = 4 * T_OWN

    cast_jobs = None
    if g.get("W1E") is not None:
        W1E, W2E, W1B, W2B = g["W1E"], g["W2E"], g["W1B"], g["W2B"]
        CWI, CWO = g["CWI"], g["CWO"]
        cast_jobs = []
        for r in range(8):
            for c in range(8):
                cast_jobs.append((W1E, W1B, 128 * r, 512 * c))
        for r in range(32):
            for c in range(2):
                cast_jobs.append((W2E, W2B, 128 * r, 512 * c))

    def cast_batch(h):
        if cast_jobs is None:
            return []
        lo = 0 if h == 0 else 4 + 18 * (h - 1)
        hi = 4 if h == 0 else min(len(cast_jobs), 4 + 18 * h)
        return range(lo, hi)

    sems = {}
    for n in ["LD", "TT", "TE", "PJM", "PJE", "PJD0", "PJD1", "PJD2", "PJD3", "CCS", "CCD",
              "PLD0", "PLD1", "PLDA0", "PLDA1", "QLD0", "QLD1", "SVC", "FMM", "THE", "YMM", "YEV", "YDM0", "YDM1",
              "GLD", "GMM", "GEV", "CLNV", "CSQS", "CVAE", "CSQT", "CEPV"]:
        sems[n] = es.enter_context(nc.semaphore("c_" + n))

    with ExitStack() as es1:
        x3T = es1.enter_context(nc.sbuf_tensor("x3T", [128, 8 * T_OWN], BF16))
        uio_sb = es1.enter_context(nc.sbuf_tensor("uio_sb", [128, 8 * D], BF16))
        vio_sb = es1.enter_context(nc.sbuf_tensor("vio_sb", [128, 8 * D], BF16))
        qev = es1.enter_context(nc.sbuf_tensor("qev", [128, 4 * 512], BF16))
        ptrC = es1.enter_context(nc.psum_tensor("ptrC", [128, 1024], BF16))
        pPJ = es1.enter_context(nc.psum_tensor("pPJ", [128, 1024], F32))

        with nc.Block() as blk:
            @blk.gpsimd
            def _(gp):
                gp.dma_start(out=uio_sb[:, :].rearrange("p (b e) -> p b e", b=8),
                             in_=u_io.rearrange("(b p) e -> p b e", p=128)).then_inc(sems["LD"], 16)
                gp.dma_start(out=vio_sb[:, :].rearrange("p (b e) -> p b e", b=8),
                             in_=v_io.rearrange("(b p) e -> p b e", p=128)).then_inc(sems["LD"], 16)
                npj_feat = 2 * 8 * (T_OWN // 512)
                for q4 in range(4):
                    gp.wait_ge(sems["PJD%d" % q4], 16 * (npj_feat // 4))
                gp.collective_compute(
                    "AllGather", ALU.bypass,
                    replica_groups=[[0, 1, 2, 3], [4, 5, 6, 7]],
                    ins=[cc_in[0:2 * SEC]], outs=[cc_out[0:8 * SEC]],
                ).then_inc(sems["CCS"], 1)

            @blk.tensor
            def _(tn):
                for i in range(NT):
                    tn.wait_ge(sems["TE"], i)
                    for jb in range(8):
                        ins = tn.transpose(ptrC[:, jb * 128:(jb + 1) * 128],
                                           stream[:, i * D + jb * 128:i * D + (jb + 1) * 128],
                                           ident_b[:, :])
                    ins.then_inc(sems["TT"], 1)
                tn.wait_ge(sems["LD"], 32)
                tn.wait_ge(sems["TE"], NT)
                gidx = 0
                for w_sb in (uio_sb, vio_sb):
                    for ct in range(8):
                        for pc in range(T_OWN // 512):
                            if gidx >= 2:
                                tn.wait_ge(sems["PJE"], gidx - 1)
                            ps = pPJ[:, (gidx % 2) * 512:(gidx % 2 + 1) * 512]
                            for jb in range(8):
                                ins = tn.matmul(
                                    ps,
                                    w_sb[:, jb * D + 128 * ct: jb * D + 128 * (ct + 1)],
                                    x3T[:, jb * T_OWN + 512 * pc: jb * T_OWN + 512 * (pc + 1)],
                                    start=(jb == 0), stop=(jb == 7))
                            ins.then_inc(sems["PJM"], 1)
                            gidx += 1
                for w_sb in (uio_sb, vio_sb):
                    for i in range(NT):
                        for crh in range(2):
                            tn.wait_ge(sems["PJE"], gidx - 1)
                            ps = pPJ[:, (gidx % 2) * 512:(gidx % 2 + 1) * 512]
                            for jb in range(8):
                                ins = tn.matmul(
                                    ps,
                                    x3T[:, jb * T_OWN + 128 * i: jb * T_OWN + 128 * (i + 1)],
                                    w_sb[:, jb * D + 512 * crh: jb * D + 512 * (crh + 1)],
                                    start=(jb == 0), stop=(jb == 7))
                            ins.then_inc(sems["PJM"], 1)
                            gidx += 1

            @blk.scalar
            def _(s):
                for i in range(NT):
                    s.wait_ge(sems["TT"], i + 1)
                    s.activation(
                        _ap(x3T, i * 128, [[8 * T_OWN, 128], [T_OWN, 8], [1, 128]]),
                        _ap(ptrC, 0, [[1024, 128], [128, 8], [1, 128]]),
                        ACT.Copy).then_inc(sems["TE"], 1)
                NPJ = 2 * 8 * (T_OWN // 512) + 2 * NT * 2
                for gidx in range(NPJ):
                    s.wait_ge(sems["PJM"], gidx + 1)
                    if gidx >= 4:
                        s.wait_ge(sems["PJD%d" % (gidx % 4)], 16 * (gidx // 4))
                    ps = pPJ[:, (gidx % 2) * 512:(gidx % 2 + 1) * 512]
                    s.activation(qev[:, (gidx % 4) * 512:(gidx % 4 + 1) * 512],
                                 ps, ACT.Copy, scale=SCALE_C).then_inc(sems["PJE"], 1)

            @blk.sync
            def _(sy):
                NPC4 = T_OWN // 512
                gidx = 0
                for sec in range(2):
                    for ct in range(8):
                        for pc in range(NPC4):
                            sy.wait_ge(sems["PJE"], gidx + 1)
                            off = sec * SEC + (128 * ct) * T_OWN + 512 * pc
                            sy.dma_start(
                                out=_ap(cc_in, off, [[T_OWN, 128], [1, 512]]),
                                in_=qev[:, (gidx % 4) * 512:(gidx % 4 + 1) * 512]
                            ).then_inc(sems["PJD%d" % (gidx % 4)], 16)
                            gidx += 1
                for sec in range(2, 4):
                    for i in range(NT):
                        for crh in range(2):
                            sy.wait_ge(sems["PJE"], gidx + 1)
                            off = sec * SEC + (128 * i) * D + 512 * crh
                            sy.dma_start(
                                out=_ap(cc_in, off, [[D, 128], [1, 512]]),
                                in_=qev[:, (gidx % 4) * 512:(gidx % 4 + 1) * 512]
                            ).then_inc(sems["PJD%d" % (gidx % 4)], 16)
                            gidx += 1
                for q4 in range(4):
                    sy.wait_ge(sems["PJD%d" % q4], 16 * (gidx // 4))

    y1d = g["y1d"]
    y2d = g["y2d"]
    NSK = S_ALL // 128
    NTC = T_OWN // 512
    NG = NSK * NTC
    with ExitStack() as es3:
        u2t = es3.enter_context(nc.sbuf_tensor("u2t", [128, 2 * S_ALL], BF16))
        v2t = es3.enter_context(nc.sbuf_tensor("v2t", [128, 2 * S_ALL], BF16))
        utk = es3.enter_context(nc.sbuf_tensor("utk", [128, 2 * S_ALL], BF16))
        vtk = es3.enter_context(nc.sbuf_tensor("vtk", [128, 2 * S_ALL], BF16))
        qown = es3.enter_context(nc.sbuf_tensor("qown", [128, 2 * 2 * T_OWN], BF16))
        thsb = es3.enter_context(nc.sbuf_tensor("thsb", [128, 3072], BF16))
        yev = es3.enter_context(nc.sbuf_tensor("yev", [128, 2 * 2 * T_OWN], BF16))
        svc = es3.enter_context(nc.sbuf_tensor("svc", [128, 8], F32))
        if cast_jobs is not None:
            scrW = es3.enter_context(nc.sbuf_tensor("scrW", [128, 4 * 512], BF16))
        pF = es3.enter_context(nc.psum_tensor("pF", [128, 3072], F32))
        pY12 = es3.enter_context(nc.psum_tensor("pY12", [128, 1024], F32))

        with nc.Block() as blk:
            @blk.gpsimd
            def _(gp):
                npj_all = 2 * 8 * (T_OWN // 512) + 4 * NT
                gp.wait_ge(sems["CCS"], 1)
                for q4 in range(4):
                    gp.wait_ge(sems["PJD%d" % q4], 16 * (npj_all // 4))
                gp.collective_compute(
                    "AllGather", ALU.bypass,
                    replica_groups=[[0, 1, 2, 3], [4, 5, 6, 7]],
                    ins=[cc_in[2 * SEC:4 * SEC]], outs=[cc_out[8 * SEC:16 * SEC]],
                ).then_inc(sems["CCS"], 1)
                for hp in range(8):
                    o = (hp % 2) * S_ALL
                    if hp == 0:
                        gp.wait_ge(sems["CCS"], 1)
                    if hp >= 2:
                        gp.wait_ge(sems["FMM"], 2 * NG * (hp - 1) + 1)
                    gp.dma_start(out=_ap(u2t, o, [[2 * S_ALL, 128], [T_OWN, 4], [1, T_OWN]]),
                                 in_=_ap(cc_out, 128 * hp * T_OWN,
                                         [[T_OWN, 128], [2 * SEC, 4], [1, T_OWN]])
                                 ).then_inc(sems["PLDA%d" % (hp % 2)], 16)
                    gp.dma_start(out=_ap(v2t, o, [[2 * S_ALL, 128], [T_OWN, 4], [1, T_OWN]]),
                                 in_=_ap(cc_out, SEC + 128 * hp * T_OWN,
                                         [[T_OWN, 128], [2 * SEC, 4], [1, T_OWN]])
                                 ).then_inc(sems["PLDA%d" % (hp % 2)], 16)
                    if hp == 0:
                        gp.wait_ge(sems["CCS"], 2)
                    for r in range(4):
                        gp.dma_start(out=_ap(utk, o + r * 128 * NT, [[2 * S_ALL, 128], [128, NT], [1, 128]]),
                                     in_=_ap(cc_out, 8 * SEC + 128 * hp + r * 2 * SEC,
                                             [[D, 128], [128 * D, NT], [1, 128]])
                                     ).then_inc(sems["PLD%d" % (hp % 2)], 16)
                        gp.dma_start(out=_ap(vtk, o + r * 128 * NT, [[2 * S_ALL, 128], [128, NT], [1, 128]]),
                                     in_=_ap(cc_out, 8 * SEC + SEC + 128 * hp + r * 2 * SEC,
                                             [[D, 128], [128 * D, NT], [1, 128]])
                                     ).then_inc(sems["PLD%d" % (hp % 2)], 16)
                    for ch in cast_batch(hp):
                        src, dst, r0, c0 = cast_jobs[ch]
                        q = ch % 4
                        if ch >= 4:
                            gp.wait_ge(CWO[q], 16 * (ch // 4))
                        gp.dma_start(out=scrW[:, q * 512:(q + 1) * 512],
                                     in_=src[r0:r0 + 128, c0:c0 + 512]
                                     ).then_inc(CWI[q], 16)

            @blk.sync
            def _(sy):
                for hp in range(8):
                    oq = (hp % 2) * 2 * T_OWN
                    if hp >= 2:
                        sy.wait_ge(sems["FMM"], 2 * NG * (hp - 1))
                    sy.dma_start(out=_ap(qown, oq, [[4 * T_OWN, 128], [1, T_OWN]]),
                                 in_=_ap(cc_in, 128 * hp * T_OWN, [[T_OWN, 128], [1, T_OWN]])
                                 ).then_inc(sems["QLD%d" % (hp % 2)], 16)
                    sy.dma_start(out=_ap(qown, oq + T_OWN, [[4 * T_OWN, 128], [1, T_OWN]]),
                                 in_=_ap(cc_in, SEC + 128 * hp * T_OWN, [[T_OWN, 128], [1, T_OWN]])
                                 ).then_inc(sems["QLD%d" % (hp % 2)], 16)
                    for ch in cast_batch(hp):
                        src, dst, r0, c0 = cast_jobs[ch]
                        q = ch % 4
                        sy.wait_ge(CWI[q], 16 * (ch // 4 + 1))
                        sy.dma_start(out=dst[r0:r0 + 128, c0:c0 + 512],
                                     in_=scrW[:, q * 512:(q + 1) * 512]
                                     ).then_inc(CWO[q], 16)
                    ob = (hp % 2) * 2 * T_OWN
                    sy.wait_ge(sems["YEV"], 2 * NTC * hp + NTC)
                    sy.dma_start(out=_ap(y2d, 128 * hp * T_OWN, [[T_OWN, 128], [1, T_OWN]]),
                                 in_=yev[:, ob:ob + T_OWN]).then_inc(sems["YDM%d" % (hp % 2)], 16)
                    sy.wait_ge(sems["YEV"], 2 * NTC * hp + 2 * NTC)
                    sy.dma_start(out=_ap(y1d, 128 * hp * T_OWN, [[T_OWN, 128], [1, T_OWN]]),
                                 in_=yev[:, ob + T_OWN:ob + 2 * T_OWN]).then_inc(sems["YDM%d" % (hp % 2)], 16)
                sy.wait_ge(sems["YDM0"], 16 * 8)
                sy.wait_ge(sems["YDM1"], 16 * 8)
                if cast_jobs is not None:
                    for q in range(4):
                        sy.wait_ge(CWO[q], 16 * ((len(cast_jobs) + 3 - q) // 4))

            @blk.vector
            def _(v):
                for hp in range(8):
                    o = (hp % 2) * S_ALL
                    ob = (hp % 2) * 2 * T_OWN
                    sv = (hp % 2) * 4
                    v.wait_ge(sems["PLDA%d" % (hp % 2)], 32 * (hp // 2 + 1))
                    v.wait_ge(sems["PLD%d" % (hp % 2)], 128 * (hp // 2 + 1))
                    if hp >= 2:
                        v.wait_ge(sems["YMM"], 2 * (hp - 1))
                    v.reduce_sum(svc[:, sv:sv + 1], v2t[:, o:o + S_ALL], AX.X)
                    v.drain()
                    v.tensor_scalar(svc[:, sv + 1:sv + 2], svc[:, sv:sv + 1], 0.5, None, ALU.mult)
                    v.reduce_sum(svc[:, sv + 2:sv + 3], u2t[:, o:o + S_ALL], AX.X)
                    v.drain()
                    v.tensor_scalar(svc[:, sv + 3:sv + 4], svc[:, sv + 2:sv + 3],
                                    0.5, None, ALU.mult).then_inc(sems["SVC"], 1)
                    v.drain()
                    if hp >= 2:
                        v.wait_ge(sems["YDM%d" % (hp % 2)], 32 * (hp // 2))
                    for sl in range(2):
                        bias = svc[:, sv + 3:sv + 4] if sl == 0 else svc[:, sv + 1:sv + 2]
                        od = ob if sl == 0 else ob + T_OWN
                        for tc in range(NTC):
                            e = (2 * hp + sl) * NTC + tc
                            v.wait_ge(sems["YMM"], 2 * NTC * hp + sl * NTC + tc + 1)
                            v.tensor_scalar(yev[:, od + 512 * tc:od + 512 * (tc + 1)],
                                            pY12[:, (e % 2) * 512:(e % 2) * 512 + 512],
                                            0.5, bias, ALU.mult,
                                            ALU.add).then_inc(sems["YEV"], 1)

            @blk.tensor
            def _(tn):
                for hp in range(8):
                    o = (hp % 2) * S_ALL
                    oq = (hp % 2) * 2 * T_OWN
                    tn.wait_ge(sems["QLD%d" % (hp % 2)], 32 * (hp // 2 + 1))
                    GB = 2 * NG * hp

                    def gparts(gg):
                        gl = gg - GB
                        slab, rem = divmod(gl, NG)
                        tc, sk = divmod(rem, NSK)
                        return gl, slab, tc, sk

                    def emit_y(gg):
                        gl, slab, tc, sk = gparts(gg)
                        tok = utk if slab == 0 else vtk
                        ps = (gg % 3) * 1024
                        e = 2 * hp * NTC + slab * NTC + tc
                        if sk == 0:
                            if gl == 0:
                                tn.wait_ge(sems["PLD%d" % (hp % 2)], 128 * (hp // 2 + 1))
                            if e > 0:
                                tn.wait_ge(sems["YEV"], e)
                        tn.wait_ge(sems["THE"], gg + 1)
                        pyb = pY12[:, (e % 2) * 512:(e % 2) * 512 + 512]
                        tn.matmul(pyb[0:64, :],
                                  tok[:, o + 128 * sk:o + 128 * sk + 64],
                                  thsb[:, ps:ps + 512],
                                  start=(sk == 0), stop=(sk == NSK - 1),
                                  tile_position=(0, 0), skip_group_check=True)
                        ins = tn.matmul(pyb[64:128, :],
                                        tok[:, o + 128 * sk + 64:o + 128 * sk + 128],
                                        thsb[:, ps + 512:ps + 1024],
                                        start=(sk == 0), stop=(sk == NSK - 1),
                                        tile_position=(0, 64), skip_group_check=True)
                        if sk == NSK - 1:
                            ins.then_inc(sems["YMM"], 1)

                    for gl in range(2 * NG):
                        gg = GB + gl
                        _, slab, tc, sk = gparts(gg)
                        at = u2t if slab == 0 else v2t
                        boff = oq + (T_OWN if slab == 0 else 0)
                        ps = (gg % 3) * 1024
                        if gl == 0:
                            tn.wait_ge(sems["PLDA%d" % (hp % 2)], 32 * (hp // 2 + 1))
                        if gg >= 3:
                            tn.wait_ge(sems["THE"], gg - 2)
                        tn.matmul(pF[:, ps:ps + 512],
                                  at[0:64, o + 128 * sk:o + 128 * (sk + 1)],
                                  qown[0:64, boff + 512 * tc:boff + 512 * (tc + 1)],
                                  start=True, stop=True, tile_position=(0, 0))
                        tn.matmul(pF[:, ps + 512:ps + 1024],
                                  at[64:128, o + 128 * sk:o + 128 * (sk + 1)],
                                  qown[64:128, boff + 512 * tc:boff + 512 * (tc + 1)],
                                  start=True, stop=True,
                                  tile_position=(64, 0)).then_inc(sems["FMM"], 1)
                        if gl >= 2:
                            emit_y(gg - 2)
                    emit_y(GB + 2 * NG - 2)
                    emit_y(GB + 2 * NG - 1)

            @blk.scalar
            def _(s):
                for hp in range(8):
                    s.wait_ge(sems["SVC"], hp + 1)
                    for gl in range(2 * NG):
                        gg = 2 * NG * hp + gl
                        ps = (gg % 3) * 1024
                        s.wait_ge(sems["FMM"], gg + 1)
                        s.activation(thsb[:, ps:ps + 1024],
                                     pF[:, ps:ps + 1024],
                                     ACT.Tanh, scale=0.5).then_inc(sems["THE"], 1)

    with ExitStack() as es4:
        y1s = es4.enter_context(nc.sbuf_tensor("y1s", [128, 8 * T_OWN], BF16))
        y2s = es4.enter_context(nc.sbuf_tensor("y2s", [128, 8 * T_OWN], BF16))
        un_sb = es4.enter_context(nc.sbuf_tensor("un_sb", [128, 8 * D], BF16))
        vn_sb = es4.enter_context(nc.sbuf_tensor("vn_sb", [128, 8 * D], BF16))
        epiC = es4.enter_context(nc.sbuf_tensor("epiC", [128, 4 * D], F32))
        lnstC = es4.enter_context(nc.sbuf_tensor("lnstC", [128, 64], F32))
        pGC = es4.enter_context(nc.psum_tensor("pGC", [128, 2048], F32))

        with nc.Block() as blk:
            @blk.gpsimd
            def _(gp):
                gp.dma_start(out=un_sb[:, :].rearrange("p (b e) -> p b e", b=8),
                             in_=u_nat.rearrange("(b p) e -> p b e", p=128)).then_inc(sems["GLD"], 16)
                gp.dma_start(out=vn_sb[:, :].rearrange("p (b e) -> p b e", b=8),
                             in_=v_nat.rearrange("(b p) e -> p b e", p=128)).then_inc(sems["GLD"], 16)
                gp.dma_start(out=y1s[:, :].rearrange("p (k t) -> p k t", k=8),
                             in_=_ap(y1d, 0, [[T_OWN, 128], [128 * T_OWN, 8], [1, T_OWN]])
                             ).then_inc(sems["GLD"], 16)
                gp.dma_start(out=y2s[:, :].rearrange("p (k t) -> p k t", k=8),
                             in_=_ap(y2d, 0, [[T_OWN, 128], [128 * T_OWN, 8], [1, T_OWN]])
                             ).then_inc(sems["GLD"], 16)

            @blk.tensor
            def _(tn):
                tn.wait_ge(sems["GLD"], 64)
                for i in range(NT):
                    for eh in range(2):
                        gidx = 2 * i + eh
                        if gidx >= 4:
                            tn.wait_ge(sems["GEV"], gidx - 3)
                        ps = pGC[:, (gidx % 4) * 512:(gidx % 4 + 1) * 512]
                        for k in range(8):
                            tn.matmul(ps, y1s[:, k * T_OWN + 128 * i: k * T_OWN + 128 * (i + 1)],
                                      un_sb[:, k * D + 512 * eh: k * D + 512 * (eh + 1)],
                                      start=(k == 0), stop=False)
                        for k in range(8):
                            ins = tn.matmul(ps, y2s[:, k * T_OWN + 128 * i: k * T_OWN + 128 * (i + 1)],
                                            vn_sb[:, k * D + 512 * eh: k * D + 512 * (eh + 1)],
                                            start=False, stop=(k == 7))
                        ins.then_inc(sems["GMM"], 1)

            @blk.scalar
            def _(s):
                for i in range(NT):
                    ep = (i % 2) * 2 * D
                    for eh in range(2):
                        gidx = 2 * i + eh
                        s.wait_ge(sems["GMM"], gidx + 1)
                        if i >= 2:
                            s.wait_ge(sems["CEPV"], i - 1)
                        ps = pGC[:, (gidx % 4) * 512:(gidx % 4 + 1) * 512]
                        s.activation(epiC[:, ep + 512 * eh:ep + 512 * (eh + 1)], ps,
                                     ACT.Copy, scale=SCALE_C).then_inc(sems["GEV"], 1)
                    if i > 0:
                        q = i - 1
                        eq = (q % 2) * 2 * D
                        s.wait_ge(sems["CLNV"], i)
                        s.drain()
                        s.activation(epiC[:, eq + D:eq + 2 * D], epiC[:, eq:eq + D],
                                     ACT.Square, bias=lnstC[:, 4 * q:4 * q + 1],
                                     accum_out=lnstC[:, 4 * q + 1:4 * q + 2]).then_inc(sems["CSQS"], 1)
                        s.wait_ge(sems["CVAE"], i)
                        s.activation(lnstC[:, 4 * q + 3:4 * q + 4], lnstC[:, 4 * q + 2:4 * q + 3],
                                     ACT.Sqrt).then_inc(sems["CSQT"], 1)
                q = NT - 1
                eq = (q % 2) * 2 * D
                s.wait_ge(sems["CLNV"], NT)
                s.drain()
                s.activation(epiC[:, eq + D:eq + 2 * D], epiC[:, eq:eq + D],
                             ACT.Square, bias=lnstC[:, 4 * q:4 * q + 1],
                             accum_out=lnstC[:, 4 * q + 1:4 * q + 2]).then_inc(sems["CSQS"], 1)
                s.wait_ge(sems["CVAE"], NT)
                s.activation(lnstC[:, 4 * q + 3:4 * q + 4], lnstC[:, 4 * q + 2:4 * q + 3],
                             ACT.Sqrt).then_inc(sems["CSQT"], 1)

            @blk.vector
            def _(v):
                for i in range(NT):
                    ep = (i % 2) * 2 * D
                    v.wait_ge(sems["GEV"], 2 * i + 2)
                    v.tensor_add(epiC[:, ep:ep + D], stream[:, i * D:(i + 1) * D],
                                 epiC[:, ep:ep + D])
                    v.drain()
                    v.reduce_sum(lnstC[:, 4 * i + 1:4 * i + 2], epiC[:, ep:ep + D], AX.X)
                    v.drain()
                    v.tensor_scalar(lnstC[:, 4 * i:4 * i + 1], lnstC[:, 4 * i + 1:4 * i + 2],
                                    -1.0 / D, None, ALU.mult).then_inc(sems["CLNV"], 1)
                    v.wait_ge(sems["CSQS"], i + 1)
                    v.tensor_scalar(lnstC[:, 4 * i + 2:4 * i + 3], lnstC[:, 4 * i + 1:4 * i + 2],
                                    1.0 / D, EPS, ALU.mult, ALU.add).then_inc(sems["CVAE"], 1)
                    v.wait_ge(sems["CSQT"], i + 1)
                    v.reciprocal(lnstC[:, 4 * i + 1:4 * i + 2], lnstC[:, 4 * i + 3:4 * i + 4])
                    v.drain()
                    v.tensor_mul(lnstC[:, 4 * i + 2:4 * i + 3], lnstC[:, 4 * i:4 * i + 1],
                                 lnstC[:, 4 * i + 1:4 * i + 2])
                    v.drain()
                    v.tensor_scalar(stream[:, i * D:(i + 1) * D], epiC[:, ep:ep + D],
                                    lnstC[:, 4 * i + 1:4 * i + 2],
                                    lnstC[:, 4 * i + 2:4 * i + 3],
                                    ALU.mult, ALU.add).then_inc(sems["CEPV"], 1)
    _clear_sems(nc, sems)



def emit_phase_d(nc, es, g):
    W1B, W2B, CWO = g["W1B"], g["W2B"], g["CWO"]
    b1col, b2rep, wrcol = g["b1col"], g["b2rep"], g["wrcol"]
    cc2_in, cc2_out = g["cc2_in"], g["cc2_out"]
    stream, ident_b, onesr_f, out_hbm = g["stream"], g["ident_b"], g["onesr_f"], g["out_hbm"]
    NBIS = 45
    KSEL = 2 * T_OWN

    sems = {}
    for n in ["LD", "TT", "TE", "SMM", "SEV", "SDM", "CCS", "BLD", "BV", "BT",
              "W1P0", "W1P1", "W2L", "HMM", "HEV", "YMM", "YEV", "GTV",
              "DLNV", "DSQS", "DVAE", "DSQT", "DEPV", "FDM0", "FDM1"]:
        sems[n] = es.enter_context(nc.semaphore("d_" + n))

    x4T = es.enter_context(nc.sbuf_tensor("x4T", [128, 8 * T_OWN], BF16))
    scS = es.enter_context(nc.sbuf_tensor("scS", [128, NT], F32))
    wr_sb = es.enter_context(nc.sbuf_tensor("wr_sb", [128, 8], BF16))
    onesc_f = es.enter_context(nc.sbuf_tensor("onesc_f", [128, 1], F32))
    b1_sb = es.enter_context(nc.sbuf_tensor("b1_sb", [128, 32], F32))
    b2_sb = es.enter_context(nc.sbuf_tensor("b2_sb", [128, D], F32))

    with ExitStack() as es1:
        ptrD = es1.enter_context(nc.psum_tensor("ptrD", [128, 1024], BF16))
        pSC = es1.enter_context(nc.psum_tensor("pSC", [128, 1024], F32))
        with nc.Block() as blk:
            @blk.gpsimd
            def _(gp):
                gp.dma_start(out=wr_sb[:, :], in_=wrcol).then_inc(sems["LD"], 16)
                gp.dma_start(out=b1_sb[:, :], in_=b1col).then_inc(sems["LD"], 16)
                gp.dma_start(out=b2_sb[:, :], in_=b2rep).then_inc(sems["LD"], 16)
                gp.wait_ge(sems["SDM"], 16)
                gp.collective_compute(
                    "AllGather", ALU.bypass,
                    replica_groups=[[0, 1, 2, 3], [4, 5, 6, 7]],
                    ins=[cc2_in[:]], outs=[cc2_out[:]],
                ).then_inc(sems["CCS"], 1)

            @blk.tensor
            def _(tn):
                for i in range(NT):
                    tn.wait_ge(sems["TE"], i)
                    for jb in range(8):
                        ins = tn.transpose(ptrD[:, jb * 128:(jb + 1) * 128],
                                           stream[:, i * D + jb * 128:i * D + (jb + 1) * 128],
                                           ident_b[:, :])
                    ins.then_inc(sems["TT"], 1)
                tn.wait_ge(sems["LD"], 48)
                for i in range(NT):
                    tn.wait_ge(sems["TE"], i + 1)
                    if i >= 2:
                        tn.wait_ge(sems["SEV"], i - 1)
                    ps = pSC[:, (i % 2) * 512:(i % 2) * 512 + 1]
                    for jb in range(8):
                        ins = tn.matmul(ps,
                                        x4T[:, jb * T_OWN + 128 * i: jb * T_OWN + 128 * (i + 1)],
                                        wr_sb[:, jb:jb + 1],
                                        start=(jb == 0), stop=(jb == 7))
                    ins.then_inc(sems["SMM"], 1)

            @blk.scalar
            def _(s):
                for i in range(NT):
                    s.wait_ge(sems["TT"], i + 1)
                    s.activation(
                        _ap(x4T, i * 128, [[8 * T_OWN, 128], [T_OWN, 8], [1, 128]]),
                        _ap(ptrD, 0, [[1024, 128], [128, 8], [1, 128]]),
                        ACT.Copy).then_inc(sems["TE"], 1)

            @blk.vector
            def _(v):
                v.memset(onesc_f[:, 0:1], 1.0)
                for i in range(NT):
                    v.wait_ge(sems["SMM"], i + 1)
                    v.tensor_copy(scS[:, i:i + 1], pSC[:, (i % 2) * 512:(i % 2) * 512 + 1]
                                  ).then_inc(sems["SEV"], 1)

            @blk.sync
            def _(sy):
                sy.wait_ge(sems["SEV"], NT)
                sy.dma_start(out=_ap(cc2_in, 0, [[NT, 128], [1, NT]]),
                             in_=scS[:, :]).then_inc(sems["SDM"], 16)
                sy.wait_ge(sems["SDM"], 16)

    GB_T = min(1024, T_OWN)
    NGRP = T_OWN // GB_T
    NPC = GB_T // 512
    NPTL = GB_T // 128
    NHG = 16 * NPC
    NYG = NPTL * 2

    w1p = es.enter_context(nc.sbuf_tensor("dw1p", [128, 2 * 4096], BF16))
    w2sb = es.enter_context(nc.sbuf_tensor("dw2sb", [128, 16 * 1024], BF16))
    hT = es.enter_context(nc.sbuf_tensor("dhT", [128, 16 * GB_T], BF16))
    acc_bf = es.enter_context(nc.sbuf_tensor("daccb", [128, NT * 1024], BF16))
    ytmp = es.enter_context(nc.sbuf_tensor("dytmp", [128, 512], BF16))
    gat = es.enter_context(nc.sbuf_tensor("gat", [128, NT], F32))
    epiD = es.enter_context(nc.sbuf_tensor("epiD", [128, 4 * D], F32))
    lnstD = es.enter_context(nc.sbuf_tensor("lnstD", [128, 64], F32))
    fin = es.enter_context(nc.sbuf_tensor("fin", [128, 2 * D], F32))
    pH = es.enter_context(nc.psum_tensor("dpH", [128, 1024], F32))
    pY = es.enter_context(nc.psum_tensor("dpY", [128, 1024], F32))

    thr_b = es.enter_context(nc.sbuf_tensor("thr_b", [128, 2], F32))
    scA = es.enter_context(nc.sbuf_tensor("scA", [128, 4 * NT], F32))
    cmp64 = es.enter_context(nc.sbuf_tensor("cmp64", [128, 4 * NT], F32))
    pc = es.enter_context(nc.sbuf_tensor("pc", [128, 2], F32))
    bis = es.enter_context(nc.sbuf_tensor("bis", [1, 8], F32))
    pB = es.enter_context(nc.psum_tensor("pB", [128, 1024], F32))

    bis_ops = [(w, j) for j in range(NBIS) for w in ("cnt", "bct")] + [("fin", 0), ("fin", 1)]
    N_Y0A = max(0, NYG - 8)
    N_H1A = NHG
    N_Y1A = NYG

    def bisPE(tn, op):
        w, j = op
        if w == "cnt":
            tn.wait_ge(sems["BV"], 2 * j + 1)
            tn.matmul(pB[0:1, 0:1], pc[:, 0:1], onesc_f[:, 0:1],
                      start=True, stop=True).then_inc(sems["BT"], 1)
        elif w == "bct":
            tn.wait_ge(sems["BV"], 2 * j + 2)
            tn.matmul(pB[:, 512:513], onesr_f[0:1, :], bis[0:1, 2:3],
                      start=True, stop=True).then_inc(sems["BT"], 1)
        elif j == 0:
            tn.wait_ge(sems["BV"], 2 * NBIS + 1)
            tn.matmul(pB[:, 512:513], onesr_f[0:1, :], bis[0:1, 0:1],
                      start=True, stop=True).then_inc(sems["BT"], 1)

    def bisV(v, op):
        w, j = op
        if w == "cnt":
            if j == 0:
                v.wait_ge(sems["BLD"], 16)
            v.wait_ge(sems["BT"], 2 * j)
            if j > 0:
                v.tensor_copy(thr_b[:, 0:1], pB[:, 512:513])
                v.drain()
            v.tensor_scalar(cmp64[:, :], scA[:, :], thr_b[:, 0:1], None, ALU.is_gt)
            v.drain()
            v.reduce_sum(pc[:, 0:1], cmp64[:, :], AX.X).then_inc(sems["BV"], 1)
        elif w == "bct":
            v.wait_ge(sems["BT"], 2 * j + 1)
            v.tensor_scalar(bis[0:1, 3:4], pB[0:1, 0:1], float(KSEL), None, ALU.is_ge)
            v.drain()
            v.tensor_sub(bis[0:1, 5:6], bis[0:1, 2:3], bis[0:1, 0:1])
            v.tensor_sub(bis[0:1, 6:7], bis[0:1, 1:2], bis[0:1, 2:3])
            v.drain()
            v.scalar_tensor_tensor(bis[0:1, 0:1], bis[0:1, 5:6], bis[0:1, 3:4],
                                   bis[0:1, 0:1], ALU.mult, ALU.add)
            v.scalar_tensor_tensor(bis[0:1, 1:2], bis[0:1, 6:7], bis[0:1, 3:4],
                                   bis[0:1, 2:3], ALU.mult, ALU.add)
            v.drain()
            v.tensor_add(bis[0:1, 4:5], bis[0:1, 0:1], bis[0:1, 1:2])
            v.drain()
            v.tensor_scalar(bis[0:1, 2:3], bis[0:1, 4:5], 0.5, None,
                            ALU.mult).then_inc(sems["BV"], 1)
        elif j == 0:
            v.drain()
            v.tensor_copy(bis[0:1, 5:6], bis[0:1, 0:1]).then_inc(sems["BV"], 1)
        else:
            v.wait_ge(sems["BT"], 2 * NBIS + 1)
            v.tensor_copy(thr_b[:, 0:1], pB[:, 512:513]).then_inc(sems["BV"], 1)

    C = dict(W1P0=0, W1P1=0, W2L=0, HMM=0, HEV=0, YMM=0, YEV=0)

    for grp in range(NGRP):
        base = dict(C)
        last = grp == NGRP - 1
        with nc.Block() as blk:
            @blk.gpsimd
            def _(gp):
                if grp == 0:
                    for q in range(4):
                        gp.wait_ge(CWO[q], 16 * 32)
                for fh in range(2):
                    if fh > 0:
                        gp.wait_ge(sems["YMM"], base["YMM"] + NYG * fh)
                    gp.dma_start(
                        out=w2sb[:, :].rearrange("p (fb c) -> p fb c", fb=16),
                        in_=W2B[2048 * fh:2048 * (fh + 1), :].rearrange("(fb p) c -> p fb c", p=128)
                    ).then_inc(sems["W2L"], 16)
                    for k in range(4):
                        gidx = 4 * fh + k
                        if gidx >= 2:
                            pe, pk = (gidx - 2) // 4, (gidx - 2) % 4
                            gp.wait_ge(sems["HMM"],
                                       base["HMM"] + NHG * pe + 4 * NPC * (pk + 1))
                        buf = gidx % 2
                        gp.dma_start(
                            out=w1p[:, buf * 4096:(buf + 1) * 4096].rearrange(
                                "p (cb f) -> p cb f", cb=8),
                            in_=W1B[:, 2048 * fh + 512 * k:2048 * fh + 512 * (k + 1)].rearrange(
                                "(cb p) f -> p cb f", p=128)
                        ).then_inc(sems["W1P0" if buf == 0 else "W1P1"], 16)

            @blk.tensor
            def _(tn):
                bi = [0]

                def banchor(cond):
                    if last and cond and bi[0] < len(bis_ops):
                        bisPE(tn, bis_ops[bi[0]])
                        bi[0] += 1

                for fh in range(2):
                    tn.wait_ge(sems["W2L"], base["W2L"] + 16 * (fh + 1))
                    for k in range(4):
                        gidx = 4 * fh + k
                        buf = gidx % 2
                        tn.wait_ge(sems["W1P0" if buf == 0 else "W1P1"],
                                   base["W1P0" if buf == 0 else "W1P1"] + 16 * (gidx // 2 + 1))
                        for ft2 in range(4):
                            ft = 4 * k + ft2
                            for pcc in range(NPC):
                                hg = NPC * ft + pcc
                                if NHG * fh + hg >= 2:
                                    tn.wait_ge(sems["HEV"], base["HEV"] + NHG * fh + hg - 1)
                                ph = pH[:, (hg % 2) * 512:(hg % 2 + 1) * 512]
                                for cb in range(8):
                                    ins = tn.matmul(
                                        ph,
                                        w1p[:, buf * 4096 + cb * 512 + 128 * ft2:
                                            buf * 4096 + cb * 512 + 128 * (ft2 + 1)],
                                        x4T[:, cb * T_OWN + GB_T * grp + 512 * pcc:
                                            cb * T_OWN + GB_T * grp + 512 * (pcc + 1)],
                                        start=(cb == 0), stop=(cb == 7))
                                ins.then_inc(sems["HMM"], 1)
                                banchor(fh == 1)
                    tn.wait_ge(sems["HEV"], base["HEV"] + NHG * (fh + 1))
                    for yg in range(NYG):
                        ptl, ch = yg // 2, yg % 2
                        if NYG * fh + yg >= 2:
                            tn.wait_ge(sems["YEV"], base["YEV"] + NYG * fh + yg - 1)
                        py = pY[:, (yg % 2) * 512:(yg % 2 + 1) * 512]
                        for fk in range(16):
                            ins = tn.matmul(
                                py,
                                hT[:, GB_T * fk + 128 * ptl: GB_T * fk + 128 * (ptl + 1)],
                                w2sb[:, 1024 * fk + 512 * ch: 1024 * fk + 512 * (ch + 1)],
                                start=(fk == 0), stop=(fk == 15))
                        ins.then_inc(sems["YMM"], 1)
                        banchor(fh == 1 or yg >= 8)
                if last:
                    while bi[0] < len(bis_ops):
                        bisPE(tn, bis_ops[bi[0]])
                        bi[0] += 1
                    tn.wait_ge(sems["BV"], 2 * NBIS + 2)

            @blk.scalar
            def _(s):
                if grp == 0:
                    s.wait_ge(sems["LD"], 48)
                for fh in range(2):
                    if fh > 0:
                        s.wait_ge(sems["YMM"], base["YMM"] + NYG * fh)
                    for hg in range(NHG):
                        ft, pcc = hg // NPC, hg % NPC
                        s.wait_ge(sems["HMM"], base["HMM"] + NHG * fh + hg + 1)
                        ph = pH[:, (hg % 2) * 512:(hg % 2 + 1) * 512]
                        s.activation(hT[:, GB_T * ft + 512 * pcc: GB_T * ft + 512 * (pcc + 1)],
                                     ph, ACT.Gelu_apprx_tanh,
                                     bias=b1_sb[:, 16 * fh + ft:16 * fh + ft + 1]
                                     ).then_inc(sems["HEV"], 1)

            @blk.vector
            def _(v):
                bi = [0]

                def vanchor(cond):
                    if last and cond and bi[0] < len(bis_ops):
                        bisV(v, bis_ops[bi[0]])
                        bi[0] += 1

                if last:
                    v.memset(bis[0:1, 0:1], -10000.0)
                    v.memset(bis[0:1, 1:2], 10000.0)
                    v.memset(bis[0:1, 2:3], 0.0)
                    v.memset(thr_b[:, 0:1], 0.0)
                    v.drain()
                for fh in range(2):
                    if fh == 1:
                        for _ in range(NHG):
                            vanchor(True)
                    for yg in range(NYG):
                        ptl, ch = yg // 2, yg % 2
                        v.wait_ge(sems["YMM"], base["YMM"] + NYG * fh + yg + 1)
                        py = pY[:, (yg % 2) * 512:(yg % 2 + 1) * 512]
                        a = acc_bf[:, 1024 * (NPTL * grp + ptl) + 512 * ch:
                                   1024 * (NPTL * grp + ptl) + 512 * (ch + 1)]
                        if fh == 0:
                            v.tensor_copy(a, py).then_inc(sems["YEV"], 1)
                        else:
                            v.tensor_copy(ytmp[:, 0:512], py)
                            v.drain()
                            v.tensor_add(a, a, ytmp[:, 0:512]
                                         ).then_inc(sems["YEV"], 1)
                            v.drain()
                        vanchor(fh == 1 or yg >= 8)
                if last:
                    while bi[0] < len(bis_ops):
                        bisV(v, bis_ops[bi[0]])
                        bi[0] += 1

            if last:
                @blk.sync
                def _(sy):
                    sy.wait_ge(sems["CCS"], 1)
                    sy.dma_start(out=scA[:, :].rearrange("p (r i) -> p r i", r=4),
                                 in_=_ap(cc2_out, 0, [[NT, 128], [T_OWN, 4], [1, NT]])
                                 ).then_inc(sems["BLD"], 16)

        C["W2L"] += 32
        C["W1P0"] += 16 * 4
        C["W1P1"] += 16 * 4
        C["HMM"] += NHG * 2
        C["HEV"] += NHG * 2
        C["YMM"] += NYG * 2
        C["YEV"] += NYG * 2

    with nc.Block() as blk:
        @blk.scalar
        def _(s):
            s.activation(epiD[:, 0:NT], scS[:, :], ACT.Tanh, scale=0.5).then_inc(sems["GTV"], 1)

        @blk.vector
        def _(v):
            v.tensor_scalar(gat[:, :], scS[:, :], thr_b[:, 0:1], None, ALU.is_gt)
            v.drain()
            v.wait_ge(sems["GTV"], 1)
            v.tensor_scalar(epiD[:, 0:NT], epiD[:, 0:NT], 0.5, 0.5, ALU.mult, ALU.add)
            v.drain()
            v.tensor_mul(gat[:, :], gat[:, :], epiD[:, 0:NT])
    with nc.Block() as blk:
        @blk.scalar
        def _(s):
            for i in range(NT):
                s.wait_ge(sems["DLNV"], i + 1)
                s.drain()
                eq = (i % 2) * 2 * D
                s.activation(epiD[:, eq + D:eq + 2 * D], epiD[:, eq:eq + D],
                             ACT.Square, bias=lnstD[:, 4 * i:4 * i + 1],
                             accum_out=lnstD[:, 4 * i + 1:4 * i + 2]).then_inc(sems["DSQS"], 1)
                s.wait_ge(sems["DVAE"], i + 1)
                s.activation(lnstD[:, 4 * i + 3:4 * i + 4],
                             lnstD[:, 4 * i + 2:4 * i + 3],
                             ACT.Sqrt).then_inc(sems["DSQT"], 1)

        @blk.vector
        def _(v):
            for i in range(NT):
                ep = (i % 2) * 2 * D
                if i >= 2:
                    v.wait_ge(sems["DSQS"], i - 1)
                v.tensor_copy(epiD[:, ep:ep + D], acc_bf[:, i * 1024:(i + 1) * 1024])
                v.drain()
                v.tensor_add(epiD[:, ep:ep + D], epiD[:, ep:ep + D], b2_sb[:, :])
                v.drain()
                v.tensor_scalar(epiD[:, ep:ep + D], epiD[:, ep:ep + D],
                                gat[:, i:i + 1], None, ALU.mult)
                v.drain()
                v.scalar_tensor_tensor(epiD[:, ep:ep + D], stream[:, i * D:(i + 1) * D],
                                       2.0, epiD[:, ep:ep + D], ALU.mult, ALU.add)
                v.drain()
                v.reduce_sum(lnstD[:, 4 * i + 1:4 * i + 2], epiD[:, ep:ep + D], AX.X)
                v.drain()
                v.tensor_scalar(lnstD[:, 4 * i:4 * i + 1], lnstD[:, 4 * i + 1:4 * i + 2],
                                -1.0 / D, None, ALU.mult).then_inc(sems["DLNV"], 1)
                v.wait_ge(sems["DSQS"], i + 1)
                v.tensor_scalar(lnstD[:, 4 * i + 2:4 * i + 3],
                                lnstD[:, 4 * i + 1:4 * i + 2],
                                1.0 / D, EPS, ALU.mult, ALU.add).then_inc(sems["DVAE"], 1)
                v.wait_ge(sems["DSQT"], i + 1)
                v.reciprocal(lnstD[:, 4 * i + 1:4 * i + 2], lnstD[:, 4 * i + 3:4 * i + 4])
                v.drain()
                v.tensor_mul(lnstD[:, 4 * i + 2:4 * i + 3], lnstD[:, 4 * i:4 * i + 1],
                             lnstD[:, 4 * i + 1:4 * i + 2])
                v.drain()
                if i >= 2:
                    v.wait_ge(sems["FDM0" if i % 2 == 0 else "FDM1"],
                              16 * (i // 2))
                v.tensor_scalar(fin[:, (i % 2) * D:(i % 2 + 1) * D], epiD[:, ep:ep + D],
                                lnstD[:, 4 * i + 1:4 * i + 2],
                                lnstD[:, 4 * i + 2:4 * i + 3],
                                ALU.mult, ALU.add).then_inc(sems["DEPV"], 1)

        @blk.sync
        def _(sy):
            for i in range(NT):
                sy.wait_ge(sems["DEPV"], i + 1)
                sy.dma_start(out=out_hbm[i * 128:(i + 1) * 128, :],
                             in_=fin[:, (i % 2) * D:(i % 2 + 1) * D]).then_inc(
                                 sems["FDM0" if i % 2 == 0 else "FDM1"], 16)

    with nc.Block() as blk:
        @blk.gpsimd
        def _(gp):
            gp.wait_ge(sems["FDM0"], 16 * (NT // 2))
            gp.wait_ge(sems["FDM1"], 16 * (NT // 2))



def build_program(n_phases=N_PHASES):
    nc = bass.Bass("TRN2", target_bir_lowering=False)

    x_ext = nc.dram_tensor("x_ext", [1 + T_PREF + T_OWN, D], F32, kind="ExternalInput").ap()
    wwT = nc.dram_tensor("wwT", [D, D], F32, kind="ExternalInput").ap()
    wrT = nc.dram_tensor("wrT", [D, D], F32, kind="ExternalInput").ap()
    decay = nc.dram_tensor("decay", [1, 1], F32, kind="ExternalInput").ap()
    log_alpha = nc.dram_tensor("log_alpha", [1, 1], F32, kind="ExternalInput").ap()
    iota4 = nc.dram_tensor("iota4", [128, 4], F32, kind="ExternalInput").ap()
    gwex4 = nc.dram_tensor("gwex4", [128, 4], F32, kind="ExternalInput").ap()
    diffsT = nc.dram_tensor("diffsT", [SCL, SCL], F32, kind="ExternalInput").ap()
    causalT = nc.dram_tensor("causalT", [SCL, SCL], F32, kind="ExternalInput").ap()
    identity = nc.dram_tensor("identity", [128, 128], F32, kind="ExternalInput").ap()
    onesr = nc.dram_tensor("onesr", [1, 128], F32, kind="ExternalInput").ap()
    if n_phases >= 2:
        gate_io = nc.dram_tensor("gate_io", [D, 8], F32, kind="ExternalInput").ap()
        gammacol = nc.dram_tensor("gammacol", [128, 8], F32, kind="ExternalInput").ap()
        betacol = nc.dram_tensor("betacol", [128, 8], F32, kind="ExternalInput").ap()
        w1_io = nc.dram_tensor("w1_io", [8, D, 2048], F32, kind="ExternalInput").ap()
        w2_io = nc.dram_tensor("w2_io", [8, 2048, D], F32, kind="ExternalInput").ap()
    if n_phases >= 3:
        u_io = nc.dram_tensor("u_io", [D, D], F32, kind="ExternalInput").ap()
        v_io = nc.dram_tensor("v_io", [D, D], F32, kind="ExternalInput").ap()
        u_nat = nc.dram_tensor("u_nat", [D, D], F32, kind="ExternalInput").ap()
        v_nat = nc.dram_tensor("v_nat", [D, D], F32, kind="ExternalInput").ap()
        cc_in = nc.dram_tensor("cc_in", [CCIN_N], BF16)
        cc_out = nc.dram_tensor("cc_out", [4 * CCIN_N], BF16)
        y1d = nc.dram_tensor("y1d", [8 * 128 * T_OWN], BF16)
        y2d = nc.dram_tensor("y2d", [8 * 128 * T_OWN], BF16)
    if n_phases >= 4:
        W1d = nc.dram_tensor("W1d", [D, 4096], F32, kind="ExternalInput").ap()
        W2d = nc.dram_tensor("W2d", [4096, D], F32, kind="ExternalInput").ap()
        b1col = nc.dram_tensor("b1col", [128, 32], F32, kind="ExternalInput").ap()
        b2rep = nc.dram_tensor("b2rep", [128, D], F32, kind="ExternalInput").ap()
        wrcol = nc.dram_tensor("wrcol", [128, 8], F32, kind="ExternalInput").ap()
        cc2_in = nc.dram_tensor("cc2_in", [T_OWN], F32)
        cc2_out = nc.dram_tensor("cc2_out", [4 * T_OWN], F32)
        W1B = nc.dram_tensor("w1bf", [D, 4096], BF16).ap()
        W2B = nc.dram_tensor("w2bf", [4096, D], BF16).ap()
    out_hbm = nc.dram_tensor("out", [T_OWN, D], F32, kind="ExternalOutput").ap()

    with ExitStack() as es:
        stream = es.enter_context(nc.sbuf_tensor("stream", [128, NT * D], BF16))
        ident_b = es.enter_context(nc.sbuf_tensor("ident_b", [128, 128], BF16))
        onesr_f = es.enter_context(nc.sbuf_tensor("onesr_f", [1, 128], F32))
        scal = es.enter_context(nc.sbuf_tensor("scal", [128, 16], F32))
        s0_ld = es.enter_context(nc.semaphore("s0_ld"))
        s0_mm = es.enter_context(nc.semaphore("s0_mm"))

        with nc.Block() as blk:
            c_iota4 = nc.sbuf_tensor("c_iota4", [128, 4], F32)
            c_gwex4 = nc.sbuf_tensor("c_gwex4", [128, 4], F32)
            c_two = nc.sbuf_tensor("c_two", [1, 2], F32)
            p_a0 = nc.psum_tensor("p_a0", [128, 8], F32)
            with c_iota4 as ci, c_gwex4 as cg, c_two as ct, p_a0 as pa:
                @blk.gpsimd
                def _(gp):
                    gp.dma_start(out=ident_b[:, :], in_=identity).then_inc(s0_ld, 16)
                    gp.dma_start(out=onesr_f[:, :], in_=onesr).then_inc(s0_ld, 16)
                    gp.dma_start(out=ci[:, :], in_=iota4).then_inc(s0_ld, 16)
                    gp.dma_start(out=cg[:, :], in_=gwex4).then_inc(s0_ld, 16)
                    gp.dma_start(out=ct[0:1, 0:1], in_=decay).then_inc(s0_ld, 16)
                    gp.dma_start(out=ct[0:1, 1:2], in_=log_alpha).then_inc(s0_ld, 16)

                @blk.tensor
                def _(tn):
                    tn.wait_ge(s0_ld, 96)
                    tn.matmul(pa[:, 0:1], onesr_f[0:1, :], ct[0:1, 0:1],
                              start=True, stop=True)
                    tn.matmul(pa[:, 1:2], onesr_f[0:1, :], ct[0:1, 1:2],
                              start=True, stop=True).then_inc(s0_mm, 1)

                @blk.scalar
                def _(s):
                    s.wait_ge(s0_mm, 1)
                    s.copy(scal[:, 0:1], pa[:, 0:1])
                    s.activation(scal[:, 1:2], pa[:, 0:1], ACT.Tanh, scale=0.5)
                    s.activation(scal[:, 3:4], pa[:, 1:2], ACT.Exp)
                    s.drain()
                    s.activation(scal[:, 1:2], scal[:, 1:2], ACT.Copy, scale=0.5, bias=0.5)
                    s.drain()
                    s.activation(scal[:, 2:3], scal[:, 1:2], ACT.Ln)
                    s.drain()
                    s.activation(scal[:, 4:5], scal[:, 2:3], ACT.Exp, scale=float(SCL))
                    for t in range(4):
                        s.activation(scal[:, 5 + t:6 + t], ci[:, t:t + 1], ACT.Exp,
                                     scale=scal[:, 2:3])
                        s.activation(scal[:, 9 + t:10 + t], cg[:, t:t + 1], ACT.Exp,
                                     scale=scal[:, 2:3])

        with ExitStack() as es_a:
            emit_phase_a(nc, es_a, dict(
                x_ext=x_ext, wwT=wwT, wrT=wrT, diffsT=diffsT, causalT=causalT,
                stream=stream, ident_b=ident_b, scal=scal))

        if n_phases >= 2:
            with ExitStack() as es_b:
                emit_phase_b(nc, es_b, dict(
                    gate_io=gate_io, gammacol=gammacol, betacol=betacol,
                    w1_io=w1_io, w2_io=w2_io, stream=stream,
                    ident_b=ident_b, onesr_f=onesr_f))

        if n_phases >= 3:
            cast_g = {}
            if n_phases >= 4:
                cast_g = dict(
                    W1E=W1d, W2E=W2d, W1B=W1B, W2B=W2B,
                    CWI=[es.enter_context(nc.semaphore("cwi%d" % q)) for q in range(4)],
                    CWO=[es.enter_context(nc.semaphore("cwo%d" % q)) for q in range(4)])
            with ExitStack() as es_c:
                emit_phase_c(nc, es_c, dict(
                    u_io=u_io, v_io=v_io, u_nat=u_nat, v_nat=v_nat,
                    cc_in=cc_in, cc_out=cc_out, y1d=y1d, y2d=y2d,
                    stream=stream, ident_b=ident_b, **cast_g))

        if n_phases >= 4:
            with ExitStack() as es_d:
                emit_phase_d(nc, es_d, dict(
                    W1B=W1B, W2B=W2B, CWO=cast_g["CWO"],
                    b1col=b1col, b2rep=b2rep, wrcol=wrcol,
                    cc2_in=cc2_in, cc2_out=cc2_out, stream=stream,
                    ident_b=ident_b, onesr_f=onesr_f, out_hbm=out_hbm))


        if n_phases >= 4:
            return nc
        s_fcp = es.enter_context(nc.semaphore("s_fcp"))
        s_fdm0 = es.enter_context(nc.semaphore("s_fdm0"))
        s_fdm1 = es.enter_context(nc.semaphore("s_fdm1"))
        with nc.Block() as blk:
            fin = nc.sbuf_tensor("fin", [128, 2 * D], F32)
            with fin as fb:
                @blk.vector
                def _(v):
                    for i in range(NT):
                        if i >= 2:
                            v.wait_ge(s_fdm0 if i % 2 == 0 else s_fdm1,
                                      16 * (i // 2))
                        fo = (i % 2) * D
                        v.tensor_copy(fb[:, fo:fo + D],
                                      stream[:, i * D:(i + 1) * D]).then_inc(s_fcp, 1)

                @blk.sync
                def _(sy):
                    for i in range(NT):
                        sy.wait_ge(s_fcp, i + 1)
                        fo = (i % 2) * D
                        sy.dma_start(out=out_hbm[i * 128:(i + 1) * 128, :],
                                     in_=fb[:, fo:fo + D]).then_inc(
                                         s_fdm0 if i % 2 == 0 else s_fdm1, 16)

                @blk.gpsimd
                def _(gp):
                    gp.wait_ge(s_fdm0, 16 * (NT // 2))
                    gp.wait_ge(s_fdm1, 16 * (NT // 2))

    return nc



_CONSTS = None


def _consts():
    global _CONSTS
    if _CONSTS is None:
        p = np.arange(128, dtype=np.float32)[:, None]
        iota4 = np.concatenate([p + 128 * t for t in range(4)], axis=1)
        gwex4 = np.concatenate([(SCL - 1) - (128 * t + p) for t in range(4)], axis=1)
        k = np.arange(SCL, dtype=np.float32)
        diffsT = (k[None, :] - 1.0 - k[:, None]).astype(np.float32)
        causalT = (k[None, :] > k[:, None]).astype(np.float32)
        identity = np.eye(128, dtype=np.float32)
        onesr = np.ones((1, 128), np.float32)
        _CONSTS = dict(iota4=np.ascontiguousarray(iota4),
                       gwex4=np.ascontiguousarray(gwex4),
                       diffsT=np.ascontiguousarray(diffsT),
                       causalT=np.ascontiguousarray(causalT),
                       identity=identity, onesr=onesr)
    return _CONSTS


def make_in_maps(inputs):
    x = np.ascontiguousarray(np.asarray(inputs["L0_attn_out"], dtype=np.float32))
    wwT = np.ascontiguousarray(np.asarray(inputs["L0_attn_W_write"], np.float32).T)
    wrT = np.ascontiguousarray(np.asarray(inputs["L0_attn_W_read"], np.float32).T)
    dec = np.asarray(inputs["L0_attn_decay"], np.float32).reshape(1, 1)
    la = np.asarray(inputs["L0_attn_log_alpha"], np.float32).reshape(1, 1)
    cst = _consts()
    extra = {}
    if N_PHASES >= 2:
        extra["gate_io"] = np.ascontiguousarray(np.asarray(inputs["L0_ffn_gate_w"], np.float32).T)
        extra["gammacol"] = np.ascontiguousarray(np.asarray(inputs["L0_ffn_ln_gamma"], np.float32).reshape(8, 128).T)
        extra["betacol"] = np.ascontiguousarray(np.asarray(inputs["L0_ffn_ln_beta"], np.float32).reshape(8, 128).T)
        extra["w1_io"] = np.ascontiguousarray(np.asarray(inputs["L0_ffn_w1"], np.float32).transpose(0, 2, 1))
        extra["w2_io"] = np.ascontiguousarray(np.asarray(inputs["L0_ffn_w2"], np.float32).transpose(0, 2, 1))
    if N_PHASES >= 3:
        tu = np.asarray(inputs["L1_attn_ternary_factor_u"], np.float32)
        tv = np.asarray(inputs["L1_attn_ternary_factor_v"], np.float32)
        extra["u_io"] = np.ascontiguousarray(tu.T)
        extra["v_io"] = np.ascontiguousarray(tv.T)
        extra["u_nat"] = np.ascontiguousarray(tu)
        extra["v_nat"] = np.ascontiguousarray(tv)
    if N_PHASES >= 4:
        extra["W1d"] = np.ascontiguousarray(np.asarray(inputs["L1_ffn_W1"], np.float32))
        extra["W2d"] = np.ascontiguousarray(np.asarray(inputs["L1_ffn_W2"], np.float32))
        extra["b1col"] = np.ascontiguousarray(np.asarray(inputs["L1_ffn_b1"], np.float32).reshape(32, 128).T)
        extra["b2rep"] = np.ascontiguousarray(np.broadcast_to(
            np.asarray(inputs["L1_ffn_b2"], np.float32), (128, D)))
        extra["wrcol"] = np.ascontiguousarray(np.asarray(inputs["L1_ffn_w_router"], np.float32).reshape(8, 128).T)
    in_maps = []
    for c in range(8):
        b, j = c // 4, c % 4
        xe = np.zeros((1 + T_PREF + T_OWN, D), np.float32)
        if j:
            xe[1 + T_PREF - T_OWN * j: 1 + T_PREF] = x[b, :T_OWN * j]
        xe[1 + T_PREF:] = x[b, T_OWN * j: T_OWN * (j + 1)]
        im = dict(x_ext=xe, wwT=wwT, wrT=wrT, decay=dec, log_alpha=la, **cst, **extra)
        in_maps.append(im)
    return in_maps


def kernel(**inputs):
    nc = build_program()
    in_maps = make_in_maps(inputs)
    res = run_bass_kernel_spmd(nc, in_maps, core_ids=list(range(8)))
    out = np.empty((2, 8192, D), np.float32)
    for c in range(8):
        b, j = c // 4, c % 4
        out[b, T_OWN * j: T_OWN * (j + 1)] = res.results[c]["out"]
    return out


# version 15
# speedup vs baseline: 1.2544x; 1.0078x over previous
from contextlib import ExitStack

import numpy as np
import ml_dtypes
import concourse.bass as bass
import concourse.mybir as mybir
from concourse.bass_utils import run_bass_kernel_spmd

F32 = mybir.dt.float32
BF16 = mybir.dt.bfloat16
ACT = mybir.ActivationFunctionType
ALU = mybir.AluOpType
AX = mybir.AxisListType

D = 1024
T_OWN = 2048
T_PREF = 6144
NT = T_OWN // 128
SCL = 512
NSC_PRE = T_PREF // SCL
NSC_OWN = T_OWN // SCL
NSC = NSC_PRE + NSC_OWN
XTW = 513
EPS = 1e-5

N_PHASES = 4


def _ap(t, off, dims):
    return bass.AP(t, off, dims)



def _clear_sems(nc, sems):
    with nc.Block() as blk:
        @blk.gpsimd
        def _(gp):
            for s in sems.values():
                gp.sem_clear(s)


def emit_phase_a(nc, es, g):
    x_ext, wwT, wrT = g["x_ext"], g["wwT"], g["wrT"]
    diffsT, causalT = g["diffsT"], g["causalT"]
    stream, ident_b, scal = g["stream"], g["ident_b"], g["scal"]

    wwT_sb = es.enter_context(nc.sbuf_tensor("wwT_sb", [128, 8 * D], BF16))
    wrT_sb = es.enter_context(nc.sbuf_tensor("wrT_sb", [128, 8 * D], BF16))
    state_bf = es.enter_context(nc.sbuf_tensor("state_bf", [128, 8 * D], BF16))
    xT = es.enter_context(nc.sbuf_tensor("xT", [128, 8 * XTW], BF16))
    xbf = es.enter_context(nc.sbuf_tensor("xbf", [128, 2 * 4 * D], BF16))
    wkbf = es.enter_context(nc.sbuf_tensor("wkbf", [128, 2 * 4 * D], BF16))
    xf32 = es.enter_context(nc.sbuf_tensor("xf32", [128, 2 * D], F32))
    vgw = es.enter_context(nc.sbuf_tensor("vgw", [128, 4 * D], BF16))
    vtm = es.enter_context(nc.sbuf_tensor("vtm", [128, 4 * D], BF16))
    mt = es.enter_context(nc.sbuf_tensor("mt", [128, 4 * SCL], F32))
    smt = es.enter_context(nc.sbuf_tensor("smt", [128, 4 * SCL], BF16))
    inter_sb = es.enter_context(nc.sbuf_tensor("inter_sb", [128, D], F32))
    reads_sb = es.enter_context(nc.sbuf_tensor("reads_sb", [128, D], BF16))
    readsT_sb = es.enter_context(nc.sbuf_tensor("readsT_sb", [128, D], BF16))
    epi = es.enter_context(nc.sbuf_tensor("epi", [128, 4 * D], F32))
    lnst = es.enter_context(nc.sbuf_tensor("lnst", [128, 16], F32))

    pv = es.enter_context(nc.psum_tensor("pv", [128, 1024], F32))
    pio_a = es.enter_context(nc.psum_tensor("pio_a", [128, 1024], F32))
    pio_b = es.enter_context(nc.psum_tensor("pio_b", [128, 1024], F32))
    pS = es.enter_context(nc.psum_tensor("pS", [128, 512], F32))
    ptr = es.enter_context(nc.psum_tensor("ptr", [128, 1024], BF16))

    sems = {}
    for n in ["LD0", "LD1", "XF0", "XF1", "MT", "TTR", "TEV", "VMM", "VEV",
              "SMM", "SMK", "IMM", "NMM", "REV", "RCB", "RTT", "RTE", "ROM",
              "EPS_", "EPV", "LNV", "SQS", "VAE", "SQT", "AMM", "AUP"]:
        sems[n] = es.enter_context(nc.semaphore("a_" + n))

    with nc.Block() as blk:
        @blk.gpsimd
        def _(gp):
            gp.dma_start(out=wwT_sb[:, :].rearrange("p (b e) -> p b e", b=8),
                         in_=wwT.rearrange("(b p) e -> p b e", p=128)).then_inc(sems["LD0"], 16)
            gp.dma_start(out=wrT_sb[:, :].rearrange("p (b e) -> p b e", b=8),
                         in_=wrT.rearrange("(b p) e -> p b e", p=128)).then_inc(sems["LD0"], 16)
            gp.dma_start(out=smt[:, :].rearrange("k (kt p) -> k kt p", kt=4),
                         in_=causalT.rearrange("(kt k) p -> k kt p", k=128)).then_inc(sems["LD0"], 16)

        @blk.sync
        def _(sy):
            sy.dma_start(out=mt[:, :].rearrange("k (kt p) -> k kt p", kt=4),
                         in_=diffsT.rearrange("(kt k) p -> k kt p", k=128)).then_inc(sems["MT"], 16)

        @blk.scalar
        def _(s):
            s.wait_ge(sems["LD0"], 48)
            s.wait_ge(sems["MT"], 16)
            for kt in range(4):
                s.activation(mt[:, kt * SCL:(kt + 1) * SCL],
                             mt[:, kt * SCL:(kt + 1) * SCL], ACT.Exp,
                             scale=scal[:, 2:3]).then_inc(sems["TEV"], 1)

        @blk.vector
        def _(v):
            v.memset(state_bf[:, :], 0.0)
            v.memset(xT[:, :], 0.0)
            v.wait_ge(sems["TEV"], 4)
            for kt in range(4):
                v.tensor_mul(mt[:, kt * SCL:(kt + 1) * SCL],
                             mt[:, kt * SCL:(kt + 1) * SCL],
                             smt[:, kt * SCL:(kt + 1) * SCL])

    Z = dict(LD0=48, LD1=0, XF0=0, XF1=0, TTR=0, TEV=4, VMM=0, VEV=0, SMM=0,
             SMK=0, IMM=0, NMM=0, REV=0, RCB=0, RTT=0, RTE=0, ROM=0,
             EPS_=0, EPV=0, LNV=0, SQS=0, VAE=0, SQT=0, AMM=0, AUP=0)
    bases = []
    for sc in range(NSC):
        bases.append(dict(Z))
        own = sc >= NSC_PRE
        Z["LD0" if sc % 2 == 0 else "LD1"] += 128
        Z["TTR"] += 4
        Z["TEV"] += 4
        Z["VMM"] += 4
        Z["VEV"] += 4
        Z["AMM"] += 8
        Z["AUP"] += 8
        if own:
            Z["XF0"] += 32
            Z["XF1"] += 32
            for nm in ["SMM", "SMK", "IMM", "NMM", "REV", "RCB", "RTT",
                       "RTE", "ROM", "EPS_", "EPV", "LNV", "SQS", "VAE", "SQT"]:
                Z[nm] += 4

    with nc.Block() as blk:

        @blk.gpsimd
        def _(gp):
            for sc in range(NSC):
                base = bases[sc]
                po = (sc % 2) * 4 * D
                if sc >= 2:
                    gp.wait_ge(sems["AMM"], bases[sc - 1]["AMM"])
                for t in range(4):
                    r = 1 + sc * SCL + t * 128
                    gp.dma_start(out=xbf[:, po + t * D:po + (t + 1) * D],
                                 in_=x_ext[r:r + 128, :]).then_inc(
                                     sems["LD0" if sc % 2 == 0 else "LD1"], 16)
                    gp.dma_start(out=wkbf[:, po + t * D:po + (t + 1) * D],
                                 in_=x_ext[r - 1:r + 127, :]).then_inc(
                                     sems["LD0" if sc % 2 == 0 else "LD1"], 16)

        @blk.sync
        def _(sy):
            for sc in range(NSC):
                base = bases[sc]
                if sc < NSC_PRE:
                    continue
                for t in range(4):
                    r = 1 + sc * SCL + t * 128
                    gi = (sc - NSC_PRE) * 4 + t
                    if gi >= 2:
                        sy.wait_ge(sems["EPV"], gi - 1)
                    sy.dma_start(out=xf32[:, (t % 2) * D:(t % 2 + 1) * D],
                                 in_=x_ext[r:r + 128, :]).then_inc(
                                     sems["XF0" if t % 2 == 0 else "XF1"], 16)

        @blk.tensor
        def _(tn):
            for sc in range(NSC):
                base = bases[sc]
                own = sc >= NSC_PRE
                po = (sc % 2) * 4 * D
                ldk = "LD0" if sc % 2 == 0 else "LD1"
                tn.wait_ge(sems[ldk], base[ldk] + 128)
                tn.wait_ge(sems["AUP"], base["AUP"])
                tn.wait_ge(sems["RTE"], base["RTE"])
                for t in range(4):
                    tn.wait_ge(sems["TEV"], base["TEV"] + t)
                    for jb in range(8):
                        ins = tn.transpose(ptr[:, jb * 128:(jb + 1) * 128],
                                           xbf[:, po + t * D + jb * 128:po + t * D + (jb + 1) * 128],
                                           ident_b[:, :])
                        if jb == 7:
                            ins.then_inc(sems["TTR"], 1)
                    tn.wait_ge(sems["TEV"], base["TEV"] + t + 1)
                    tn.wait_ge(sems["VEV"], base["VEV"] + t)
                    for hh in range(2):
                        for jb in range(8):
                            ins = tn.matmul(pv[:, hh * 512:(hh + 1) * 512],
                                            xT[:, jb * XTW + 1 + t * 128: jb * XTW + 1 + (t + 1) * 128],
                                            wwT_sb[:, jb * D + hh * 512: jb * D + (hh + 1) * 512],
                                            start=(jb == 0), stop=(jb == 7))
                    ins.then_inc(sems["VMM"], 1)
                if own:
                    for kt in range(4):
                        tn.wait_ge(sems["SMK"], base["SMK"] + kt)
                        n = 512 - kt * 128
                        for jb in range(8):
                            ins = tn.matmul(pS[:, 0:n],
                                            xT[:, jb * XTW + kt * 128: jb * XTW + (kt + 1) * 128],
                                            xT[:, jb * XTW + 1 + kt * 128: jb * XTW + 513],
                                            start=(jb == 0), stop=(jb == 7))
                        ins.then_inc(sems["SMM"], 1)
                    tn.wait_ge(sems["VEV"], base["VEV"] + 4)
                    tn.wait_ge(sems["AUP"], base["AUP"])
                    for pt in range(4):
                        tn.wait_ge(sems["REV"], base["REV"] + pt)
                        for hh in range(2):
                            for jb in range(8):
                                ins = tn.matmul(pio_a[:, hh * 512:(hh + 1) * 512],
                                                xT[:, jb * XTW + 1 + pt * 128: jb * XTW + 1 + (pt + 1) * 128],
                                                state_bf[:, jb * D + hh * 512: jb * D + (hh + 1) * 512],
                                                start=(jb == 0), stop=(jb == 7))
                        ins.then_inc(sems["IMM"], 1)
                        tn.wait_ge(sems["SMK"], base["SMK"] + pt + 1)
                        tn.wait_ge(sems["RCB"], base["RCB"] + pt)
                        for hh in range(2):
                            for kt in range(pt + 1):
                                ins = tn.matmul(pio_b[:, hh * 512:(hh + 1) * 512],
                                                smt[:, kt * SCL + pt * 128: kt * SCL + (pt + 1) * 128],
                                                vtm[:, kt * D + hh * 512: kt * D + (hh + 1) * 512],
                                                start=(kt == 0), stop=(kt == pt))
                        ins.then_inc(sems["NMM"], 1)
                        tn.wait_ge(sems["RCB"], base["RCB"] + pt + 1)
                        tn.wait_ge(sems["RTE"], base["RTE"] + pt)
                        for jb in range(8):
                            ins = tn.transpose(ptr[:, jb * 128:(jb + 1) * 128],
                                               reads_sb[:, jb * 128:(jb + 1) * 128],
                                               ident_b[:, :])
                        ins.then_inc(sems["RTT"], 1)
                        tn.wait_ge(sems["RTE"], base["RTE"] + pt + 1)
                        tn.wait_ge(sems["EPS_"], base["EPS_"] + pt)
                        for hh in range(2):
                            for jb in range(8):
                                ins = tn.matmul(pv[:, hh * 512:(hh + 1) * 512],
                                                readsT_sb[:, jb * 128:(jb + 1) * 128],
                                                wrT_sb[:, jb * D + hh * 512: jb * D + (hh + 1) * 512],
                                                start=(jb == 0), stop=(jb == 7))
                        ins.then_inc(sems["ROM"], 1)
                for et in range(8):
                    pA = pv if et % 2 == 0 else pio_a
                    if et == 0:
                        if own:
                            tn.wait_ge(sems["EPS_"], base["EPS_"] + 4)
                        else:
                            tn.wait_ge(sems["VEV"], base["VEV"] + 4)
                    if et == 1:
                        if own:
                            tn.wait_ge(sems["REV"], base["REV"] + 4)
                        else:
                            tn.wait_ge(sems["AUP"], base["AUP"])
                    if et >= 2:
                        tn.wait_ge(sems["AUP"], base["AUP"] + et - 1)
                    for hh in range(2):
                        for t in range(4):
                            ins = tn.matmul(pA[:, hh * 512:(hh + 1) * 512],
                                            wkbf[:, po + t * D + et * 128: po + t * D + (et + 1) * 128],
                                            vgw[:, t * D + hh * 512: t * D + (hh + 1) * 512],
                                            start=(t == 0), stop=(t == 3))
                    ins.then_inc(sems["AMM"], 1)

        @blk.scalar
        def _(s):
            for sc in range(NSC):
                base = bases[sc]
                own = sc >= NSC_PRE
                if sc > 0:
                    if sc - 1 >= NSC_PRE:
                        s.wait_ge(sems["IMM"], bases[sc - 1]["IMM"] + 4)
                    else:
                        s.wait_ge(sems["VMM"], bases[sc - 1]["VMM"] + 4)
                    s.copy(_ap(xT, 0, [[8 * XTW, 128], [XTW, 8], [1, 1]]),
                           _ap(xT, 512, [[8 * XTW, 128], [XTW, 8], [1, 1]]))
                for t in range(4):
                    s.wait_ge(sems["TTR"], base["TTR"] + t + 1)
                    ins = s.activation(
                        _ap(xT, 1 + t * 128, [[8 * XTW, 128], [XTW, 8], [1, 128]]),
                        _ap(ptr, 0, [[1024, 128], [128, 8], [1, 128]]),
                        ACT.Copy)
                    ins.then_inc(sems["TEV"], 1)
                    s.wait_ge(sems["VMM"], base["VMM"] + t + 1)
                    ins = s.activation(vgw[:, t * D:(t + 1) * D], pv[:, :], ACT.Copy,
                                       scale=scal[:, 9 + t:10 + t])
                    if own:
                        ins = s.activation(vtm[:, t * D:(t + 1) * D], pv[:, :], ACT.Copy)
                    ins.then_inc(sems["VEV"], 1)
                if own:
                    for pt in range(5):
                        if pt > 0:
                            eq = ((pt - 1) % 2) * 2 * D
                            q = pt - 1
                            s.wait_ge(sems["LNV"], base["LNV"] + pt)
                            s.activation(epi[:, eq:eq + D], epi[:, eq + D:eq + 2 * D],
                                         ACT.Square, bias=lnst[:, 12 + q:13 + q],
                                         accum_out=lnst[:, 0 + q:1 + q]).then_inc(sems["SQS"], 1)
                            s.wait_ge(sems["VAE"], base["VAE"] + pt)
                            s.activation(lnst[:, 8 + q:9 + q], lnst[:, 4 + q:5 + q],
                                         ACT.Sqrt).then_inc(sems["SQT"], 1)
                        if pt == 4:
                            break
                        s.wait_ge(sems["IMM"], base["IMM"] + pt + 1)
                        s.wait_ge(sems["RCB"], base["RCB"] + pt)
                        s.activation(inter_sb[:, :], pio_a[:, :], ACT.Copy,
                                     scale=scal[:, 5 + pt:6 + pt]).then_inc(sems["REV"], 1)
                        s.wait_ge(sems["RTT"], base["RTT"] + pt + 1)
                        if pt > 0:
                            s.wait_ge(sems["ROM"], base["ROM"] + pt)
                        s.copy(readsT_sb[:, :], ptr[:, :]).then_inc(sems["RTE"], 1)
                        s.wait_ge(sems["ROM"], base["ROM"] + pt + 1)
                        if pt >= 2:
                            s.wait_ge(sems["EPV"], base["EPV"] + pt - 1)
                        ep = (pt % 2) * 2 * D
                        s.activation(epi[:, ep:ep + D], pv[:, :], ACT.Copy,
                                     scale=scal[:, 3:4]).then_inc(sems["EPS_"], 1)

        @blk.vector
        def _(v):
            for sc in range(NSC):
                base = bases[sc]
                own = sc >= NSC_PRE
                if own:
                    for kt in range(4):
                        n = 512 - kt * 128
                        v.wait_ge(sems["SMM"], base["SMM"] + kt + 1)
                        v.tensor_mul(smt[:, kt * SCL + kt * 128: kt * SCL + 512],
                                     pS[:, 0:n],
                                     mt[:, kt * SCL + kt * 128: kt * SCL + 512]
                                     ).then_inc(sems["SMK"], 1)
                    for pt in range(4):
                        ep = (pt % 2) * 2 * D
                        v.wait_ge(sems["NMM"], base["NMM"] + pt + 1)
                        v.wait_ge(sems["REV"], base["REV"] + pt + 1)
                        if pt > 0:
                            v.wait_ge(sems["RTT"], base["RTT"] + pt)
                        v.tensor_add(reads_sb[:, :], inter_sb[:, :],
                                     pio_b[:, :]).then_inc(sems["RCB"], 1)
                        gi = (sc - NSC_PRE) * 4 + pt
                        v.wait_ge(sems["EPS_"], base["EPS_"] + pt + 1)
                        v.wait_ge(sems["XF0" if pt % 2 == 0 else "XF1"],
                                  base["XF0" if pt % 2 == 0 else "XF1"] + 16 * (pt // 2 + 1))
                        v.scalar_tensor_tensor(epi[:, ep + D:ep + 2 * D],
                                               xf32[:, (pt % 2) * D:(pt % 2 + 1) * D], 2.0,
                                               epi[:, ep:ep + D], ALU.mult, ALU.add)
                        v.drain()
                        v.reduce_sum(lnst[:, 8 + pt:9 + pt], epi[:, ep + D:ep + 2 * D], AX.X)
                        v.drain()
                        v.tensor_scalar(lnst[:, 12 + pt:13 + pt], lnst[:, 8 + pt:9 + pt],
                                        -1.0 / D, None, ALU.mult).then_inc(sems["LNV"], 1)
                        v.wait_ge(sems["SQS"], base["SQS"] + pt + 1)
                        v.tensor_scalar(lnst[:, 4 + pt:5 + pt], lnst[:, 0 + pt:1 + pt],
                                        1.0 / D, EPS, ALU.mult, ALU.add).then_inc(sems["VAE"], 1)
                        v.wait_ge(sems["SQT"], base["SQT"] + pt + 1)
                        v.reciprocal(lnst[:, 0 + pt:1 + pt], lnst[:, 8 + pt:9 + pt])
                        v.drain()
                        v.tensor_mul(lnst[:, 4 + pt:5 + pt], lnst[:, 12 + pt:13 + pt],
                                     lnst[:, 0 + pt:1 + pt])
                        v.drain()
                        oc = gi * D
                        v.tensor_scalar(stream[:, oc:oc + D], epi[:, ep + D:ep + 2 * D],
                                        lnst[:, 0 + pt:1 + pt], lnst[:, 4 + pt:5 + pt],
                                        ALU.mult, ALU.add).then_inc(sems["EPV"], 1)
                for et in range(8):
                    pA = pv if et % 2 == 0 else pio_a
                    v.wait_ge(sems["AMM"], base["AMM"] + et + 1)
                    v.scalar_tensor_tensor(state_bf[:, et * D:(et + 1) * D],
                                           state_bf[:, et * D:(et + 1) * D],
                                           scal[:, 4:5], pA[:, :],
                                           ALU.mult, ALU.add).then_inc(sems["AUP"], 1)
    _clear_sems(nc, sems)



def emit_phase_b(nc, es, g):
    gate_io, gammacol, betacol = g["gate_io"], g["gammacol"], g["betacol"]
    w1_io, w2_io = g["w1_io"], g["w2_io"]
    stream, ident_b, onesr_f = g["stream"], g["ident_b"], g["onesr_f"]

    x2T = es.enter_context(nc.sbuf_tensor("x2T", [128, 8 * T_OWN], BF16))
    hT = es.enter_context(nc.sbuf_tensor("hT", [128, 16 * 1024], BF16))
    w1p = es.enter_context(nc.sbuf_tensor("w1p", [128, 2 * 4096], BF16))
    w2sb = es.enter_context(nc.sbuf_tensor("w2sb", [128, 16 * 1024], BF16))
    acc = es.enter_context(nc.sbuf_tensor("acc", [128, 8 * 1024], F32))
    comb = es.enter_context(nc.sbuf_tensor("comb", [128, 16 * 8], F32))
    gsc = es.enter_context(nc.sbuf_tensor("gsc", [128, 16 * 16], F32))
    gv8 = es.enter_context(nc.sbuf_tensor("gv8", [128, 16 * 24], F32))
    gate_sb = es.enter_context(nc.sbuf_tensor("gate_sb", [128, 64], F32))
    gam_sb = es.enter_context(nc.sbuf_tensor("gam_sb", [128, 8], F32))
    bet_sb = es.enter_context(nc.sbuf_tensor("bet_sb", [128, 8], F32))
    g2 = es.enter_context(nc.sbuf_tensor("g2", [128, 64], BF16))
    ceb = es.enter_context(nc.sbuf_tensor("ceb", [128, 8], F32))
    beb = es.enter_context(nc.sbuf_tensor("beb", [128, 8], F32))
    cb_row = es.enter_context(nc.sbuf_tensor("cb_row", [1, 16], F32))
    sqdump = es.enter_context(nc.sbuf_tensor("sqdump", [128, D], F32))
    epiB = es.enter_context(nc.sbuf_tensor("epiB", [128, 4 * D], F32))
    lnstB = es.enter_context(nc.sbuf_tensor("lnstB", [128, 32], F32))

    psB_init = ExitStack()
    pG = psB_init.enter_context(nc.psum_tensor("pG", [128, 8], F32))
    pT = psB_init.enter_context(nc.psum_tensor("pT", [1, 16], F32))
    ptrB = psB_init.enter_context(nc.psum_tensor("ptrB", [128, 1024], BF16))

    sems = {}
    for n in ["LD", "TT", "TE", "G2V", "GT0", "GS0", "GBC", "GT", "GA", "GB",
              "GC_", "GD", "GE", "GF", "CMB", "W1P0", "W1P1", "W2L", "HMM",
              "HEV", "HPC", "YMM", "YEV", "BLNV", "BSQS", "BVAE", "BSQT", "BEPV"]:
        sems[n] = es.enter_context(nc.semaphore("b_" + n))

    with nc.Block() as blk:
        @blk.gpsimd
        def _(gp):
            gp.dma_start(out=gate_sb[:, :].rearrange("p (cb e) -> p cb e", cb=8),
                         in_=gate_io.rearrange("(cb p) e -> p cb e", p=128)).then_inc(sems["LD"], 16)
            gp.dma_start(out=gam_sb[:, :], in_=gammacol).then_inc(sems["LD"], 16)
            gp.dma_start(out=bet_sb[:, :], in_=betacol).then_inc(sems["LD"], 16)

        @blk.tensor
        def _(tn):
            for i in range(NT):
                tn.wait_ge(sems["TE"], i)
                for jb in range(8):
                    ins = tn.transpose(ptrB[:, jb * 128:(jb + 1) * 128],
                                       stream[:, i * D + jb * 128:i * D + (jb + 1) * 128],
                                       ident_b[:, :])
                ins.then_inc(sems["TT"], 1)
            tn.wait_ge(sems["LD"], 48)
            for cb in range(8):
                tn.matmul(pT[0:1, 0:8], gam_sb[:, cb:cb + 1],
                          gate_sb[:, cb * 8:(cb + 1) * 8],
                          start=(cb == 0), stop=(cb == 7))
            for cb in range(8):
                ins = tn.matmul(pT[0:1, 8:16], bet_sb[:, cb:cb + 1],
                                gate_sb[:, cb * 8:(cb + 1) * 8],
                                start=(cb == 0), stop=(cb == 7))
            ins.then_inc(sems["GT0"], 1)
            tn.wait_ge(sems["GS0"], 1)
            tn.matmul(pG[:, 0:8], onesr_f[0:1, :], cb_row[0:1, 0:8],
                      start=True, stop=True).then_inc(sems["GT0"], 1)
            tn.wait_ge(sems["GBC"], 1)
            tn.matmul(pG[:, 0:8], onesr_f[0:1, :], cb_row[0:1, 8:16],
                      start=True, stop=True).then_inc(sems["GT0"], 1)
            tn.wait_ge(sems["G2V"], 1)
            tn.wait_ge(sems["GBC"], 2)
            for i in range(NT):
                if i > 0:
                    tn.wait_ge(sems["GE"], i)
                tn.wait_ge(sems["TE"], i + 1)
                for cb in range(8):
                    ins = tn.matmul(pG[:, 0:8],
                                    x2T[:, cb * T_OWN + i * 128: cb * T_OWN + (i + 1) * 128],
                                    g2[:, cb * 8:(cb + 1) * 8],
                                    start=(cb == 0), stop=(cb == 7))
                ins.then_inc(sems["GT"], 1)

        @blk.scalar
        def _(s):
            for i in range(NT):
                s.wait_ge(sems["TT"], i + 1)
                s.activation(
                    _ap(x2T, i * 128, [[8 * T_OWN, 128], [T_OWN, 8], [1, 128]]),
                    _ap(ptrB, 0, [[1024, 128], [128, 8], [1, 128]]),
                    ACT.Copy).then_inc(sems["TE"], 1)
            s.wait_ge(sems["GT0"], 1)
            s.copy(cb_row[0:1, 0:16], pT[0:1, 0:16]).then_inc(sems["GS0"], 1)
            for i in range(NT):
                gc = i * 16
                s.wait_ge(sems["GA"], i + 1)
                s.drain()
                s.activation(sqdump[:, :], stream[:, i * D:(i + 1) * D],
                             ACT.Square, bias=gsc[:, gc + 0:gc + 1],
                             accum_out=gsc[:, gc + 1:gc + 2]).then_inc(sems["GB"], 1)
                s.wait_ge(sems["GC_"], i + 1)
                s.activation(gsc[:, gc + 3:gc + 4], gsc[:, gc + 2:gc + 3],
                             ACT.Sqrt).then_inc(sems["GD"], 1)
                s.wait_ge(sems["GE"], i + 1)
                s.activation(gv8[:, i * 24 + 16:i * 24 + 24],
                             gv8[:, i * 24 + 8:i * 24 + 16], ACT.Exp,
                             bias=gsc[:, gc + 7:gc + 8],
                             accum_out=gsc[:, gc + 8:gc + 9]).then_inc(sems["GF"], 1)

        @blk.vector
        def _(v):
            v.wait_ge(sems["LD"], 48)
            for cb in range(8):
                ins = v.tensor_scalar(g2[:, cb * 8:(cb + 1) * 8], gate_sb[:, cb * 8:(cb + 1) * 8],
                                      gam_sb[:, cb:cb + 1], None, ALU.mult)
            ins.then_inc(sems["G2V"], 1)
            v.wait_ge(sems["GT0"], 2)
            v.tensor_copy(ceb[:, :], pG[:, 0:8]).then_inc(sems["GBC"], 1)
            v.wait_ge(sems["GT0"], 3)
            v.tensor_copy(beb[:, :], pG[:, 0:8]).then_inc(sems["GBC"], 1)
            for i in range(NT):
                gc, gvv = i * 16, i * 24
                v.reduce_sum(gsc[:, gc + 15:gc + 16], stream[:, i * D:(i + 1) * D], AX.X)
                v.drain()
                v.tensor_scalar(gsc[:, gc + 0:gc + 1], gsc[:, gc + 15:gc + 16],
                                -1.0 / D, None, ALU.mult).then_inc(sems["GA"], 1)
                v.wait_ge(sems["GB"], i + 1)
                v.tensor_scalar(gsc[:, gc + 2:gc + 3], gsc[:, gc + 1:gc + 2],
                                1.0 / D, EPS, ALU.mult, ALU.add).then_inc(sems["GC_"], 1)
                v.wait_ge(sems["GD"], i + 1)
                v.reciprocal(gsc[:, gc + 4:gc + 5], gsc[:, gc + 3:gc + 4])
                v.drain()
                v.tensor_mul(gsc[:, gc + 5:gc + 6], gsc[:, gc + 0:gc + 1],
                             gsc[:, gc + 4:gc + 5])
                v.drain()
                v.tensor_scalar(gv8[:, gvv:gvv + 8], ceb[:, :],
                                gsc[:, gc + 5:gc + 6], None, ALU.mult)
                v.drain()
                v.tensor_add(gv8[:, gvv:gvv + 8], gv8[:, gvv:gvv + 8], beb[:, :])
                v.drain()
                v.wait_ge(sems["GT"], i + 1)
                v.scalar_tensor_tensor(gv8[:, gvv + 8:gvv + 16], pG[:, 0:8],
                                       gsc[:, gc + 4:gc + 5], gv8[:, gvv:gvv + 8],
                                       ALU.mult, ALU.add)
                v.drain()
                v.reduce_max(gsc[:, gc + 6:gc + 7], gv8[:, gvv + 8:gvv + 16], AX.X)
                v.drain()
                v.tensor_scalar(gsc[:, gc + 7:gc + 8], gsc[:, gc + 6:gc + 7],
                                -1.0, None, ALU.mult).then_inc(sems["GE"], 1)
                v.wait_ge(sems["GF"], i + 1)
                v.reciprocal(gsc[:, gc + 9:gc + 10], gsc[:, gc + 8:gc + 9])
                v.drain()
                v.tensor_scalar(gv8[:, gvv + 16:gvv + 24], gv8[:, gvv + 16:gvv + 24],
                                gsc[:, gc + 9:gc + 10], None, ALU.mult)
                v.drain()
                v.reduce_max(gsc[:, gc + 10:gc + 11], gv8[:, gvv + 16:gvv + 24], AX.X)
                v.drain()
                v.tensor_scalar(gv8[:, gvv:gvv + 8], gv8[:, gvv + 16:gvv + 24],
                                gsc[:, gc + 10:gc + 11], None, ALU.is_lt)
                v.drain()
                v.tensor_mul(gv8[:, gvv:gvv + 8], gv8[:, gvv:gvv + 8],
                             gv8[:, gvv + 16:gvv + 24])
                v.drain()
                v.reduce_max(gsc[:, gc + 11:gc + 12], gv8[:, gvv:gvv + 8], AX.X)
                v.drain()
                v.tensor_scalar(gv8[:, gvv:gvv + 8], gv8[:, gvv + 16:gvv + 24],
                                gsc[:, gc + 11:gc + 12], None, ALU.is_ge)
                v.drain()
                v.tensor_mul(comb[:, i * 8:(i + 1) * 8], gv8[:, gvv:gvv + 8],
                             gv8[:, gvv + 16:gvv + 24]).then_inc(sems["CMB"], 1)

    psB_init.close()
    pH = es.enter_context(nc.psum_tensor("pH", [128, 2048], F32))
    pY = es.enter_context(nc.psum_tensor("pY", [128, 2048], F32))
    C = dict(LD=48, W1P0=0, W1P1=0, W2L=0, HMM=0, HEV=0, HPC=0, YMM=0,
             YEV=0, BLNV=0, BSQS=0, BVAE=0, BSQT=0, BEPV=0)

    GB_T = min(1024, T_OWN)
    NGRP = T_OWN // GB_T
    NPC = GB_T // 512
    NPTL = GB_T // 128
    NHG = 16 * NPC
    NYG = NPTL * 2

    C = dict(W1P0=0, W1P1=0, W2L=0, HMM=0, HEV=0, HPC=0, YMM=0,
             YEV=0, BLNV=0, BSQS=0, BVAE=0, BSQT=0, BEPV=0)

    for grp in range(NGRP):
        base = dict(C)
        with nc.Block() as blk:

            @blk.gpsimd
            def _(gp):
                for e in range(8):
                    if e > 0:
                        gp.wait_ge(sems["YMM"], base["YMM"] + NPTL * e)
                    gp.dma_start(
                        out=w2sb[:, :].rearrange("p (fb c) -> p fb c", fb=16),
                        in_=w2_io[e].rearrange("(fb p) c -> p fb c", p=128)
                    ).then_inc(sems["W2L"], 16)
                    for k in range(4):
                        gidx = 4 * e + k
                        if gidx >= 2:
                            pe, pk = (gidx - 2) // 4, (gidx - 2) % 4
                            gp.wait_ge(sems["HMM"],
                                       base["HMM"] + 16 * pe + 4 * (pk + 1))
                        buf = gidx % 2
                        gp.dma_start(
                            out=w1p[:, buf * 4096:(buf + 1) * 4096].rearrange(
                                "p (cb f) -> p cb f", cb=8),
                            in_=w1_io[e, :, 512 * k:512 * (k + 1)].rearrange(
                                "(cb p) f -> p cb f", p=128)
                        ).then_inc(sems["W1P0" if buf == 0 else "W1P1"], 16)

            @blk.tensor
            def _(tn):
                for e in range(8):
                    tn.wait_ge(sems["W2L"], base["W2L"] + 16 * (e + 1))
                    for k in range(4):
                        gidx = 4 * e + k
                        buf = gidx % 2
                        tn.wait_ge(sems["W1P0" if buf == 0 else "W1P1"],
                                   base["W1P0" if buf == 0 else "W1P1"] + 16 * (gidx // 2 + 1))
                        for ft2 in range(4):
                            ft = 4 * k + ft2
                            if 16 * e + ft >= 2:
                                tn.wait_ge(sems["HEV"], base["HEV"] + 16 * e + ft - 1)
                            ph = pH[:, (ft % 2) * 1024:(ft % 2 + 1) * 1024]
                            for cb in range(8):
                                for pc in range(NPC):
                                    ins = tn.matmul(
                                        ph[:, 512 * pc:512 * (pc + 1)],
                                        w1p[:, buf * 4096 + cb * 512 + 128 * ft2:
                                            buf * 4096 + cb * 512 + 128 * (ft2 + 1)],
                                        x2T[:, cb * T_OWN + GB_T * grp + 512 * pc:
                                            cb * T_OWN + GB_T * grp + 512 * (pc + 1)],
                                        start=(cb == 0), stop=(cb == 7),
                                        skip_group_check=True)
                            ins.then_inc(sems["HMM"], 1)
                    tn.wait_ge(sems["HEV"], base["HEV"] + 16 * (e + 1))
                    for ptl in range(NPTL):
                        gptl = NPTL * e + ptl
                        if gptl >= 2:
                            tn.wait_ge(sems["YEV"], base["YEV"] + 2 * (gptl - 1))
                        po = (gptl % 2) * 1024
                        for fk in range(16):
                            for ch in range(2):
                                ins = tn.matmul(
                                    pY[:, po + 512 * ch:po + 512 * (ch + 1)],
                                    hT[:, GB_T * fk + 128 * ptl: GB_T * fk + 128 * (ptl + 1)],
                                    w2sb[:, 1024 * fk + 512 * ch: 1024 * fk + 512 * (ch + 1)],
                                    start=(fk == 0), stop=(fk == 15),
                                    skip_group_check=True)
                        ins.then_inc(sems["YMM"], 1)

            @blk.scalar
            def _(s):
                for e in range(8):
                    if e > 0:
                        s.wait_ge(sems["YMM"], base["YMM"] + NPTL * e)
                    for ft in range(16):
                        s.wait_ge(sems["HMM"], base["HMM"] + 16 * e + ft + 1)
                        ph = pH[:, (ft % 2) * 1024:(ft % 2) * 1024 + GB_T]
                        s.activation(hT[:, GB_T * ft: GB_T * (ft + 1)],
                                     ph, ACT.Gelu).then_inc(sems["HEV"], 1)
                for i in range(NPTL):
                    s.wait_ge(sems["BLNV"], base["BLNV"] + i + 1)
                    s.drain()
                    eq = (i % 2) * 2 * D
                    s.activation(epiB[:, eq + D:eq + 2 * D], epiB[:, eq:eq + D],
                                 ACT.Square, bias=lnstB[:, 4 * i:4 * i + 1],
                                 accum_out=lnstB[:, 4 * i + 1:4 * i + 2]).then_inc(sems["BSQS"], 1)
                    s.wait_ge(sems["BVAE"], base["BVAE"] + i + 1)
                    s.activation(lnstB[:, 4 * i + 3:4 * i + 4],
                                 lnstB[:, 4 * i + 2:4 * i + 3],
                                 ACT.Sqrt).then_inc(sems["BSQT"], 1)

            @blk.vector
            def _(v):
                for e in range(8):
                    for ptl in range(NPTL):
                        gptl = NPTL * e + ptl
                        v.wait_ge(sems["YMM"], base["YMM"] + gptl + 1)
                        gi = NPTL * grp + ptl
                        cc = comb[:, gi * 8 + e: gi * 8 + e + 1]
                        po = (gptl % 2) * 1024
                        for ch in range(2):
                            py = pY[:, po + 512 * ch:po + 512 * (ch + 1)]
                            a = acc[:, 1024 * ptl + 512 * ch: 1024 * ptl + 512 * (ch + 1)]
                            if e == 0:
                                ins = v.tensor_scalar(a, py, cc, None, ALU.mult)
                            else:
                                ins = v.scalar_tensor_tensor(a, py, cc, a, ALU.mult, ALU.add)
                            ins.then_inc(sems["YEV"], 1)
                for i in range(NPTL):
                    gi = NPTL * grp + i
                    ep = (i % 2) * 2 * D
                    if i >= 2:
                        v.wait_ge(sems["BSQS"], base["BSQS"] + i - 1)
                    v.tensor_add(epiB[:, ep:ep + D], stream[:, gi * D:(gi + 1) * D],
                                 acc[:, i * D:(i + 1) * D])
                    v.drain()
                    v.reduce_sum(lnstB[:, 4 * i + 1:4 * i + 2], epiB[:, ep:ep + D], AX.X)
                    v.drain()
                    v.tensor_scalar(lnstB[:, 4 * i:4 * i + 1], lnstB[:, 4 * i + 1:4 * i + 2],
                                    -1.0 / D, None, ALU.mult).then_inc(sems["BLNV"], 1)
                    v.wait_ge(sems["BSQS"], base["BSQS"] + i + 1)
                    v.tensor_scalar(lnstB[:, 4 * i + 2:4 * i + 3],
                                    lnstB[:, 4 * i + 1:4 * i + 2],
                                    1.0 / D, EPS, ALU.mult, ALU.add).then_inc(sems["BVAE"], 1)
                    v.wait_ge(sems["BSQT"], base["BSQT"] + i + 1)
                    v.reciprocal(lnstB[:, 4 * i + 1:4 * i + 2], lnstB[:, 4 * i + 3:4 * i + 4])
                    v.drain()
                    v.tensor_mul(lnstB[:, 4 * i + 2:4 * i + 3], lnstB[:, 4 * i:4 * i + 1],
                                 lnstB[:, 4 * i + 1:4 * i + 2])
                    v.drain()
                    v.tensor_scalar(stream[:, gi * D:(gi + 1) * D], epiB[:, ep:ep + D],
                                    lnstB[:, 4 * i + 1:4 * i + 2],
                                    lnstB[:, 4 * i + 2:4 * i + 3],
                                    ALU.mult, ALU.add).then_inc(sems["BEPV"], 1)

        C["W2L"] += 16 * 8
        C["W1P0"] += 16 * 16
        C["W1P1"] += 16 * 16
        C["HMM"] += 16 * 8
        C["HEV"] += 16 * 8
        C["YMM"] += NPTL * 8
        C["YEV"] += NYG * 8
        C["BLNV"] += NPTL
        C["BSQS"] += NPTL
        C["BVAE"] += NPTL
        C["BSQT"] += NPTL
        C["BEPV"] += NPTL
    _clear_sems(nc, sems)



SCALE_C = 0.03125
SEC = 2048 * 1024
CCIN_N = 4 * SEC


def emit_phase_c(nc, es, g):
    u_io, v_io, u_nat, v_nat = g["u_io"], g["v_io"], g["u_nat"], g["v_nat"]
    stream, ident_b = g["stream"], g["ident_b"]
    cc_in, cc_out = g["cc_in"], g["cc_out"]
    S_ALL = 4 * T_OWN

    cast_jobs = None
    if g.get("W1E") is not None:
        W1E, W2E, W1B, W2B = g["W1E"], g["W2E"], g["W1B"], g["W2B"]
        CWI, CWO = g["CWI"], g["CWO"]
        cast_jobs = []
        for r in range(8):
            for c in range(8):
                cast_jobs.append((W1E, W1B, 128 * r, 512 * c))
        for r in range(32):
            for c in range(2):
                cast_jobs.append((W2E, W2B, 128 * r, 512 * c))

    def cast_batch(h):
        if cast_jobs is None:
            return []
        lo = 0 if h == 0 else 4 + 18 * (h - 1)
        hi = 4 if h == 0 else min(len(cast_jobs), 4 + 18 * h)
        return range(lo, hi)

    sems = {}
    for n in ["LD", "TT", "TE", "PJM", "PJE", "PJD0", "PJD1", "PJD2", "PJD3", "CCS", "CCD",
              "PLD0", "PLD1", "PLDA0", "PLDA1", "QLD0", "QLD1", "SVC", "FMM", "THE", "YMM", "YEV", "YDM0", "YDM1",
              "GLD", "GMM", "GEV", "CLNV", "CSQS", "CVAE", "CSQT", "CEPV"]:
        sems[n] = es.enter_context(nc.semaphore("c_" + n))

    with ExitStack() as es1:
        x3T = es1.enter_context(nc.sbuf_tensor("x3T", [128, 8 * T_OWN], BF16))
        uio_sb = es1.enter_context(nc.sbuf_tensor("uio_sb", [128, 8 * D], BF16))
        vio_sb = es1.enter_context(nc.sbuf_tensor("vio_sb", [128, 8 * D], BF16))
        qev = es1.enter_context(nc.sbuf_tensor("qev", [128, 4 * 512], BF16))
        ptrC = es1.enter_context(nc.psum_tensor("ptrC", [128, 1024], BF16))
        pPJ = es1.enter_context(nc.psum_tensor("pPJ", [128, 1024], F32))

        with nc.Block() as blk:
            @blk.gpsimd
            def _(gp):
                gp.dma_start(out=uio_sb[:, :].rearrange("p (b e) -> p b e", b=8),
                             in_=u_io.rearrange("(b p) e -> p b e", p=128)).then_inc(sems["LD"], 16)
                gp.dma_start(out=vio_sb[:, :].rearrange("p (b e) -> p b e", b=8),
                             in_=v_io.rearrange("(b p) e -> p b e", p=128)).then_inc(sems["LD"], 16)
                npj_feat = 2 * 8 * (T_OWN // 512)
                for q4 in range(4):
                    gp.wait_ge(sems["PJD%d" % q4], 16 * (npj_feat // 4))
                gp.collective_compute(
                    "AllGather", ALU.bypass,
                    replica_groups=[[0, 1, 2, 3], [4, 5, 6, 7]],
                    ins=[cc_in[0:2 * SEC]], outs=[cc_out[0:8 * SEC]],
                ).then_inc(sems["CCS"], 1)

            @blk.tensor
            def _(tn):
                for i in range(NT):
                    tn.wait_ge(sems["TE"], i)
                    for jb in range(8):
                        ins = tn.transpose(ptrC[:, jb * 128:(jb + 1) * 128],
                                           stream[:, i * D + jb * 128:i * D + (jb + 1) * 128],
                                           ident_b[:, :])
                    ins.then_inc(sems["TT"], 1)
                tn.wait_ge(sems["LD"], 32)
                tn.wait_ge(sems["TE"], NT)
                gidx = 0
                for w_sb in (uio_sb, vio_sb):
                    for ct in range(8):
                        for pc in range(T_OWN // 512):
                            if gidx >= 2:
                                tn.wait_ge(sems["PJE"], gidx - 1)
                            ps = pPJ[:, (gidx % 2) * 512:(gidx % 2 + 1) * 512]
                            for jb in range(8):
                                ins = tn.matmul(
                                    ps,
                                    w_sb[:, jb * D + 128 * ct: jb * D + 128 * (ct + 1)],
                                    x3T[:, jb * T_OWN + 512 * pc: jb * T_OWN + 512 * (pc + 1)],
                                    start=(jb == 0), stop=(jb == 7))
                            ins.then_inc(sems["PJM"], 1)
                            gidx += 1
                for w_sb in (uio_sb, vio_sb):
                    for i in range(NT):
                        for crh in range(2):
                            tn.wait_ge(sems["PJE"], gidx - 1)
                            ps = pPJ[:, (gidx % 2) * 512:(gidx % 2 + 1) * 512]
                            for jb in range(8):
                                ins = tn.matmul(
                                    ps,
                                    x3T[:, jb * T_OWN + 128 * i: jb * T_OWN + 128 * (i + 1)],
                                    w_sb[:, jb * D + 512 * crh: jb * D + 512 * (crh + 1)],
                                    start=(jb == 0), stop=(jb == 7))
                            ins.then_inc(sems["PJM"], 1)
                            gidx += 1

            @blk.scalar
            def _(s):
                for i in range(NT):
                    s.wait_ge(sems["TT"], i + 1)
                    s.activation(
                        _ap(x3T, i * 128, [[8 * T_OWN, 128], [T_OWN, 8], [1, 128]]),
                        _ap(ptrC, 0, [[1024, 128], [128, 8], [1, 128]]),
                        ACT.Copy).then_inc(sems["TE"], 1)
                NPJ = 2 * 8 * (T_OWN // 512) + 2 * NT * 2
                for gidx in range(NPJ):
                    s.wait_ge(sems["PJM"], gidx + 1)
                    if gidx >= 4:
                        s.wait_ge(sems["PJD%d" % (gidx % 4)], 16 * (gidx // 4))
                    ps = pPJ[:, (gidx % 2) * 512:(gidx % 2 + 1) * 512]
                    s.activation(qev[:, (gidx % 4) * 512:(gidx % 4 + 1) * 512],
                                 ps, ACT.Copy, scale=SCALE_C).then_inc(sems["PJE"], 1)

            @blk.sync
            def _(sy):
                NPC4 = T_OWN // 512
                gidx = 0
                for sec in range(2):
                    for ct in range(8):
                        for pc in range(NPC4):
                            sy.wait_ge(sems["PJE"], gidx + 1)
                            off = sec * SEC + (128 * ct) * T_OWN + 512 * pc
                            sy.dma_start(
                                out=_ap(cc_in, off, [[T_OWN, 128], [1, 512]]),
                                in_=qev[:, (gidx % 4) * 512:(gidx % 4 + 1) * 512]
                            ).then_inc(sems["PJD%d" % (gidx % 4)], 16)
                            gidx += 1
                for sec in range(2, 4):
                    for i in range(NT):
                        for crh in range(2):
                            sy.wait_ge(sems["PJE"], gidx + 1)
                            off = sec * SEC + (128 * i) * D + 512 * crh
                            sy.dma_start(
                                out=_ap(cc_in, off, [[D, 128], [1, 512]]),
                                in_=qev[:, (gidx % 4) * 512:(gidx % 4 + 1) * 512]
                            ).then_inc(sems["PJD%d" % (gidx % 4)], 16)
                            gidx += 1
                for q4 in range(4):
                    sy.wait_ge(sems["PJD%d" % q4], 16 * (gidx // 4))

    y1d = g["y1d"]
    y2d = g["y2d"]
    NSK = S_ALL // 128
    NTC = T_OWN // 512
    NG = NSK * NTC
    with ExitStack() as es3:
        u2t = es3.enter_context(nc.sbuf_tensor("u2t", [128, 2 * S_ALL], BF16))
        v2t = es3.enter_context(nc.sbuf_tensor("v2t", [128, 2 * S_ALL], BF16))
        utk = es3.enter_context(nc.sbuf_tensor("utk", [128, 2 * S_ALL], BF16))
        vtk = es3.enter_context(nc.sbuf_tensor("vtk", [128, 2 * S_ALL], BF16))
        qown = es3.enter_context(nc.sbuf_tensor("qown", [128, 2 * 2 * T_OWN], BF16))
        thsb = es3.enter_context(nc.sbuf_tensor("thsb", [128, 3072], BF16))
        yev = es3.enter_context(nc.sbuf_tensor("yev", [128, 2 * 2 * T_OWN], BF16))
        svc = es3.enter_context(nc.sbuf_tensor("svc", [128, 8], F32))
        if cast_jobs is not None:
            scrW = es3.enter_context(nc.sbuf_tensor("scrW", [128, 4 * 512], BF16))
        pF = es3.enter_context(nc.psum_tensor("pF", [128, 3072], F32))
        pY12 = es3.enter_context(nc.psum_tensor("pY12", [128, 1024], F32))

        with nc.Block() as blk:
            @blk.gpsimd
            def _(gp):
                npj_all = 2 * 8 * (T_OWN // 512) + 4 * NT
                gp.wait_ge(sems["CCS"], 1)
                for q4 in range(4):
                    gp.wait_ge(sems["PJD%d" % q4], 16 * (npj_all // 4))
                gp.collective_compute(
                    "AllGather", ALU.bypass,
                    replica_groups=[[0, 1, 2, 3], [4, 5, 6, 7]],
                    ins=[cc_in[2 * SEC:4 * SEC]], outs=[cc_out[8 * SEC:16 * SEC]],
                ).then_inc(sems["CCS"], 1)
                for hp in range(8):
                    o = (hp % 2) * S_ALL
                    if hp == 0:
                        gp.wait_ge(sems["CCS"], 1)
                    if hp >= 2:
                        gp.wait_ge(sems["FMM"], 2 * NG * (hp - 1) + 1)
                    gp.dma_start(out=_ap(u2t, o, [[2 * S_ALL, 128], [T_OWN, 4], [1, T_OWN]]),
                                 in_=_ap(cc_out, 128 * hp * T_OWN,
                                         [[T_OWN, 128], [2 * SEC, 4], [1, T_OWN]])
                                 ).then_inc(sems["PLDA%d" % (hp % 2)], 16)
                    gp.dma_start(out=_ap(v2t, o, [[2 * S_ALL, 128], [T_OWN, 4], [1, T_OWN]]),
                                 in_=_ap(cc_out, SEC + 128 * hp * T_OWN,
                                         [[T_OWN, 128], [2 * SEC, 4], [1, T_OWN]])
                                 ).then_inc(sems["PLDA%d" % (hp % 2)], 16)
                    if hp == 0:
                        gp.wait_ge(sems["CCS"], 2)
                    for r in range(4):
                        gp.dma_start(out=_ap(utk, o + r * 128 * NT, [[2 * S_ALL, 128], [128, NT], [1, 128]]),
                                     in_=_ap(cc_out, 8 * SEC + 128 * hp + r * 2 * SEC,
                                             [[D, 128], [128 * D, NT], [1, 128]])
                                     ).then_inc(sems["PLD%d" % (hp % 2)], 16)
                        gp.dma_start(out=_ap(vtk, o + r * 128 * NT, [[2 * S_ALL, 128], [128, NT], [1, 128]]),
                                     in_=_ap(cc_out, 8 * SEC + SEC + 128 * hp + r * 2 * SEC,
                                             [[D, 128], [128 * D, NT], [1, 128]])
                                     ).then_inc(sems["PLD%d" % (hp % 2)], 16)
                    for ch in cast_batch(hp):
                        src, dst, r0, c0 = cast_jobs[ch]
                        q = ch % 4
                        if ch >= 4:
                            gp.wait_ge(CWO[q], 16 * (ch // 4))
                        gp.dma_start(out=scrW[:, q * 512:(q + 1) * 512],
                                     in_=src[r0:r0 + 128, c0:c0 + 512]
                                     ).then_inc(CWI[q], 16)

            @blk.sync
            def _(sy):
                for hp in range(8):
                    oq = (hp % 2) * 2 * T_OWN
                    if hp >= 2:
                        sy.wait_ge(sems["FMM"], 2 * NG * (hp - 1))
                    sy.dma_start(out=_ap(qown, oq, [[4 * T_OWN, 128], [1, T_OWN]]),
                                 in_=_ap(cc_in, 128 * hp * T_OWN, [[T_OWN, 128], [1, T_OWN]])
                                 ).then_inc(sems["QLD%d" % (hp % 2)], 16)
                    sy.dma_start(out=_ap(qown, oq + T_OWN, [[4 * T_OWN, 128], [1, T_OWN]]),
                                 in_=_ap(cc_in, SEC + 128 * hp * T_OWN, [[T_OWN, 128], [1, T_OWN]])
                                 ).then_inc(sems["QLD%d" % (hp % 2)], 16)
                    for ch in cast_batch(hp):
                        src, dst, r0, c0 = cast_jobs[ch]
                        q = ch % 4
                        sy.wait_ge(CWI[q], 16 * (ch // 4 + 1))
                        sy.dma_start(out=dst[r0:r0 + 128, c0:c0 + 512],
                                     in_=scrW[:, q * 512:(q + 1) * 512]
                                     ).then_inc(CWO[q], 16)
                    ob = (hp % 2) * 2 * T_OWN
                    sy.wait_ge(sems["YEV"], 2 * NTC * hp + NTC)
                    sy.dma_start(out=_ap(y2d, 128 * hp * T_OWN, [[T_OWN, 128], [1, T_OWN]]),
                                 in_=yev[:, ob:ob + T_OWN]).then_inc(sems["YDM%d" % (hp % 2)], 16)
                    sy.wait_ge(sems["YEV"], 2 * NTC * hp + 2 * NTC)
                    sy.dma_start(out=_ap(y1d, 128 * hp * T_OWN, [[T_OWN, 128], [1, T_OWN]]),
                                 in_=yev[:, ob + T_OWN:ob + 2 * T_OWN]).then_inc(sems["YDM%d" % (hp % 2)], 16)
                sy.wait_ge(sems["YDM0"], 16 * 8)
                sy.wait_ge(sems["YDM1"], 16 * 8)
                if cast_jobs is not None:
                    for q in range(4):
                        sy.wait_ge(CWO[q], 16 * ((len(cast_jobs) + 3 - q) // 4))

            @blk.vector
            def _(v):
                for hp in range(8):
                    o = (hp % 2) * S_ALL
                    ob = (hp % 2) * 2 * T_OWN
                    sv = (hp % 2) * 4
                    v.wait_ge(sems["PLDA%d" % (hp % 2)], 32 * (hp // 2 + 1))
                    v.wait_ge(sems["PLD%d" % (hp % 2)], 128 * (hp // 2 + 1))
                    if hp >= 2:
                        v.wait_ge(sems["YMM"], 2 * (hp - 1))
                    v.reduce_sum(svc[:, sv:sv + 1], v2t[:, o:o + S_ALL], AX.X)
                    v.drain()
                    v.tensor_scalar(svc[:, sv + 1:sv + 2], svc[:, sv:sv + 1], 0.5, None, ALU.mult)
                    v.reduce_sum(svc[:, sv + 2:sv + 3], u2t[:, o:o + S_ALL], AX.X)
                    v.drain()
                    v.tensor_scalar(svc[:, sv + 3:sv + 4], svc[:, sv + 2:sv + 3],
                                    0.5, None, ALU.mult).then_inc(sems["SVC"], 1)
                    v.drain()
                    if hp >= 2:
                        v.wait_ge(sems["YDM%d" % (hp % 2)], 32 * (hp // 2))
                    for sl in range(2):
                        bias = svc[:, sv + 3:sv + 4] if sl == 0 else svc[:, sv + 1:sv + 2]
                        od = ob if sl == 0 else ob + T_OWN
                        for tc in range(NTC):
                            e = (2 * hp + sl) * NTC + tc
                            v.wait_ge(sems["YMM"], 2 * NTC * hp + sl * NTC + tc + 1)
                            v.tensor_scalar(yev[:, od + 512 * tc:od + 512 * (tc + 1)],
                                            pY12[:, (e % 2) * 512:(e % 2) * 512 + 512],
                                            0.5, bias, ALU.mult,
                                            ALU.add).then_inc(sems["YEV"], 1)

            @blk.tensor
            def _(tn):
                for hp in range(8):
                    o = (hp % 2) * S_ALL
                    oq = (hp % 2) * 2 * T_OWN
                    tn.wait_ge(sems["QLD%d" % (hp % 2)], 32 * (hp // 2 + 1))
                    GB = 2 * NG * hp

                    def gparts(gg):
                        gl = gg - GB
                        slab, rem = divmod(gl, NG)
                        tc, sk = divmod(rem, NSK)
                        return gl, slab, tc, sk

                    def emit_y(gg):
                        gl, slab, tc, sk = gparts(gg)
                        tok = utk if slab == 0 else vtk
                        ps = (gg % 3) * 1024
                        e = 2 * hp * NTC + slab * NTC + tc
                        if sk == 0:
                            if gl == 0:
                                tn.wait_ge(sems["PLD%d" % (hp % 2)], 128 * (hp // 2 + 1))
                            if e > 0:
                                tn.wait_ge(sems["YEV"], e)
                        tn.wait_ge(sems["THE"], gg + 1)
                        pyb = pY12[:, (e % 2) * 512:(e % 2) * 512 + 512]
                        tn.matmul(pyb[0:64, :],
                                  tok[:, o + 128 * sk:o + 128 * sk + 64],
                                  thsb[:, ps:ps + 512],
                                  start=(sk == 0), stop=(sk == NSK - 1),
                                  tile_position=(0, 0), skip_group_check=True)
                        ins = tn.matmul(pyb[64:128, :],
                                        tok[:, o + 128 * sk + 64:o + 128 * sk + 128],
                                        thsb[:, ps + 512:ps + 1024],
                                        start=(sk == 0), stop=(sk == NSK - 1),
                                        tile_position=(0, 64), skip_group_check=True)
                        if sk == NSK - 1:
                            ins.then_inc(sems["YMM"], 1)

                    for gl in range(2 * NG):
                        gg = GB + gl
                        _, slab, tc, sk = gparts(gg)
                        at = u2t if slab == 0 else v2t
                        boff = oq + (T_OWN if slab == 0 else 0)
                        ps = (gg % 3) * 1024
                        if gl == 0:
                            tn.wait_ge(sems["PLDA%d" % (hp % 2)], 32 * (hp // 2 + 1))
                        if gg >= 3:
                            tn.wait_ge(sems["THE"], gg - 2)
                        tn.matmul(pF[:, ps:ps + 512],
                                  at[0:64, o + 128 * sk:o + 128 * (sk + 1)],
                                  qown[0:64, boff + 512 * tc:boff + 512 * (tc + 1)],
                                  start=True, stop=True, tile_position=(0, 0))
                        tn.matmul(pF[:, ps + 512:ps + 1024],
                                  at[64:128, o + 128 * sk:o + 128 * (sk + 1)],
                                  qown[64:128, boff + 512 * tc:boff + 512 * (tc + 1)],
                                  start=True, stop=True,
                                  tile_position=(64, 0)).then_inc(sems["FMM"], 1)
                        if gl >= 2:
                            emit_y(gg - 2)
                    emit_y(GB + 2 * NG - 2)
                    emit_y(GB + 2 * NG - 1)

            @blk.scalar
            def _(s):
                for hp in range(8):
                    s.wait_ge(sems["SVC"], hp + 1)
                    for gl in range(2 * NG):
                        gg = 2 * NG * hp + gl
                        ps = (gg % 3) * 1024
                        s.wait_ge(sems["FMM"], gg + 1)
                        s.activation(thsb[:, ps:ps + 1024],
                                     pF[:, ps:ps + 1024],
                                     ACT.Tanh, scale=0.5).then_inc(sems["THE"], 1)

    with ExitStack() as es4:
        y1s = es4.enter_context(nc.sbuf_tensor("y1s", [128, 8 * T_OWN], BF16))
        y2s = es4.enter_context(nc.sbuf_tensor("y2s", [128, 8 * T_OWN], BF16))
        un_sb = es4.enter_context(nc.sbuf_tensor("un_sb", [128, 8 * D], BF16))
        vn_sb = es4.enter_context(nc.sbuf_tensor("vn_sb", [128, 8 * D], BF16))
        epiC = es4.enter_context(nc.sbuf_tensor("epiC", [128, 4 * D], F32))
        lnstC = es4.enter_context(nc.sbuf_tensor("lnstC", [128, 64], F32))
        pGC = es4.enter_context(nc.psum_tensor("pGC", [128, 2048], F32))

        with nc.Block() as blk:
            @blk.gpsimd
            def _(gp):
                gp.dma_start(out=un_sb[:, :].rearrange("p (b e) -> p b e", b=8),
                             in_=u_nat.rearrange("(b p) e -> p b e", p=128)).then_inc(sems["GLD"], 16)
                gp.dma_start(out=vn_sb[:, :].rearrange("p (b e) -> p b e", b=8),
                             in_=v_nat.rearrange("(b p) e -> p b e", p=128)).then_inc(sems["GLD"], 16)
                gp.dma_start(out=y1s[:, :].rearrange("p (k t) -> p k t", k=8),
                             in_=_ap(y1d, 0, [[T_OWN, 128], [128 * T_OWN, 8], [1, T_OWN]])
                             ).then_inc(sems["GLD"], 16)
                gp.dma_start(out=y2s[:, :].rearrange("p (k t) -> p k t", k=8),
                             in_=_ap(y2d, 0, [[T_OWN, 128], [128 * T_OWN, 8], [1, T_OWN]])
                             ).then_inc(sems["GLD"], 16)

            @blk.tensor
            def _(tn):
                tn.wait_ge(sems["GLD"], 64)
                for i in range(NT):
                    for eh in range(2):
                        gidx = 2 * i + eh
                        if gidx >= 4:
                            tn.wait_ge(sems["GEV"], gidx - 3)
                        ps = pGC[:, (gidx % 4) * 512:(gidx % 4 + 1) * 512]
                        for k in range(8):
                            tn.matmul(ps, y1s[:, k * T_OWN + 128 * i: k * T_OWN + 128 * (i + 1)],
                                      un_sb[:, k * D + 512 * eh: k * D + 512 * (eh + 1)],
                                      start=(k == 0), stop=False)
                        for k in range(8):
                            ins = tn.matmul(ps, y2s[:, k * T_OWN + 128 * i: k * T_OWN + 128 * (i + 1)],
                                            vn_sb[:, k * D + 512 * eh: k * D + 512 * (eh + 1)],
                                            start=False, stop=(k == 7))
                        ins.then_inc(sems["GMM"], 1)

            @blk.scalar
            def _(s):
                for i in range(NT):
                    ep = (i % 2) * 2 * D
                    for eh in range(2):
                        gidx = 2 * i + eh
                        s.wait_ge(sems["GMM"], gidx + 1)
                        if i >= 2:
                            s.wait_ge(sems["CEPV"], i - 1)
                        ps = pGC[:, (gidx % 4) * 512:(gidx % 4 + 1) * 512]
                        s.activation(epiC[:, ep + 512 * eh:ep + 512 * (eh + 1)], ps,
                                     ACT.Copy, scale=SCALE_C).then_inc(sems["GEV"], 1)
                    if i > 0:
                        q = i - 1
                        eq = (q % 2) * 2 * D
                        s.wait_ge(sems["CLNV"], i)
                        s.drain()
                        s.activation(epiC[:, eq + D:eq + 2 * D], epiC[:, eq:eq + D],
                                     ACT.Square, bias=lnstC[:, 4 * q:4 * q + 1],
                                     accum_out=lnstC[:, 4 * q + 1:4 * q + 2]).then_inc(sems["CSQS"], 1)
                        s.wait_ge(sems["CVAE"], i)
                        s.activation(lnstC[:, 4 * q + 3:4 * q + 4], lnstC[:, 4 * q + 2:4 * q + 3],
                                     ACT.Sqrt).then_inc(sems["CSQT"], 1)
                q = NT - 1
                eq = (q % 2) * 2 * D
                s.wait_ge(sems["CLNV"], NT)
                s.drain()
                s.activation(epiC[:, eq + D:eq + 2 * D], epiC[:, eq:eq + D],
                             ACT.Square, bias=lnstC[:, 4 * q:4 * q + 1],
                             accum_out=lnstC[:, 4 * q + 1:4 * q + 2]).then_inc(sems["CSQS"], 1)
                s.wait_ge(sems["CVAE"], NT)
                s.activation(lnstC[:, 4 * q + 3:4 * q + 4], lnstC[:, 4 * q + 2:4 * q + 3],
                             ACT.Sqrt).then_inc(sems["CSQT"], 1)

            @blk.vector
            def _(v):
                for i in range(NT):
                    ep = (i % 2) * 2 * D
                    v.wait_ge(sems["GEV"], 2 * i + 2)
                    v.tensor_add(epiC[:, ep:ep + D], stream[:, i * D:(i + 1) * D],
                                 epiC[:, ep:ep + D])
                    v.drain()
                    v.reduce_sum(lnstC[:, 4 * i + 1:4 * i + 2], epiC[:, ep:ep + D], AX.X)
                    v.drain()
                    v.tensor_scalar(lnstC[:, 4 * i:4 * i + 1], lnstC[:, 4 * i + 1:4 * i + 2],
                                    -1.0 / D, None, ALU.mult).then_inc(sems["CLNV"], 1)
                    v.wait_ge(sems["CSQS"], i + 1)
                    v.tensor_scalar(lnstC[:, 4 * i + 2:4 * i + 3], lnstC[:, 4 * i + 1:4 * i + 2],
                                    1.0 / D, EPS, ALU.mult, ALU.add).then_inc(sems["CVAE"], 1)
                    v.wait_ge(sems["CSQT"], i + 1)
                    v.reciprocal(lnstC[:, 4 * i + 1:4 * i + 2], lnstC[:, 4 * i + 3:4 * i + 4])
                    v.drain()
                    v.tensor_mul(lnstC[:, 4 * i + 2:4 * i + 3], lnstC[:, 4 * i:4 * i + 1],
                                 lnstC[:, 4 * i + 1:4 * i + 2])
                    v.drain()
                    v.tensor_scalar(stream[:, i * D:(i + 1) * D], epiC[:, ep:ep + D],
                                    lnstC[:, 4 * i + 1:4 * i + 2],
                                    lnstC[:, 4 * i + 2:4 * i + 3],
                                    ALU.mult, ALU.add).then_inc(sems["CEPV"], 1)
    _clear_sems(nc, sems)



def emit_phase_d(nc, es, g):
    W1B, W2B, CWO = g["W1B"], g["W2B"], g["CWO"]
    b1col, b2rep, wrcol = g["b1col"], g["b2rep"], g["wrcol"]
    cc2_in, cc2_out = g["cc2_in"], g["cc2_out"]
    stream, ident_b, onesr_f, out_hbm = g["stream"], g["ident_b"], g["onesr_f"], g["out_hbm"]
    NBIS = 45
    KSEL = 2 * T_OWN

    sems = {}
    for n in ["LD", "TT", "TE", "SMM", "SEV", "SDM", "CCS", "BLD", "BV", "BT",
              "W1P0", "W1P1", "W2L", "HMM", "HEV", "YMM", "YEV", "GTV",
              "DLNV", "DSQS", "DVAE", "DSQT", "DEPV", "FDM0", "FDM1"]:
        sems[n] = es.enter_context(nc.semaphore("d_" + n))

    x4T = es.enter_context(nc.sbuf_tensor("x4T", [128, 8 * T_OWN], BF16))
    scS = es.enter_context(nc.sbuf_tensor("scS", [128, NT], F32))
    wr_sb = es.enter_context(nc.sbuf_tensor("wr_sb", [128, 8], BF16))
    onesc_f = es.enter_context(nc.sbuf_tensor("onesc_f", [128, 1], F32))
    b1_sb = es.enter_context(nc.sbuf_tensor("b1_sb", [128, 32], F32))
    b2_sb = es.enter_context(nc.sbuf_tensor("b2_sb", [128, D], F32))

    with ExitStack() as es1:
        ptrD = es1.enter_context(nc.psum_tensor("ptrD", [128, 1024], BF16))
        pSC = es1.enter_context(nc.psum_tensor("pSC", [128, 1024], F32))
        with nc.Block() as blk:
            @blk.gpsimd
            def _(gp):
                gp.dma_start(out=wr_sb[:, :], in_=wrcol).then_inc(sems["LD"], 16)
                gp.dma_start(out=b1_sb[:, :], in_=b1col).then_inc(sems["LD"], 16)
                gp.dma_start(out=b2_sb[:, :], in_=b2rep).then_inc(sems["LD"], 16)
                gp.wait_ge(sems["SDM"], 16)
                gp.collective_compute(
                    "AllGather", ALU.bypass,
                    replica_groups=[[0, 1, 2, 3], [4, 5, 6, 7]],
                    ins=[cc2_in[:]], outs=[cc2_out[:]],
                ).then_inc(sems["CCS"], 1)

            @blk.tensor
            def _(tn):
                for i in range(NT):
                    tn.wait_ge(sems["TE"], i)
                    for jb in range(8):
                        ins = tn.transpose(ptrD[:, jb * 128:(jb + 1) * 128],
                                           stream[:, i * D + jb * 128:i * D + (jb + 1) * 128],
                                           ident_b[:, :])
                    ins.then_inc(sems["TT"], 1)
                tn.wait_ge(sems["LD"], 48)
                for i in range(NT):
                    tn.wait_ge(sems["TE"], i + 1)
                    if i >= 2:
                        tn.wait_ge(sems["SEV"], i - 1)
                    ps = pSC[:, (i % 2) * 512:(i % 2) * 512 + 1]
                    for jb in range(8):
                        ins = tn.matmul(ps,
                                        x4T[:, jb * T_OWN + 128 * i: jb * T_OWN + 128 * (i + 1)],
                                        wr_sb[:, jb:jb + 1],
                                        start=(jb == 0), stop=(jb == 7))
                    ins.then_inc(sems["SMM"], 1)

            @blk.scalar
            def _(s):
                for i in range(NT):
                    s.wait_ge(sems["TT"], i + 1)
                    s.activation(
                        _ap(x4T, i * 128, [[8 * T_OWN, 128], [T_OWN, 8], [1, 128]]),
                        _ap(ptrD, 0, [[1024, 128], [128, 8], [1, 128]]),
                        ACT.Copy).then_inc(sems["TE"], 1)

            @blk.vector
            def _(v):
                v.memset(onesc_f[:, 0:1], 1.0)
                for i in range(NT):
                    v.wait_ge(sems["SMM"], i + 1)
                    v.tensor_copy(scS[:, i:i + 1], pSC[:, (i % 2) * 512:(i % 2) * 512 + 1]
                                  ).then_inc(sems["SEV"], 1)

            @blk.sync
            def _(sy):
                sy.wait_ge(sems["SEV"], NT)
                sy.dma_start(out=_ap(cc2_in, 0, [[NT, 128], [1, NT]]),
                             in_=scS[:, :]).then_inc(sems["SDM"], 16)
                sy.wait_ge(sems["SDM"], 16)

    GB_T = min(1024, T_OWN)
    NGRP = T_OWN // GB_T
    NPC = GB_T // 512
    NPTL = GB_T // 128
    NHG = 16 * NPC
    NYG = NPTL * 2

    w1p = es.enter_context(nc.sbuf_tensor("dw1p", [128, 2 * 4096], BF16))
    w2sb = es.enter_context(nc.sbuf_tensor("dw2sb", [128, 16 * 1024], BF16))
    hT = es.enter_context(nc.sbuf_tensor("dhT", [128, 16 * GB_T], BF16))
    acc_bf = es.enter_context(nc.sbuf_tensor("daccb", [128, NT * 1024], BF16))
    ytmp = es.enter_context(nc.sbuf_tensor("dytmp", [128, 512], BF16))
    gat = es.enter_context(nc.sbuf_tensor("gat", [128, NT], F32))
    epiD = es.enter_context(nc.sbuf_tensor("epiD", [128, 4 * D], F32))
    lnstD = es.enter_context(nc.sbuf_tensor("lnstD", [128, 64], F32))
    fin = es.enter_context(nc.sbuf_tensor("fin", [128, 2 * D], F32))
    pH = es.enter_context(nc.psum_tensor("dpH", [128, 1024], F32))
    pY = es.enter_context(nc.psum_tensor("dpY", [128, 1024], F32))

    thr_b = es.enter_context(nc.sbuf_tensor("thr_b", [128, 2], F32))
    scA = es.enter_context(nc.sbuf_tensor("scA", [128, 4 * NT], F32))
    cmp64 = es.enter_context(nc.sbuf_tensor("cmp64", [128, 4 * NT], F32))
    pc = es.enter_context(nc.sbuf_tensor("pc", [128, 2], F32))
    bis = es.enter_context(nc.sbuf_tensor("bis", [1, 8], F32))
    pB = es.enter_context(nc.psum_tensor("pB", [128, 1024], F32))

    bis_ops = [(w, j) for j in range(NBIS) for w in ("cnt", "bct")] + [("fin", 0), ("fin", 1)]
    N_Y0A = max(0, NYG - 8)
    N_H1A = NHG
    N_Y1A = NYG

    def bisPE(tn, op):
        w, j = op
        if w == "cnt":
            tn.wait_ge(sems["BV"], 2 * j + 1)
            tn.matmul(pB[0:1, 0:1], pc[:, 0:1], onesc_f[:, 0:1],
                      start=True, stop=True).then_inc(sems["BT"], 1)
        elif w == "bct":
            tn.wait_ge(sems["BV"], 2 * j + 2)
            tn.matmul(pB[:, 512:513], onesr_f[0:1, :], bis[0:1, 2:3],
                      start=True, stop=True).then_inc(sems["BT"], 1)
        elif j == 0:
            tn.wait_ge(sems["BV"], 2 * NBIS + 1)
            tn.matmul(pB[:, 512:513], onesr_f[0:1, :], bis[0:1, 0:1],
                      start=True, stop=True).then_inc(sems["BT"], 1)

    def bisV(v, op):
        w, j = op
        if w == "cnt":
            if j == 0:
                v.wait_ge(sems["BLD"], 16)
            v.wait_ge(sems["BT"], 2 * j)
            if j > 0:
                v.tensor_copy(thr_b[:, 0:1], pB[:, 512:513])
                v.drain()
            v.tensor_scalar(cmp64[:, :], scA[:, :], thr_b[:, 0:1], None, ALU.is_gt)
            v.drain()
            v.reduce_sum(pc[:, 0:1], cmp64[:, :], AX.X).then_inc(sems["BV"], 1)
        elif w == "bct":
            v.wait_ge(sems["BT"], 2 * j + 1)
            v.tensor_scalar(bis[0:1, 3:4], pB[0:1, 0:1], float(KSEL), None, ALU.is_ge)
            v.drain()
            v.tensor_sub(bis[0:1, 5:6], bis[0:1, 2:3], bis[0:1, 0:1])
            v.tensor_sub(bis[0:1, 6:7], bis[0:1, 1:2], bis[0:1, 2:3])
            v.drain()
            v.scalar_tensor_tensor(bis[0:1, 0:1], bis[0:1, 5:6], bis[0:1, 3:4],
                                   bis[0:1, 0:1], ALU.mult, ALU.add)
            v.scalar_tensor_tensor(bis[0:1, 1:2], bis[0:1, 6:7], bis[0:1, 3:4],
                                   bis[0:1, 2:3], ALU.mult, ALU.add)
            v.drain()
            v.tensor_add(bis[0:1, 4:5], bis[0:1, 0:1], bis[0:1, 1:2])
            v.drain()
            v.tensor_scalar(bis[0:1, 2:3], bis[0:1, 4:5], 0.5, None,
                            ALU.mult).then_inc(sems["BV"], 1)
        elif j == 0:
            v.drain()
            v.tensor_copy(bis[0:1, 5:6], bis[0:1, 0:1]).then_inc(sems["BV"], 1)
        else:
            v.wait_ge(sems["BT"], 2 * NBIS + 1)
            v.tensor_copy(thr_b[:, 0:1], pB[:, 512:513]).then_inc(sems["BV"], 1)

    C = dict(W1P0=0, W1P1=0, W2L=0, HMM=0, HEV=0, YMM=0, YEV=0)

    for grp in range(NGRP):
        base = dict(C)
        last = grp == NGRP - 1
        with nc.Block() as blk:
            @blk.gpsimd
            def _(gp):
                if grp == 0:
                    for q in range(4):
                        gp.wait_ge(CWO[q], 16 * 32)
                for fh in range(2):
                    if fh > 0:
                        gp.wait_ge(sems["YMM"], base["YMM"] + NYG * fh)
                    gp.dma_start(
                        out=w2sb[:, :].rearrange("p (fb c) -> p fb c", fb=16),
                        in_=W2B[2048 * fh:2048 * (fh + 1), :].rearrange("(fb p) c -> p fb c", p=128)
                    ).then_inc(sems["W2L"], 16)
                    for k in range(4):
                        gidx = 4 * fh + k
                        if gidx >= 2:
                            pe, pk = (gidx - 2) // 4, (gidx - 2) % 4
                            gp.wait_ge(sems["HMM"],
                                       base["HMM"] + NHG * pe + 4 * NPC * (pk + 1))
                        buf = gidx % 2
                        gp.dma_start(
                            out=w1p[:, buf * 4096:(buf + 1) * 4096].rearrange(
                                "p (cb f) -> p cb f", cb=8),
                            in_=W1B[:, 2048 * fh + 512 * k:2048 * fh + 512 * (k + 1)].rearrange(
                                "(cb p) f -> p cb f", p=128)
                        ).then_inc(sems["W1P0" if buf == 0 else "W1P1"], 16)

            @blk.tensor
            def _(tn):
                bi = [0]

                def banchor(cond):
                    if last and cond and bi[0] < len(bis_ops):
                        bisPE(tn, bis_ops[bi[0]])
                        bi[0] += 1

                for fh in range(2):
                    tn.wait_ge(sems["W2L"], base["W2L"] + 16 * (fh + 1))
                    for k in range(4):
                        gidx = 4 * fh + k
                        buf = gidx % 2
                        tn.wait_ge(sems["W1P0" if buf == 0 else "W1P1"],
                                   base["W1P0" if buf == 0 else "W1P1"] + 16 * (gidx // 2 + 1))
                        for ft2 in range(4):
                            ft = 4 * k + ft2
                            for pcc in range(NPC):
                                hg = NPC * ft + pcc
                                if NHG * fh + hg >= 2:
                                    tn.wait_ge(sems["HEV"], base["HEV"] + NHG * fh + hg - 1)
                                ph = pH[:, (hg % 2) * 512:(hg % 2 + 1) * 512]
                                for cb in range(8):
                                    ins = tn.matmul(
                                        ph,
                                        w1p[:, buf * 4096 + cb * 512 + 128 * ft2:
                                            buf * 4096 + cb * 512 + 128 * (ft2 + 1)],
                                        x4T[:, cb * T_OWN + GB_T * grp + 512 * pcc:
                                            cb * T_OWN + GB_T * grp + 512 * (pcc + 1)],
                                        start=(cb == 0), stop=(cb == 7))
                                ins.then_inc(sems["HMM"], 1)
                                banchor(fh == 1)
                    tn.wait_ge(sems["HEV"], base["HEV"] + NHG * (fh + 1))
                    for yg in range(NYG):
                        ptl, ch = yg // 2, yg % 2
                        if NYG * fh + yg >= 2:
                            tn.wait_ge(sems["YEV"], base["YEV"] + NYG * fh + yg - 1)
                        py = pY[:, (yg % 2) * 512:(yg % 2 + 1) * 512]
                        for fk in range(16):
                            ins = tn.matmul(
                                py,
                                hT[:, GB_T * fk + 128 * ptl: GB_T * fk + 128 * (ptl + 1)],
                                w2sb[:, 1024 * fk + 512 * ch: 1024 * fk + 512 * (ch + 1)],
                                start=(fk == 0), stop=(fk == 15))
                        ins.then_inc(sems["YMM"], 1)
                        banchor(fh == 1 or yg >= 8)
                if last:
                    while bi[0] < len(bis_ops):
                        bisPE(tn, bis_ops[bi[0]])
                        bi[0] += 1
                    tn.wait_ge(sems["BV"], 2 * NBIS + 2)

            @blk.scalar
            def _(s):
                if grp == 0:
                    s.wait_ge(sems["LD"], 48)
                for fh in range(2):
                    if fh > 0:
                        s.wait_ge(sems["YMM"], base["YMM"] + NYG * fh)
                    for hg in range(NHG):
                        ft, pcc = hg // NPC, hg % NPC
                        s.wait_ge(sems["HMM"], base["HMM"] + NHG * fh + hg + 1)
                        ph = pH[:, (hg % 2) * 512:(hg % 2 + 1) * 512]
                        s.activation(hT[:, GB_T * ft + 512 * pcc: GB_T * ft + 512 * (pcc + 1)],
                                     ph, ACT.Gelu_apprx_tanh,
                                     bias=b1_sb[:, 16 * fh + ft:16 * fh + ft + 1]
                                     ).then_inc(sems["HEV"], 1)

            @blk.vector
            def _(v):
                bi = [0]

                def vanchor(cond):
                    if last and cond:
                        n = 2 if bi[0] == 0 else 1
                        for _ in range(n):
                            if bi[0] < len(bis_ops):
                                bisV(v, bis_ops[bi[0]])
                                bi[0] += 1

                if last:
                    v.memset(bis[0:1, 0:1], -10000.0)
                    v.memset(bis[0:1, 1:2], 10000.0)
                    v.memset(bis[0:1, 2:3], 0.0)
                    v.memset(thr_b[:, 0:1], 0.0)
                    v.drain()
                for fh in range(2):
                    if fh == 1:
                        for _ in range(NHG):
                            vanchor(True)
                    for yg in range(NYG):
                        ptl, ch = yg // 2, yg % 2
                        v.wait_ge(sems["YMM"], base["YMM"] + NYG * fh + yg + 1)
                        py = pY[:, (yg % 2) * 512:(yg % 2 + 1) * 512]
                        a = acc_bf[:, 1024 * (NPTL * grp + ptl) + 512 * ch:
                                   1024 * (NPTL * grp + ptl) + 512 * (ch + 1)]
                        if fh == 0:
                            v.tensor_copy(a, py).then_inc(sems["YEV"], 1)
                        else:
                            v.tensor_copy(ytmp[:, 0:512], py)
                            v.drain()
                            v.tensor_add(a, a, ytmp[:, 0:512]
                                         ).then_inc(sems["YEV"], 1)
                            v.drain()
                        vanchor(fh == 1 or yg >= 8)
                if last:
                    while bi[0] < len(bis_ops):
                        bisV(v, bis_ops[bi[0]])
                        bi[0] += 1

            if last:
                @blk.sync
                def _(sy):
                    sy.wait_ge(sems["CCS"], 1)
                    sy.dma_start(out=scA[:, :].rearrange("p (r i) -> p r i", r=4),
                                 in_=_ap(cc2_out, 0, [[NT, 128], [T_OWN, 4], [1, NT]])
                                 ).then_inc(sems["BLD"], 16)

        C["W2L"] += 32
        C["W1P0"] += 16 * 4
        C["W1P1"] += 16 * 4
        C["HMM"] += NHG * 2
        C["HEV"] += NHG * 2
        C["YMM"] += NYG * 2
        C["YEV"] += NYG * 2

    with nc.Block() as blk:
        @blk.scalar
        def _(s):
            s.activation(epiD[:, 0:NT], scS[:, :], ACT.Tanh, scale=0.5).then_inc(sems["GTV"], 1)

        @blk.vector
        def _(v):
            v.tensor_scalar(gat[:, :], scS[:, :], thr_b[:, 0:1], None, ALU.is_gt)
            v.drain()
            v.wait_ge(sems["GTV"], 1)
            v.tensor_scalar(epiD[:, 0:NT], epiD[:, 0:NT], 0.5, 0.5, ALU.mult, ALU.add)
            v.drain()
            v.tensor_mul(gat[:, :], gat[:, :], epiD[:, 0:NT])
    with nc.Block() as blk:
        @blk.scalar
        def _(s):
            for i in range(NT):
                s.wait_ge(sems["DLNV"], i + 1)
                s.drain()
                eq = (i % 2) * 2 * D
                s.activation(epiD[:, eq + D:eq + 2 * D], epiD[:, eq:eq + D],
                             ACT.Square, bias=lnstD[:, 4 * i:4 * i + 1],
                             accum_out=lnstD[:, 4 * i + 1:4 * i + 2]).then_inc(sems["DSQS"], 1)
                s.wait_ge(sems["DVAE"], i + 1)
                s.activation(lnstD[:, 4 * i + 3:4 * i + 4],
                             lnstD[:, 4 * i + 2:4 * i + 3],
                             ACT.Sqrt).then_inc(sems["DSQT"], 1)

        @blk.vector
        def _(v):
            for i in range(NT):
                ep = (i % 2) * 2 * D
                if i >= 2:
                    v.wait_ge(sems["DSQS"], i - 1)
                v.tensor_copy(epiD[:, ep:ep + D], acc_bf[:, i * 1024:(i + 1) * 1024])
                v.drain()
                v.tensor_add(epiD[:, ep:ep + D], epiD[:, ep:ep + D], b2_sb[:, :])
                v.drain()
                v.tensor_scalar(epiD[:, ep:ep + D], epiD[:, ep:ep + D],
                                gat[:, i:i + 1], None, ALU.mult)
                v.drain()
                v.scalar_tensor_tensor(epiD[:, ep:ep + D], stream[:, i * D:(i + 1) * D],
                                       2.0, epiD[:, ep:ep + D], ALU.mult, ALU.add)
                v.drain()
                v.reduce_sum(lnstD[:, 4 * i + 1:4 * i + 2], epiD[:, ep:ep + D], AX.X)
                v.drain()
                v.tensor_scalar(lnstD[:, 4 * i:4 * i + 1], lnstD[:, 4 * i + 1:4 * i + 2],
                                -1.0 / D, None, ALU.mult).then_inc(sems["DLNV"], 1)
                v.wait_ge(sems["DSQS"], i + 1)
                v.tensor_scalar(lnstD[:, 4 * i + 2:4 * i + 3],
                                lnstD[:, 4 * i + 1:4 * i + 2],
                                1.0 / D, EPS, ALU.mult, ALU.add).then_inc(sems["DVAE"], 1)
                v.wait_ge(sems["DSQT"], i + 1)
                v.reciprocal(lnstD[:, 4 * i + 1:4 * i + 2], lnstD[:, 4 * i + 3:4 * i + 4])
                v.drain()
                v.tensor_mul(lnstD[:, 4 * i + 2:4 * i + 3], lnstD[:, 4 * i:4 * i + 1],
                             lnstD[:, 4 * i + 1:4 * i + 2])
                v.drain()
                if i >= 2:
                    v.wait_ge(sems["FDM0" if i % 2 == 0 else "FDM1"],
                              16 * (i // 2))
                v.tensor_scalar(fin[:, (i % 2) * D:(i % 2 + 1) * D], epiD[:, ep:ep + D],
                                lnstD[:, 4 * i + 1:4 * i + 2],
                                lnstD[:, 4 * i + 2:4 * i + 3],
                                ALU.mult, ALU.add).then_inc(sems["DEPV"], 1)

        @blk.sync
        def _(sy):
            for i in range(NT):
                sy.wait_ge(sems["DEPV"], i + 1)
                sy.dma_start(out=out_hbm[i * 128:(i + 1) * 128, :],
                             in_=fin[:, (i % 2) * D:(i % 2 + 1) * D]).then_inc(
                                 sems["FDM0" if i % 2 == 0 else "FDM1"], 16)

    with nc.Block() as blk:
        @blk.gpsimd
        def _(gp):
            gp.wait_ge(sems["FDM0"], 16 * (NT // 2))
            gp.wait_ge(sems["FDM1"], 16 * (NT // 2))



def build_program(n_phases=N_PHASES):
    nc = bass.Bass("TRN2", target_bir_lowering=False)

    x_ext = nc.dram_tensor("x_ext", [1 + T_PREF + T_OWN, D], F32, kind="ExternalInput").ap()
    wwT = nc.dram_tensor("wwT", [D, D], F32, kind="ExternalInput").ap()
    wrT = nc.dram_tensor("wrT", [D, D], F32, kind="ExternalInput").ap()
    decay = nc.dram_tensor("decay", [1, 1], F32, kind="ExternalInput").ap()
    log_alpha = nc.dram_tensor("log_alpha", [1, 1], F32, kind="ExternalInput").ap()
    iota4 = nc.dram_tensor("iota4", [128, 4], F32, kind="ExternalInput").ap()
    gwex4 = nc.dram_tensor("gwex4", [128, 4], F32, kind="ExternalInput").ap()
    diffsT = nc.dram_tensor("diffsT", [SCL, SCL], F32, kind="ExternalInput").ap()
    causalT = nc.dram_tensor("causalT", [SCL, SCL], F32, kind="ExternalInput").ap()
    identity = nc.dram_tensor("identity", [128, 128], F32, kind="ExternalInput").ap()
    onesr = nc.dram_tensor("onesr", [1, 128], F32, kind="ExternalInput").ap()
    if n_phases >= 2:
        gate_io = nc.dram_tensor("gate_io", [D, 8], F32, kind="ExternalInput").ap()
        gammacol = nc.dram_tensor("gammacol", [128, 8], F32, kind="ExternalInput").ap()
        betacol = nc.dram_tensor("betacol", [128, 8], F32, kind="ExternalInput").ap()
        w1_io = nc.dram_tensor("w1_io", [8, D, 2048], F32, kind="ExternalInput").ap()
        w2_io = nc.dram_tensor("w2_io", [8, 2048, D], F32, kind="ExternalInput").ap()
    if n_phases >= 3:
        u_io = nc.dram_tensor("u_io", [D, D], F32, kind="ExternalInput").ap()
        v_io = nc.dram_tensor("v_io", [D, D], F32, kind="ExternalInput").ap()
        u_nat = nc.dram_tensor("u_nat", [D, D], F32, kind="ExternalInput").ap()
        v_nat = nc.dram_tensor("v_nat", [D, D], F32, kind="ExternalInput").ap()
        cc_in = nc.dram_tensor("cc_in", [CCIN_N], BF16)
        cc_out = nc.dram_tensor("cc_out", [4 * CCIN_N], BF16)
        y1d = nc.dram_tensor("y1d", [8 * 128 * T_OWN], BF16)
        y2d = nc.dram_tensor("y2d", [8 * 128 * T_OWN], BF16)
    if n_phases >= 4:
        W1d = nc.dram_tensor("W1d", [D, 4096], F32, kind="ExternalInput").ap()
        W2d = nc.dram_tensor("W2d", [4096, D], F32, kind="ExternalInput").ap()
        b1col = nc.dram_tensor("b1col", [128, 32], F32, kind="ExternalInput").ap()
        b2rep = nc.dram_tensor("b2rep", [128, D], F32, kind="ExternalInput").ap()
        wrcol = nc.dram_tensor("wrcol", [128, 8], F32, kind="ExternalInput").ap()
        cc2_in = nc.dram_tensor("cc2_in", [T_OWN], F32)
        cc2_out = nc.dram_tensor("cc2_out", [4 * T_OWN], F32)
        W1B = nc.dram_tensor("w1bf", [D, 4096], BF16).ap()
        W2B = nc.dram_tensor("w2bf", [4096, D], BF16).ap()
    out_hbm = nc.dram_tensor("out", [T_OWN, D], F32, kind="ExternalOutput").ap()

    with ExitStack() as es:
        stream = es.enter_context(nc.sbuf_tensor("stream", [128, NT * D], BF16))
        ident_b = es.enter_context(nc.sbuf_tensor("ident_b", [128, 128], BF16))
        onesr_f = es.enter_context(nc.sbuf_tensor("onesr_f", [1, 128], F32))
        scal = es.enter_context(nc.sbuf_tensor("scal", [128, 16], F32))
        s0_ld = es.enter_context(nc.semaphore("s0_ld"))
        s0_mm = es.enter_context(nc.semaphore("s0_mm"))

        with nc.Block() as blk:
            c_iota4 = nc.sbuf_tensor("c_iota4", [128, 4], F32)
            c_gwex4 = nc.sbuf_tensor("c_gwex4", [128, 4], F32)
            c_two = nc.sbuf_tensor("c_two", [1, 2], F32)
            p_a0 = nc.psum_tensor("p_a0", [128, 8], F32)
            with c_iota4 as ci, c_gwex4 as cg, c_two as ct, p_a0 as pa:
                @blk.gpsimd
                def _(gp):
                    gp.dma_start(out=ident_b[:, :], in_=identity).then_inc(s0_ld, 16)
                    gp.dma_start(out=onesr_f[:, :], in_=onesr).then_inc(s0_ld, 16)
                    gp.dma_start(out=ci[:, :], in_=iota4).then_inc(s0_ld, 16)
                    gp.dma_start(out=cg[:, :], in_=gwex4).then_inc(s0_ld, 16)
                    gp.dma_start(out=ct[0:1, 0:1], in_=decay).then_inc(s0_ld, 16)
                    gp.dma_start(out=ct[0:1, 1:2], in_=log_alpha).then_inc(s0_ld, 16)

                @blk.tensor
                def _(tn):
                    tn.wait_ge(s0_ld, 96)
                    tn.matmul(pa[:, 0:1], onesr_f[0:1, :], ct[0:1, 0:1],
                              start=True, stop=True)
                    tn.matmul(pa[:, 1:2], onesr_f[0:1, :], ct[0:1, 1:2],
                              start=True, stop=True).then_inc(s0_mm, 1)

                @blk.scalar
                def _(s):
                    s.wait_ge(s0_mm, 1)
                    s.copy(scal[:, 0:1], pa[:, 0:1])
                    s.activation(scal[:, 1:2], pa[:, 0:1], ACT.Tanh, scale=0.5)
                    s.activation(scal[:, 3:4], pa[:, 1:2], ACT.Exp)
                    s.drain()
                    s.activation(scal[:, 1:2], scal[:, 1:2], ACT.Copy, scale=0.5, bias=0.5)
                    s.drain()
                    s.activation(scal[:, 2:3], scal[:, 1:2], ACT.Ln)
                    s.drain()
                    s.activation(scal[:, 4:5], scal[:, 2:3], ACT.Exp, scale=float(SCL))
                    for t in range(4):
                        s.activation(scal[:, 5 + t:6 + t], ci[:, t:t + 1], ACT.Exp,
                                     scale=scal[:, 2:3])
                        s.activation(scal[:, 9 + t:10 + t], cg[:, t:t + 1], ACT.Exp,
                                     scale=scal[:, 2:3])

        with ExitStack() as es_a:
            emit_phase_a(nc, es_a, dict(
                x_ext=x_ext, wwT=wwT, wrT=wrT, diffsT=diffsT, causalT=causalT,
                stream=stream, ident_b=ident_b, scal=scal))

        if n_phases >= 2:
            with ExitStack() as es_b:
                emit_phase_b(nc, es_b, dict(
                    gate_io=gate_io, gammacol=gammacol, betacol=betacol,
                    w1_io=w1_io, w2_io=w2_io, stream=stream,
                    ident_b=ident_b, onesr_f=onesr_f))

        if n_phases >= 3:
            cast_g = {}
            if n_phases >= 4:
                cast_g = dict(
                    W1E=W1d, W2E=W2d, W1B=W1B, W2B=W2B,
                    CWI=[es.enter_context(nc.semaphore("cwi%d" % q)) for q in range(4)],
                    CWO=[es.enter_context(nc.semaphore("cwo%d" % q)) for q in range(4)])
            with ExitStack() as es_c:
                emit_phase_c(nc, es_c, dict(
                    u_io=u_io, v_io=v_io, u_nat=u_nat, v_nat=v_nat,
                    cc_in=cc_in, cc_out=cc_out, y1d=y1d, y2d=y2d,
                    stream=stream, ident_b=ident_b, **cast_g))

        if n_phases >= 4:
            with ExitStack() as es_d:
                emit_phase_d(nc, es_d, dict(
                    W1B=W1B, W2B=W2B, CWO=cast_g["CWO"],
                    b1col=b1col, b2rep=b2rep, wrcol=wrcol,
                    cc2_in=cc2_in, cc2_out=cc2_out, stream=stream,
                    ident_b=ident_b, onesr_f=onesr_f, out_hbm=out_hbm))


        if n_phases >= 4:
            return nc
        s_fcp = es.enter_context(nc.semaphore("s_fcp"))
        s_fdm0 = es.enter_context(nc.semaphore("s_fdm0"))
        s_fdm1 = es.enter_context(nc.semaphore("s_fdm1"))
        with nc.Block() as blk:
            fin = nc.sbuf_tensor("fin", [128, 2 * D], F32)
            with fin as fb:
                @blk.vector
                def _(v):
                    for i in range(NT):
                        if i >= 2:
                            v.wait_ge(s_fdm0 if i % 2 == 0 else s_fdm1,
                                      16 * (i // 2))
                        fo = (i % 2) * D
                        v.tensor_copy(fb[:, fo:fo + D],
                                      stream[:, i * D:(i + 1) * D]).then_inc(s_fcp, 1)

                @blk.sync
                def _(sy):
                    for i in range(NT):
                        sy.wait_ge(s_fcp, i + 1)
                        fo = (i % 2) * D
                        sy.dma_start(out=out_hbm[i * 128:(i + 1) * 128, :],
                                     in_=fb[:, fo:fo + D]).then_inc(
                                         s_fdm0 if i % 2 == 0 else s_fdm1, 16)

                @blk.gpsimd
                def _(gp):
                    gp.wait_ge(s_fdm0, 16 * (NT // 2))
                    gp.wait_ge(s_fdm1, 16 * (NT // 2))

    return nc



_CONSTS = None


def _consts():
    global _CONSTS
    if _CONSTS is None:
        p = np.arange(128, dtype=np.float32)[:, None]
        iota4 = np.concatenate([p + 128 * t for t in range(4)], axis=1)
        gwex4 = np.concatenate([(SCL - 1) - (128 * t + p) for t in range(4)], axis=1)
        k = np.arange(SCL, dtype=np.float32)
        diffsT = (k[None, :] - 1.0 - k[:, None]).astype(np.float32)
        causalT = (k[None, :] > k[:, None]).astype(np.float32)
        identity = np.eye(128, dtype=np.float32)
        onesr = np.ones((1, 128), np.float32)
        _CONSTS = dict(iota4=np.ascontiguousarray(iota4),
                       gwex4=np.ascontiguousarray(gwex4),
                       diffsT=np.ascontiguousarray(diffsT),
                       causalT=np.ascontiguousarray(causalT),
                       identity=identity, onesr=onesr)
    return _CONSTS


def make_in_maps(inputs):
    x = np.ascontiguousarray(np.asarray(inputs["L0_attn_out"], dtype=np.float32))
    wwT = np.ascontiguousarray(np.asarray(inputs["L0_attn_W_write"], np.float32).T)
    wrT = np.ascontiguousarray(np.asarray(inputs["L0_attn_W_read"], np.float32).T)
    dec = np.asarray(inputs["L0_attn_decay"], np.float32).reshape(1, 1)
    la = np.asarray(inputs["L0_attn_log_alpha"], np.float32).reshape(1, 1)
    cst = _consts()
    extra = {}
    if N_PHASES >= 2:
        extra["gate_io"] = np.ascontiguousarray(np.asarray(inputs["L0_ffn_gate_w"], np.float32).T)
        extra["gammacol"] = np.ascontiguousarray(np.asarray(inputs["L0_ffn_ln_gamma"], np.float32).reshape(8, 128).T)
        extra["betacol"] = np.ascontiguousarray(np.asarray(inputs["L0_ffn_ln_beta"], np.float32).reshape(8, 128).T)
        extra["w1_io"] = np.ascontiguousarray(np.asarray(inputs["L0_ffn_w1"], np.float32).transpose(0, 2, 1))
        extra["w2_io"] = np.ascontiguousarray(np.asarray(inputs["L0_ffn_w2"], np.float32).transpose(0, 2, 1))
    if N_PHASES >= 3:
        tu = np.asarray(inputs["L1_attn_ternary_factor_u"], np.float32)
        tv = np.asarray(inputs["L1_attn_ternary_factor_v"], np.float32)
        extra["u_io"] = np.ascontiguousarray(tu.T)
        extra["v_io"] = np.ascontiguousarray(tv.T)
        extra["u_nat"] = np.ascontiguousarray(tu)
        extra["v_nat"] = np.ascontiguousarray(tv)
    if N_PHASES >= 4:
        extra["W1d"] = np.ascontiguousarray(np.asarray(inputs["L1_ffn_W1"], np.float32))
        extra["W2d"] = np.ascontiguousarray(np.asarray(inputs["L1_ffn_W2"], np.float32))
        extra["b1col"] = np.ascontiguousarray(np.asarray(inputs["L1_ffn_b1"], np.float32).reshape(32, 128).T)
        extra["b2rep"] = np.ascontiguousarray(np.broadcast_to(
            np.asarray(inputs["L1_ffn_b2"], np.float32), (128, D)))
        extra["wrcol"] = np.ascontiguousarray(np.asarray(inputs["L1_ffn_w_router"], np.float32).reshape(8, 128).T)
    in_maps = []
    for c in range(8):
        b, j = c // 4, c % 4
        xe = np.zeros((1 + T_PREF + T_OWN, D), np.float32)
        if j:
            xe[1 + T_PREF - T_OWN * j: 1 + T_PREF] = x[b, :T_OWN * j]
        xe[1 + T_PREF:] = x[b, T_OWN * j: T_OWN * (j + 1)]
        im = dict(x_ext=xe, wwT=wwT, wrT=wrT, decay=dec, log_alpha=la, **cst, **extra)
        in_maps.append(im)
    return in_maps


def kernel(**inputs):
    nc = build_program()
    in_maps = make_in_maps(inputs)
    res = run_bass_kernel_spmd(nc, in_maps, core_ids=list(range(8)))
    out = np.empty((2, 8192, D), np.float32)
    for c in range(8):
        b, j = c // 4, c % 4
        out[b, T_OWN * j: T_OWN * (j + 1)] = res.results[c]["out"]
    return out


# version 16
# speedup vs baseline: 1.2598x; 1.0043x over previous
from contextlib import ExitStack

import numpy as np
import ml_dtypes
import concourse.bass as bass
import concourse.mybir as mybir
from concourse.bass_utils import run_bass_kernel_spmd

F32 = mybir.dt.float32
BF16 = mybir.dt.bfloat16
ACT = mybir.ActivationFunctionType
ALU = mybir.AluOpType
AX = mybir.AxisListType

D = 1024
T_OWN = 2048
T_PREF = 6144
NT = T_OWN // 128
SCL = 512
NSC_PRE = T_PREF // SCL
NSC_OWN = T_OWN // SCL
NSC = NSC_PRE + NSC_OWN
XTW = 513
EPS = 1e-5

N_PHASES = 4


def _ap(t, off, dims):
    return bass.AP(t, off, dims)



def _clear_sems(nc, sems):
    with nc.Block() as blk:
        @blk.gpsimd
        def _(gp):
            for s in sems.values():
                gp.sem_clear(s)


def emit_phase_a(nc, es, g):
    x_ext, wwT, wrT = g["x_ext"], g["wwT"], g["wrT"]
    diffsT, causalT = g["diffsT"], g["causalT"]
    stream, ident_b, scal = g["stream"], g["ident_b"], g["scal"]

    wwT_sb = es.enter_context(nc.sbuf_tensor("wwT_sb", [128, 8 * D], BF16))
    wrT_sb = es.enter_context(nc.sbuf_tensor("wrT_sb", [128, 8 * D], BF16))
    state_bf = es.enter_context(nc.sbuf_tensor("state_bf", [128, 8 * D], BF16))
    xT = es.enter_context(nc.sbuf_tensor("xT", [128, 8 * XTW], BF16))
    xbf = es.enter_context(nc.sbuf_tensor("xbf", [128, 2 * 4 * D], BF16))
    wkbf = es.enter_context(nc.sbuf_tensor("wkbf", [128, 2 * 4 * D], BF16))
    xf32 = es.enter_context(nc.sbuf_tensor("xf32", [128, 2 * D], F32))
    vgw = es.enter_context(nc.sbuf_tensor("vgw", [128, 4 * D], BF16))
    vtm = es.enter_context(nc.sbuf_tensor("vtm", [128, 4 * D], BF16))
    mt = es.enter_context(nc.sbuf_tensor("mt", [128, 4 * SCL], F32))
    smt = es.enter_context(nc.sbuf_tensor("smt", [128, 4 * SCL], BF16))
    inter_sb = es.enter_context(nc.sbuf_tensor("inter_sb", [128, D], F32))
    reads_sb = es.enter_context(nc.sbuf_tensor("reads_sb", [128, D], BF16))
    readsT_sb = es.enter_context(nc.sbuf_tensor("readsT_sb", [128, D], BF16))
    epi = es.enter_context(nc.sbuf_tensor("epi", [128, 4 * D], F32))
    lnst = es.enter_context(nc.sbuf_tensor("lnst", [128, 16], F32))

    pv = es.enter_context(nc.psum_tensor("pv", [128, 1024], F32))
    pio_a = es.enter_context(nc.psum_tensor("pio_a", [128, 1024], F32))
    pio_b = es.enter_context(nc.psum_tensor("pio_b", [128, 1024], F32))
    pS = es.enter_context(nc.psum_tensor("pS", [128, 512], F32))
    ptr = es.enter_context(nc.psum_tensor("ptr", [128, 1024], BF16))

    sems = {}
    for n in ["LD0", "LD1", "XF0", "XF1", "MT", "TTR", "TEV", "VMM", "VEV",
              "SMM", "SMK", "IMM", "NMM", "REV", "RCB", "RTT", "RTE", "ROM",
              "EPS_", "EPV", "LNV", "SQS", "VAE", "SQT", "AMM", "AUP"]:
        sems[n] = es.enter_context(nc.semaphore("a_" + n))

    with nc.Block() as blk:
        @blk.gpsimd
        def _(gp):
            gp.dma_start(out=wwT_sb[:, :].rearrange("p (b e) -> p b e", b=8),
                         in_=wwT.rearrange("(b p) e -> p b e", p=128)).then_inc(sems["LD0"], 16)
            gp.dma_start(out=wrT_sb[:, :].rearrange("p (b e) -> p b e", b=8),
                         in_=wrT.rearrange("(b p) e -> p b e", p=128)).then_inc(sems["LD0"], 16)
            gp.dma_start(out=smt[:, :].rearrange("k (kt p) -> k kt p", kt=4),
                         in_=causalT.rearrange("(kt k) p -> k kt p", k=128)).then_inc(sems["LD0"], 16)

        @blk.sync
        def _(sy):
            sy.dma_start(out=mt[:, :].rearrange("k (kt p) -> k kt p", kt=4),
                         in_=diffsT.rearrange("(kt k) p -> k kt p", k=128)).then_inc(sems["MT"], 16)

        @blk.scalar
        def _(s):
            s.wait_ge(sems["LD0"], 48)
            s.wait_ge(sems["MT"], 16)
            for kt in range(4):
                s.activation(mt[:, kt * SCL:(kt + 1) * SCL],
                             mt[:, kt * SCL:(kt + 1) * SCL], ACT.Exp,
                             scale=scal[:, 2:3]).then_inc(sems["TEV"], 1)

        @blk.vector
        def _(v):
            v.memset(state_bf[:, :], 0.0)
            v.memset(xT[:, :], 0.0)
            v.wait_ge(sems["TEV"], 4)
            for kt in range(4):
                v.tensor_mul(mt[:, kt * SCL:(kt + 1) * SCL],
                             mt[:, kt * SCL:(kt + 1) * SCL],
                             smt[:, kt * SCL:(kt + 1) * SCL])

    Z = dict(LD0=48, LD1=0, XF0=0, XF1=0, TTR=0, TEV=4, VMM=0, VEV=0, SMM=0,
             SMK=0, IMM=0, NMM=0, REV=0, RCB=0, RTT=0, RTE=0, ROM=0,
             EPS_=0, EPV=0, LNV=0, SQS=0, VAE=0, SQT=0, AMM=0, AUP=0)
    bases = []
    for sc in range(NSC):
        bases.append(dict(Z))
        own = sc >= NSC_PRE
        Z["LD0" if sc % 2 == 0 else "LD1"] += 128
        Z["TTR"] += 4
        Z["TEV"] += 4
        Z["VMM"] += 4
        Z["VEV"] += 4
        Z["AMM"] += 8
        Z["AUP"] += 8
        if own:
            Z["XF0"] += 32
            Z["XF1"] += 32
            for nm in ["SMM", "SMK", "IMM", "NMM", "REV", "RCB", "RTT",
                       "RTE", "ROM", "EPS_", "EPV", "LNV", "SQS", "VAE", "SQT"]:
                Z[nm] += 4

    with nc.Block() as blk:

        @blk.gpsimd
        def _(gp):
            for sc in range(NSC):
                base = bases[sc]
                po = (sc % 2) * 4 * D
                if sc >= 2:
                    gp.wait_ge(sems["AMM"], bases[sc - 1]["AMM"])
                for t in range(4):
                    r = 1 + sc * SCL + t * 128
                    gp.dma_start(out=xbf[:, po + t * D:po + (t + 1) * D],
                                 in_=x_ext[r:r + 128, :]).then_inc(
                                     sems["LD0" if sc % 2 == 0 else "LD1"], 16)
                    gp.dma_start(out=wkbf[:, po + t * D:po + (t + 1) * D],
                                 in_=x_ext[r - 1:r + 127, :]).then_inc(
                                     sems["LD0" if sc % 2 == 0 else "LD1"], 16)

        @blk.sync
        def _(sy):
            for sc in range(NSC):
                base = bases[sc]
                if sc < NSC_PRE:
                    continue
                for t in range(4):
                    r = 1 + sc * SCL + t * 128
                    gi = (sc - NSC_PRE) * 4 + t
                    if gi >= 2:
                        sy.wait_ge(sems["EPV"], gi - 1)
                    sy.dma_start(out=xf32[:, (t % 2) * D:(t % 2 + 1) * D],
                                 in_=x_ext[r:r + 128, :]).then_inc(
                                     sems["XF0" if t % 2 == 0 else "XF1"], 16)

        @blk.tensor
        def _(tn):
            for sc in range(NSC):
                base = bases[sc]
                own = sc >= NSC_PRE
                po = (sc % 2) * 4 * D
                ldk = "LD0" if sc % 2 == 0 else "LD1"
                tn.wait_ge(sems[ldk], base[ldk] + 128)
                tn.wait_ge(sems["AUP"], base["AUP"])
                tn.wait_ge(sems["RTE"], base["RTE"])
                for t in range(4):
                    tn.wait_ge(sems["TEV"], base["TEV"] + t)
                    for jb in range(8):
                        ins = tn.transpose(ptr[:, jb * 128:(jb + 1) * 128],
                                           xbf[:, po + t * D + jb * 128:po + t * D + (jb + 1) * 128],
                                           ident_b[:, :])
                        if jb == 7:
                            ins.then_inc(sems["TTR"], 1)
                    tn.wait_ge(sems["TEV"], base["TEV"] + t + 1)
                    tn.wait_ge(sems["VEV"], base["VEV"] + t)
                    for hh in range(2):
                        for jb in range(8):
                            ins = tn.matmul(pv[:, hh * 512:(hh + 1) * 512],
                                            xT[:, jb * XTW + 1 + t * 128: jb * XTW + 1 + (t + 1) * 128],
                                            wwT_sb[:, jb * D + hh * 512: jb * D + (hh + 1) * 512],
                                            start=(jb == 0), stop=(jb == 7))
                    ins.then_inc(sems["VMM"], 1)
                if own:
                    for kt in range(4):
                        tn.wait_ge(sems["SMK"], base["SMK"] + kt)
                        n = 512 - kt * 128
                        for jb in range(8):
                            ins = tn.matmul(pS[:, 0:n],
                                            xT[:, jb * XTW + kt * 128: jb * XTW + (kt + 1) * 128],
                                            xT[:, jb * XTW + 1 + kt * 128: jb * XTW + 513],
                                            start=(jb == 0), stop=(jb == 7))
                        ins.then_inc(sems["SMM"], 1)
                    tn.wait_ge(sems["VEV"], base["VEV"] + 4)
                    tn.wait_ge(sems["AUP"], base["AUP"])
                    for pt in range(4):
                        tn.wait_ge(sems["REV"], base["REV"] + pt)
                        for hh in range(2):
                            for jb in range(8):
                                ins = tn.matmul(pio_a[:, hh * 512:(hh + 1) * 512],
                                                xT[:, jb * XTW + 1 + pt * 128: jb * XTW + 1 + (pt + 1) * 128],
                                                state_bf[:, jb * D + hh * 512: jb * D + (hh + 1) * 512],
                                                start=(jb == 0), stop=(jb == 7))
                        ins.then_inc(sems["IMM"], 1)
                        tn.wait_ge(sems["SMK"], base["SMK"] + pt + 1)
                        tn.wait_ge(sems["RCB"], base["RCB"] + pt)
                        for hh in range(2):
                            for kt in range(pt + 1):
                                ins = tn.matmul(pio_b[:, hh * 512:(hh + 1) * 512],
                                                smt[:, kt * SCL + pt * 128: kt * SCL + (pt + 1) * 128],
                                                vtm[:, kt * D + hh * 512: kt * D + (hh + 1) * 512],
                                                start=(kt == 0), stop=(kt == pt))
                        ins.then_inc(sems["NMM"], 1)
                        tn.wait_ge(sems["RCB"], base["RCB"] + pt + 1)
                        tn.wait_ge(sems["RTE"], base["RTE"] + pt)
                        for jb in range(8):
                            ins = tn.transpose(ptr[:, jb * 128:(jb + 1) * 128],
                                               reads_sb[:, jb * 128:(jb + 1) * 128],
                                               ident_b[:, :])
                        ins.then_inc(sems["RTT"], 1)
                        tn.wait_ge(sems["RTE"], base["RTE"] + pt + 1)
                        tn.wait_ge(sems["EPS_"], base["EPS_"] + pt)
                        for hh in range(2):
                            for jb in range(8):
                                ins = tn.matmul(pv[:, hh * 512:(hh + 1) * 512],
                                                readsT_sb[:, jb * 128:(jb + 1) * 128],
                                                wrT_sb[:, jb * D + hh * 512: jb * D + (hh + 1) * 512],
                                                start=(jb == 0), stop=(jb == 7))
                        ins.then_inc(sems["ROM"], 1)
                for et in range(8):
                    pA = pv if et % 2 == 0 else pio_a
                    if et == 0:
                        if own:
                            tn.wait_ge(sems["EPS_"], base["EPS_"] + 4)
                        else:
                            tn.wait_ge(sems["VEV"], base["VEV"] + 4)
                    if et == 1:
                        if own:
                            tn.wait_ge(sems["REV"], base["REV"] + 4)
                        else:
                            tn.wait_ge(sems["AUP"], base["AUP"])
                    if et >= 2:
                        tn.wait_ge(sems["AUP"], base["AUP"] + et - 1)
                    for hh in range(2):
                        for t in range(4):
                            ins = tn.matmul(pA[:, hh * 512:(hh + 1) * 512],
                                            wkbf[:, po + t * D + et * 128: po + t * D + (et + 1) * 128],
                                            vgw[:, t * D + hh * 512: t * D + (hh + 1) * 512],
                                            start=(t == 0), stop=(t == 3))
                    ins.then_inc(sems["AMM"], 1)

        @blk.scalar
        def _(s):
            for sc in range(NSC):
                base = bases[sc]
                own = sc >= NSC_PRE
                if sc > 0:
                    if sc - 1 >= NSC_PRE:
                        s.wait_ge(sems["IMM"], bases[sc - 1]["IMM"] + 4)
                    else:
                        s.wait_ge(sems["VMM"], bases[sc - 1]["VMM"] + 4)
                    s.copy(_ap(xT, 0, [[8 * XTW, 128], [XTW, 8], [1, 1]]),
                           _ap(xT, 512, [[8 * XTW, 128], [XTW, 8], [1, 1]]))
                for t in range(4):
                    s.wait_ge(sems["TTR"], base["TTR"] + t + 1)
                    ins = s.activation(
                        _ap(xT, 1 + t * 128, [[8 * XTW, 128], [XTW, 8], [1, 128]]),
                        _ap(ptr, 0, [[1024, 128], [128, 8], [1, 128]]),
                        ACT.Copy)
                    ins.then_inc(sems["TEV"], 1)
                    s.wait_ge(sems["VMM"], base["VMM"] + t + 1)
                    ins = s.activation(vgw[:, t * D:(t + 1) * D], pv[:, :], ACT.Copy,
                                       scale=scal[:, 9 + t:10 + t])
                    if own:
                        ins = s.activation(vtm[:, t * D:(t + 1) * D], pv[:, :], ACT.Copy)
                    ins.then_inc(sems["VEV"], 1)
                if own:
                    for pt in range(5):
                        if pt > 0:
                            eq = ((pt - 1) % 2) * 2 * D
                            q = pt - 1
                            s.wait_ge(sems["LNV"], base["LNV"] + pt)
                            s.activation(epi[:, eq:eq + D], epi[:, eq + D:eq + 2 * D],
                                         ACT.Square, bias=lnst[:, 12 + q:13 + q],
                                         accum_out=lnst[:, 0 + q:1 + q]).then_inc(sems["SQS"], 1)
                            s.wait_ge(sems["VAE"], base["VAE"] + pt)
                            s.activation(lnst[:, 8 + q:9 + q], lnst[:, 4 + q:5 + q],
                                         ACT.Sqrt).then_inc(sems["SQT"], 1)
                        if pt == 4:
                            break
                        s.wait_ge(sems["IMM"], base["IMM"] + pt + 1)
                        s.wait_ge(sems["RCB"], base["RCB"] + pt)
                        s.activation(inter_sb[:, :], pio_a[:, :], ACT.Copy,
                                     scale=scal[:, 5 + pt:6 + pt]).then_inc(sems["REV"], 1)
                        s.wait_ge(sems["RTT"], base["RTT"] + pt + 1)
                        if pt > 0:
                            s.wait_ge(sems["ROM"], base["ROM"] + pt)
                        s.copy(readsT_sb[:, :], ptr[:, :]).then_inc(sems["RTE"], 1)
                        s.wait_ge(sems["ROM"], base["ROM"] + pt + 1)
                        if pt >= 2:
                            s.wait_ge(sems["EPV"], base["EPV"] + pt - 1)
                        ep = (pt % 2) * 2 * D
                        s.activation(epi[:, ep:ep + D], pv[:, :], ACT.Copy,
                                     scale=scal[:, 3:4]).then_inc(sems["EPS_"], 1)

        @blk.vector
        def _(v):
            for sc in range(NSC):
                base = bases[sc]
                own = sc >= NSC_PRE
                if own:
                    for kt in range(4):
                        n = 512 - kt * 128
                        v.wait_ge(sems["SMM"], base["SMM"] + kt + 1)
                        v.tensor_mul(smt[:, kt * SCL + kt * 128: kt * SCL + 512],
                                     pS[:, 0:n],
                                     mt[:, kt * SCL + kt * 128: kt * SCL + 512]
                                     ).then_inc(sems["SMK"], 1)
                    for pt in range(4):
                        ep = (pt % 2) * 2 * D
                        v.wait_ge(sems["NMM"], base["NMM"] + pt + 1)
                        v.wait_ge(sems["REV"], base["REV"] + pt + 1)
                        if pt > 0:
                            v.wait_ge(sems["RTT"], base["RTT"] + pt)
                        v.tensor_add(reads_sb[:, :], inter_sb[:, :],
                                     pio_b[:, :]).then_inc(sems["RCB"], 1)
                        gi = (sc - NSC_PRE) * 4 + pt
                        v.wait_ge(sems["EPS_"], base["EPS_"] + pt + 1)
                        v.wait_ge(sems["XF0" if pt % 2 == 0 else "XF1"],
                                  base["XF0" if pt % 2 == 0 else "XF1"] + 16 * (pt // 2 + 1))
                        v.scalar_tensor_tensor(epi[:, ep + D:ep + 2 * D],
                                               xf32[:, (pt % 2) * D:(pt % 2 + 1) * D], 2.0,
                                               epi[:, ep:ep + D], ALU.mult, ALU.add)
                        v.drain()
                        v.reduce_sum(lnst[:, 8 + pt:9 + pt], epi[:, ep + D:ep + 2 * D], AX.X)
                        v.drain()
                        v.tensor_scalar(lnst[:, 12 + pt:13 + pt], lnst[:, 8 + pt:9 + pt],
                                        -1.0 / D, None, ALU.mult).then_inc(sems["LNV"], 1)
                        v.wait_ge(sems["SQS"], base["SQS"] + pt + 1)
                        v.tensor_scalar(lnst[:, 4 + pt:5 + pt], lnst[:, 0 + pt:1 + pt],
                                        1.0 / D, EPS, ALU.mult, ALU.add).then_inc(sems["VAE"], 1)
                        v.wait_ge(sems["SQT"], base["SQT"] + pt + 1)
                        v.reciprocal(lnst[:, 0 + pt:1 + pt], lnst[:, 8 + pt:9 + pt])
                        v.drain()
                        v.tensor_mul(lnst[:, 4 + pt:5 + pt], lnst[:, 12 + pt:13 + pt],
                                     lnst[:, 0 + pt:1 + pt])
                        v.drain()
                        oc = gi * D
                        v.tensor_scalar(stream[:, oc:oc + D], epi[:, ep + D:ep + 2 * D],
                                        lnst[:, 0 + pt:1 + pt], lnst[:, 4 + pt:5 + pt],
                                        ALU.mult, ALU.add).then_inc(sems["EPV"], 1)
                for et in range(8):
                    pA = pv if et % 2 == 0 else pio_a
                    v.wait_ge(sems["AMM"], base["AMM"] + et + 1)
                    v.scalar_tensor_tensor(state_bf[:, et * D:(et + 1) * D],
                                           state_bf[:, et * D:(et + 1) * D],
                                           scal[:, 4:5], pA[:, :],
                                           ALU.mult, ALU.add).then_inc(sems["AUP"], 1)
    _clear_sems(nc, sems)



def emit_phase_b(nc, es, g):
    gate_io, gammacol, betacol = g["gate_io"], g["gammacol"], g["betacol"]
    w1_io, w2_io = g["w1_io"], g["w2_io"]
    stream, ident_b, onesr_f = g["stream"], g["ident_b"], g["onesr_f"]

    x2T = es.enter_context(nc.sbuf_tensor("x2T", [128, 8 * T_OWN], BF16))
    hT = es.enter_context(nc.sbuf_tensor("hT", [128, 16 * 1024], BF16))
    w1p = es.enter_context(nc.sbuf_tensor("w1p", [128, 2 * 4096], BF16))
    w2sb = es.enter_context(nc.sbuf_tensor("w2sb", [128, 16 * 1024], BF16))
    acc = es.enter_context(nc.sbuf_tensor("acc", [128, 8 * 1024], F32))
    comb = es.enter_context(nc.sbuf_tensor("comb", [128, 16 * 8], F32))
    gsc = es.enter_context(nc.sbuf_tensor("gsc", [128, 16 * 16], F32))
    gv8 = es.enter_context(nc.sbuf_tensor("gv8", [128, 16 * 24], F32))
    gate_sb = es.enter_context(nc.sbuf_tensor("gate_sb", [128, 64], F32))
    gam_sb = es.enter_context(nc.sbuf_tensor("gam_sb", [128, 8], F32))
    bet_sb = es.enter_context(nc.sbuf_tensor("bet_sb", [128, 8], F32))
    g2 = es.enter_context(nc.sbuf_tensor("g2", [128, 64], BF16))
    ceb = es.enter_context(nc.sbuf_tensor("ceb", [128, 8], F32))
    beb = es.enter_context(nc.sbuf_tensor("beb", [128, 8], F32))
    cb_row = es.enter_context(nc.sbuf_tensor("cb_row", [1, 16], F32))
    sqdump = es.enter_context(nc.sbuf_tensor("sqdump", [128, D], F32))
    epiB = es.enter_context(nc.sbuf_tensor("epiB", [128, 4 * D], F32))
    lnstB = es.enter_context(nc.sbuf_tensor("lnstB", [128, 32], F32))

    psB_init = ExitStack()
    pG = psB_init.enter_context(nc.psum_tensor("pG", [128, 8], F32))
    pT = psB_init.enter_context(nc.psum_tensor("pT", [1, 16], F32))
    ptrB = psB_init.enter_context(nc.psum_tensor("ptrB", [128, 1024], BF16))

    sems = {}
    for n in ["LD", "TT", "TE", "G2V", "GT0", "GS0", "GBC", "GT", "GA", "GB",
              "GC_", "GD", "GE", "GF", "CMB", "W1P0", "W1P1", "W2L", "HMM",
              "HEV", "HPC", "YMM", "YEV", "BLNV", "BSQS", "BVAE", "BSQT", "BEPV"]:
        sems[n] = es.enter_context(nc.semaphore("b_" + n))

    with nc.Block() as blk:
        @blk.gpsimd
        def _(gp):
            gp.dma_start(out=gate_sb[:, :].rearrange("p (cb e) -> p cb e", cb=8),
                         in_=gate_io.rearrange("(cb p) e -> p cb e", p=128)).then_inc(sems["LD"], 16)
            gp.dma_start(out=gam_sb[:, :], in_=gammacol).then_inc(sems["LD"], 16)
            gp.dma_start(out=bet_sb[:, :], in_=betacol).then_inc(sems["LD"], 16)

        @blk.tensor
        def _(tn):
            for i in range(NT):
                tn.wait_ge(sems["TE"], i)
                for jb in range(8):
                    ins = tn.transpose(ptrB[:, jb * 128:(jb + 1) * 128],
                                       stream[:, i * D + jb * 128:i * D + (jb + 1) * 128],
                                       ident_b[:, :])
                ins.then_inc(sems["TT"], 1)
            tn.wait_ge(sems["LD"], 48)
            for cb in range(8):
                tn.matmul(pT[0:1, 0:8], gam_sb[:, cb:cb + 1],
                          gate_sb[:, cb * 8:(cb + 1) * 8],
                          start=(cb == 0), stop=(cb == 7))
            for cb in range(8):
                ins = tn.matmul(pT[0:1, 8:16], bet_sb[:, cb:cb + 1],
                                gate_sb[:, cb * 8:(cb + 1) * 8],
                                start=(cb == 0), stop=(cb == 7))
            ins.then_inc(sems["GT0"], 1)
            tn.wait_ge(sems["GS0"], 1)
            tn.matmul(pG[:, 0:8], onesr_f[0:1, :], cb_row[0:1, 0:8],
                      start=True, stop=True).then_inc(sems["GT0"], 1)
            tn.wait_ge(sems["GBC"], 1)
            tn.matmul(pG[:, 0:8], onesr_f[0:1, :], cb_row[0:1, 8:16],
                      start=True, stop=True).then_inc(sems["GT0"], 1)
            tn.wait_ge(sems["G2V"], 1)
            tn.wait_ge(sems["GBC"], 2)
            for i in range(NT):
                if i > 0:
                    tn.wait_ge(sems["GE"], i)
                tn.wait_ge(sems["TE"], i + 1)
                for cb in range(8):
                    ins = tn.matmul(pG[:, 0:8],
                                    x2T[:, cb * T_OWN + i * 128: cb * T_OWN + (i + 1) * 128],
                                    g2[:, cb * 8:(cb + 1) * 8],
                                    start=(cb == 0), stop=(cb == 7))
                ins.then_inc(sems["GT"], 1)

        @blk.scalar
        def _(s):
            for i in range(NT):
                s.wait_ge(sems["TT"], i + 1)
                s.activation(
                    _ap(x2T, i * 128, [[8 * T_OWN, 128], [T_OWN, 8], [1, 128]]),
                    _ap(ptrB, 0, [[1024, 128], [128, 8], [1, 128]]),
                    ACT.Copy).then_inc(sems["TE"], 1)
            s.wait_ge(sems["GT0"], 1)
            s.copy(cb_row[0:1, 0:16], pT[0:1, 0:16]).then_inc(sems["GS0"], 1)
            for i in range(NT):
                gc = i * 16
                s.wait_ge(sems["GA"], i + 1)
                s.drain()
                s.activation(sqdump[:, :], stream[:, i * D:(i + 1) * D],
                             ACT.Square, bias=gsc[:, gc + 0:gc + 1],
                             accum_out=gsc[:, gc + 1:gc + 2]).then_inc(sems["GB"], 1)
                s.wait_ge(sems["GC_"], i + 1)
                s.activation(gsc[:, gc + 3:gc + 4], gsc[:, gc + 2:gc + 3],
                             ACT.Sqrt).then_inc(sems["GD"], 1)
                s.wait_ge(sems["GE"], i + 1)
                s.activation(gv8[:, i * 24 + 16:i * 24 + 24],
                             gv8[:, i * 24 + 8:i * 24 + 16], ACT.Exp,
                             bias=gsc[:, gc + 7:gc + 8],
                             accum_out=gsc[:, gc + 8:gc + 9]).then_inc(sems["GF"], 1)

        @blk.vector
        def _(v):
            v.wait_ge(sems["LD"], 48)
            for cb in range(8):
                ins = v.tensor_scalar(g2[:, cb * 8:(cb + 1) * 8], gate_sb[:, cb * 8:(cb + 1) * 8],
                                      gam_sb[:, cb:cb + 1], None, ALU.mult)
            ins.then_inc(sems["G2V"], 1)
            v.wait_ge(sems["GT0"], 2)
            v.tensor_copy(ceb[:, :], pG[:, 0:8]).then_inc(sems["GBC"], 1)
            v.wait_ge(sems["GT0"], 3)
            v.tensor_copy(beb[:, :], pG[:, 0:8]).then_inc(sems["GBC"], 1)
            for i in range(NT):
                gc, gvv = i * 16, i * 24
                v.reduce_sum(gsc[:, gc + 15:gc + 16], stream[:, i * D:(i + 1) * D], AX.X)
                v.drain()
                v.tensor_scalar(gsc[:, gc + 0:gc + 1], gsc[:, gc + 15:gc + 16],
                                -1.0 / D, None, ALU.mult).then_inc(sems["GA"], 1)
                v.wait_ge(sems["GB"], i + 1)
                v.tensor_scalar(gsc[:, gc + 2:gc + 3], gsc[:, gc + 1:gc + 2],
                                1.0 / D, EPS, ALU.mult, ALU.add).then_inc(sems["GC_"], 1)
                v.wait_ge(sems["GD"], i + 1)
                v.reciprocal(gsc[:, gc + 4:gc + 5], gsc[:, gc + 3:gc + 4])
                v.drain()
                v.tensor_mul(gsc[:, gc + 5:gc + 6], gsc[:, gc + 0:gc + 1],
                             gsc[:, gc + 4:gc + 5])
                v.drain()
                v.tensor_scalar(gv8[:, gvv:gvv + 8], ceb[:, :],
                                gsc[:, gc + 5:gc + 6], None, ALU.mult)
                v.drain()
                v.tensor_add(gv8[:, gvv:gvv + 8], gv8[:, gvv:gvv + 8], beb[:, :])
                v.drain()
                v.wait_ge(sems["GT"], i + 1)
                v.scalar_tensor_tensor(gv8[:, gvv + 8:gvv + 16], pG[:, 0:8],
                                       gsc[:, gc + 4:gc + 5], gv8[:, gvv:gvv + 8],
                                       ALU.mult, ALU.add)
                v.drain()
                v.reduce_max(gsc[:, gc + 6:gc + 7], gv8[:, gvv + 8:gvv + 16], AX.X)
                v.drain()
                v.tensor_scalar(gsc[:, gc + 7:gc + 8], gsc[:, gc + 6:gc + 7],
                                -1.0, None, ALU.mult).then_inc(sems["GE"], 1)
                v.wait_ge(sems["GF"], i + 1)
                v.reciprocal(gsc[:, gc + 9:gc + 10], gsc[:, gc + 8:gc + 9])
                v.drain()
                v.tensor_scalar(gv8[:, gvv + 16:gvv + 24], gv8[:, gvv + 16:gvv + 24],
                                gsc[:, gc + 9:gc + 10], None, ALU.mult)
                v.drain()
                v.reduce_max(gsc[:, gc + 10:gc + 11], gv8[:, gvv + 16:gvv + 24], AX.X)
                v.drain()
                v.tensor_scalar(gv8[:, gvv:gvv + 8], gv8[:, gvv + 16:gvv + 24],
                                gsc[:, gc + 10:gc + 11], None, ALU.is_lt)
                v.drain()
                v.tensor_mul(gv8[:, gvv:gvv + 8], gv8[:, gvv:gvv + 8],
                             gv8[:, gvv + 16:gvv + 24])
                v.drain()
                v.reduce_max(gsc[:, gc + 11:gc + 12], gv8[:, gvv:gvv + 8], AX.X)
                v.drain()
                v.tensor_scalar(gv8[:, gvv:gvv + 8], gv8[:, gvv + 16:gvv + 24],
                                gsc[:, gc + 11:gc + 12], None, ALU.is_ge)
                v.drain()
                v.tensor_mul(comb[:, i * 8:(i + 1) * 8], gv8[:, gvv:gvv + 8],
                             gv8[:, gvv + 16:gvv + 24]).then_inc(sems["CMB"], 1)

    psB_init.close()
    pH = es.enter_context(nc.psum_tensor("pH", [128, 2048], F32))
    pY = es.enter_context(nc.psum_tensor("pY", [128, 2048], F32))
    C = dict(LD=48, W1P0=0, W1P1=0, W2L=0, HMM=0, HEV=0, HPC=0, YMM=0,
             YEV=0, BLNV=0, BSQS=0, BVAE=0, BSQT=0, BEPV=0)

    GB_T = min(1024, T_OWN)
    NGRP = T_OWN // GB_T
    NPC = GB_T // 512
    NPTL = GB_T // 128
    NHG = 16 * NPC
    NYG = NPTL * 2

    C = dict(W1P0=0, W1P1=0, W2L=0, HMM=0, HEV=0, HPC=0, YMM=0,
             YEV=0, BLNV=0, BSQS=0, BVAE=0, BSQT=0, BEPV=0)

    for grp in range(NGRP):
        base = dict(C)
        with nc.Block() as blk:

            @blk.gpsimd
            def _(gp):
                for e in range(8):
                    if e > 0:
                        gp.wait_ge(sems["YMM"], base["YMM"] + NPTL * e)
                    gp.dma_start(
                        out=w2sb[:, :].rearrange("p (fb c) -> p fb c", fb=16),
                        in_=w2_io[e].rearrange("(fb p) c -> p fb c", p=128)
                    ).then_inc(sems["W2L"], 16)
                    for k in range(4):
                        gidx = 4 * e + k
                        if gidx >= 2:
                            pe, pk = (gidx - 2) // 4, (gidx - 2) % 4
                            gp.wait_ge(sems["HMM"],
                                       base["HMM"] + 16 * pe + 4 * (pk + 1))
                        buf = gidx % 2
                        gp.dma_start(
                            out=w1p[:, buf * 4096:(buf + 1) * 4096].rearrange(
                                "p (cb f) -> p cb f", cb=8),
                            in_=w1_io[e, :, 512 * k:512 * (k + 1)].rearrange(
                                "(cb p) f -> p cb f", p=128)
                        ).then_inc(sems["W1P0" if buf == 0 else "W1P1"], 16)

            @blk.tensor
            def _(tn):
                for e in range(8):
                    tn.wait_ge(sems["W2L"], base["W2L"] + 16 * (e + 1))
                    for k in range(4):
                        gidx = 4 * e + k
                        buf = gidx % 2
                        tn.wait_ge(sems["W1P0" if buf == 0 else "W1P1"],
                                   base["W1P0" if buf == 0 else "W1P1"] + 16 * (gidx // 2 + 1))
                        for ft2 in range(4):
                            ft = 4 * k + ft2
                            if 16 * e + ft >= 2:
                                tn.wait_ge(sems["HEV"], base["HEV"] + 16 * e + ft - 1)
                            ph = pH[:, (ft % 2) * 1024:(ft % 2 + 1) * 1024]
                            for cb in range(8):
                                for pc in range(NPC):
                                    ins = tn.matmul(
                                        ph[:, 512 * pc:512 * (pc + 1)],
                                        w1p[:, buf * 4096 + cb * 512 + 128 * ft2:
                                            buf * 4096 + cb * 512 + 128 * (ft2 + 1)],
                                        x2T[:, cb * T_OWN + GB_T * grp + 512 * pc:
                                            cb * T_OWN + GB_T * grp + 512 * (pc + 1)],
                                        start=(cb == 0), stop=(cb == 7),
                                        skip_group_check=True)
                            ins.then_inc(sems["HMM"], 1)
                    tn.wait_ge(sems["HEV"], base["HEV"] + 16 * (e + 1))
                    for ptl in range(NPTL):
                        gptl = NPTL * e + ptl
                        if gptl >= 2:
                            tn.wait_ge(sems["YEV"], base["YEV"] + 2 * (gptl - 1))
                        po = (gptl % 2) * 1024
                        for fk in range(16):
                            for ch in range(2):
                                ins = tn.matmul(
                                    pY[:, po + 512 * ch:po + 512 * (ch + 1)],
                                    hT[:, GB_T * fk + 128 * ptl: GB_T * fk + 128 * (ptl + 1)],
                                    w2sb[:, 1024 * fk + 512 * ch: 1024 * fk + 512 * (ch + 1)],
                                    start=(fk == 0), stop=(fk == 15),
                                    skip_group_check=True)
                        ins.then_inc(sems["YMM"], 1)

            @blk.scalar
            def _(s):
                for e in range(8):
                    if e > 0:
                        s.wait_ge(sems["YMM"], base["YMM"] + NPTL * e)
                    for ft in range(16):
                        s.wait_ge(sems["HMM"], base["HMM"] + 16 * e + ft + 1)
                        ph = pH[:, (ft % 2) * 1024:(ft % 2) * 1024 + GB_T]
                        s.activation(hT[:, GB_T * ft: GB_T * (ft + 1)],
                                     ph, ACT.Gelu).then_inc(sems["HEV"], 1)
                for i in range(NPTL):
                    s.wait_ge(sems["BLNV"], base["BLNV"] + i + 1)
                    s.drain()
                    eq = (i % 2) * 2 * D
                    s.activation(epiB[:, eq + D:eq + 2 * D], epiB[:, eq:eq + D],
                                 ACT.Square, bias=lnstB[:, 4 * i:4 * i + 1],
                                 accum_out=lnstB[:, 4 * i + 1:4 * i + 2]).then_inc(sems["BSQS"], 1)
                    s.wait_ge(sems["BVAE"], base["BVAE"] + i + 1)
                    s.activation(lnstB[:, 4 * i + 3:4 * i + 4],
                                 lnstB[:, 4 * i + 2:4 * i + 3],
                                 ACT.Sqrt).then_inc(sems["BSQT"], 1)

            @blk.vector
            def _(v):
                for e in range(8):
                    for ptl in range(NPTL):
                        gptl = NPTL * e + ptl
                        v.wait_ge(sems["YMM"], base["YMM"] + gptl + 1)
                        gi = NPTL * grp + ptl
                        cc = comb[:, gi * 8 + e: gi * 8 + e + 1]
                        po = (gptl % 2) * 1024
                        for ch in range(2):
                            py = pY[:, po + 512 * ch:po + 512 * (ch + 1)]
                            a = acc[:, 1024 * ptl + 512 * ch: 1024 * ptl + 512 * (ch + 1)]
                            if e == 0:
                                ins = v.tensor_scalar(a, py, cc, None, ALU.mult)
                            else:
                                ins = v.scalar_tensor_tensor(a, py, cc, a, ALU.mult, ALU.add)
                            ins.then_inc(sems["YEV"], 1)
                for i in range(NPTL):
                    gi = NPTL * grp + i
                    ep = (i % 2) * 2 * D
                    if i >= 2:
                        v.wait_ge(sems["BSQS"], base["BSQS"] + i - 1)
                    v.tensor_add(epiB[:, ep:ep + D], stream[:, gi * D:(gi + 1) * D],
                                 acc[:, i * D:(i + 1) * D])
                    v.drain()
                    v.reduce_sum(lnstB[:, 4 * i + 1:4 * i + 2], epiB[:, ep:ep + D], AX.X)
                    v.drain()
                    v.tensor_scalar(lnstB[:, 4 * i:4 * i + 1], lnstB[:, 4 * i + 1:4 * i + 2],
                                    -1.0 / D, None, ALU.mult).then_inc(sems["BLNV"], 1)
                    v.wait_ge(sems["BSQS"], base["BSQS"] + i + 1)
                    v.tensor_scalar(lnstB[:, 4 * i + 2:4 * i + 3],
                                    lnstB[:, 4 * i + 1:4 * i + 2],
                                    1.0 / D, EPS, ALU.mult, ALU.add).then_inc(sems["BVAE"], 1)
                    v.wait_ge(sems["BSQT"], base["BSQT"] + i + 1)
                    v.reciprocal(lnstB[:, 4 * i + 1:4 * i + 2], lnstB[:, 4 * i + 3:4 * i + 4])
                    v.drain()
                    v.tensor_mul(lnstB[:, 4 * i + 2:4 * i + 3], lnstB[:, 4 * i:4 * i + 1],
                                 lnstB[:, 4 * i + 1:4 * i + 2])
                    v.drain()
                    v.tensor_scalar(stream[:, gi * D:(gi + 1) * D], epiB[:, ep:ep + D],
                                    lnstB[:, 4 * i + 1:4 * i + 2],
                                    lnstB[:, 4 * i + 2:4 * i + 3],
                                    ALU.mult, ALU.add).then_inc(sems["BEPV"], 1)

        C["W2L"] += 16 * 8
        C["W1P0"] += 16 * 16
        C["W1P1"] += 16 * 16
        C["HMM"] += 16 * 8
        C["HEV"] += 16 * 8
        C["YMM"] += NPTL * 8
        C["YEV"] += NYG * 8
        C["BLNV"] += NPTL
        C["BSQS"] += NPTL
        C["BVAE"] += NPTL
        C["BSQT"] += NPTL
        C["BEPV"] += NPTL
    _clear_sems(nc, sems)



SCALE_C = 0.03125
SEC = 2048 * 1024
CCIN_N = 4 * SEC


def emit_phase_c(nc, es, g):
    u_io, v_io, u_nat, v_nat = g["u_io"], g["v_io"], g["u_nat"], g["v_nat"]
    stream, ident_b = g["stream"], g["ident_b"]
    cc_in, cc_out = g["cc_in"], g["cc_out"]
    S_ALL = 4 * T_OWN

    cast_jobs = None
    if g.get("W1E") is not None:
        W1E, W2E, W1B, W2B = g["W1E"], g["W2E"], g["W1B"], g["W2B"]
        CWI, CWO = g["CWI"], g["CWO"]
        cast_jobs = []
        for r in range(8):
            for c in range(8):
                cast_jobs.append((W1E, W1B, 128 * r, 512 * c))
        for r in range(32):
            for c in range(2):
                cast_jobs.append((W2E, W2B, 128 * r, 512 * c))

    def cast_batch(h):
        if cast_jobs is None:
            return []
        lo = 0 if h == 0 else 4 + 18 * (h - 1)
        hi = 4 if h == 0 else min(len(cast_jobs), 4 + 18 * h)
        return range(lo, hi)

    sems = {}
    for n in ["LD", "TT", "TE", "PJM", "PJE", "PJD0", "PJD1", "PJD2", "PJD3", "CCS", "CCD",
              "PLD0", "PLD1", "PLDA0", "PLDA1", "QLD0", "QLD1", "SVC", "FMM", "THE", "YMM", "YEV", "YDM0", "YDM1",
              "GLD", "GMM", "GEV", "CLNV", "CSQS", "CVAE", "CSQT", "CEPV"]:
        sems[n] = es.enter_context(nc.semaphore("c_" + n))

    with ExitStack() as es1:
        x3T = es1.enter_context(nc.sbuf_tensor("x3T", [128, 8 * T_OWN], BF16))
        uio_sb = es1.enter_context(nc.sbuf_tensor("uio_sb", [128, 8 * D], BF16))
        vio_sb = es1.enter_context(nc.sbuf_tensor("vio_sb", [128, 8 * D], BF16))
        qev = es1.enter_context(nc.sbuf_tensor("qev", [128, 4 * 512], BF16))
        ptrC = es1.enter_context(nc.psum_tensor("ptrC", [128, 1024], BF16))
        pPJ = es1.enter_context(nc.psum_tensor("pPJ", [128, 1024], F32))

        with nc.Block() as blk:
            @blk.gpsimd
            def _(gp):
                gp.dma_start(out=uio_sb[:, :].rearrange("p (b e) -> p b e", b=8),
                             in_=u_io.rearrange("(b p) e -> p b e", p=128)).then_inc(sems["LD"], 16)
                gp.dma_start(out=vio_sb[:, :].rearrange("p (b e) -> p b e", b=8),
                             in_=v_io.rearrange("(b p) e -> p b e", p=128)).then_inc(sems["LD"], 16)
                npj_feat = 2 * 8 * (T_OWN // 512)
                for q4 in range(4):
                    gp.wait_ge(sems["PJD%d" % q4], 16 * (npj_feat // 4))
                gp.collective_compute(
                    "AllGather", ALU.bypass,
                    replica_groups=[[0, 1, 2, 3], [4, 5, 6, 7]],
                    ins=[cc_in[0:2 * SEC]], outs=[cc_out[0:8 * SEC]],
                ).then_inc(sems["CCS"], 1)

            @blk.tensor
            def _(tn):
                for i in range(NT):
                    tn.wait_ge(sems["TE"], i)
                    for jb in range(8):
                        ins = tn.transpose(ptrC[:, jb * 128:(jb + 1) * 128],
                                           stream[:, i * D + jb * 128:i * D + (jb + 1) * 128],
                                           ident_b[:, :])
                    ins.then_inc(sems["TT"], 1)
                tn.wait_ge(sems["LD"], 32)
                tn.wait_ge(sems["TE"], NT)
                gidx = 0
                for w_sb in (uio_sb, vio_sb):
                    for ct in range(8):
                        for pc in range(T_OWN // 512):
                            if gidx >= 2:
                                tn.wait_ge(sems["PJE"], gidx - 1)
                            ps = pPJ[:, (gidx % 2) * 512:(gidx % 2 + 1) * 512]
                            for jb in range(8):
                                ins = tn.matmul(
                                    ps,
                                    w_sb[:, jb * D + 128 * ct: jb * D + 128 * (ct + 1)],
                                    x3T[:, jb * T_OWN + 512 * pc: jb * T_OWN + 512 * (pc + 1)],
                                    start=(jb == 0), stop=(jb == 7))
                            ins.then_inc(sems["PJM"], 1)
                            gidx += 1
                for w_sb in (uio_sb, vio_sb):
                    for i in range(NT):
                        for crh in range(2):
                            tn.wait_ge(sems["PJE"], gidx - 1)
                            ps = pPJ[:, (gidx % 2) * 512:(gidx % 2 + 1) * 512]
                            for jb in range(8):
                                ins = tn.matmul(
                                    ps,
                                    x3T[:, jb * T_OWN + 128 * i: jb * T_OWN + 128 * (i + 1)],
                                    w_sb[:, jb * D + 512 * crh: jb * D + 512 * (crh + 1)],
                                    start=(jb == 0), stop=(jb == 7))
                            ins.then_inc(sems["PJM"], 1)
                            gidx += 1

            @blk.scalar
            def _(s):
                for i in range(NT):
                    s.wait_ge(sems["TT"], i + 1)
                    s.activation(
                        _ap(x3T, i * 128, [[8 * T_OWN, 128], [T_OWN, 8], [1, 128]]),
                        _ap(ptrC, 0, [[1024, 128], [128, 8], [1, 128]]),
                        ACT.Copy).then_inc(sems["TE"], 1)
                NPJ = 2 * 8 * (T_OWN // 512) + 2 * NT * 2
                for gidx in range(NPJ):
                    s.wait_ge(sems["PJM"], gidx + 1)
                    if gidx >= 4:
                        s.wait_ge(sems["PJD%d" % (gidx % 4)], 16 * (gidx // 4))
                    ps = pPJ[:, (gidx % 2) * 512:(gidx % 2 + 1) * 512]
                    s.activation(qev[:, (gidx % 4) * 512:(gidx % 4 + 1) * 512],
                                 ps, ACT.Copy, scale=SCALE_C).then_inc(sems["PJE"], 1)

            @blk.sync
            def _(sy):
                NPC4 = T_OWN // 512
                gidx = 0
                for sec in range(2):
                    for ct in range(8):
                        for pc in range(NPC4):
                            sy.wait_ge(sems["PJE"], gidx + 1)
                            off = sec * SEC + (128 * ct) * T_OWN + 512 * pc
                            sy.dma_start(
                                out=_ap(cc_in, off, [[T_OWN, 128], [1, 512]]),
                                in_=qev[:, (gidx % 4) * 512:(gidx % 4 + 1) * 512]
                            ).then_inc(sems["PJD%d" % (gidx % 4)], 16)
                            gidx += 1
                for sec in range(2, 4):
                    for i in range(NT):
                        for crh in range(2):
                            sy.wait_ge(sems["PJE"], gidx + 1)
                            off = sec * SEC + (128 * i) * D + 512 * crh
                            sy.dma_start(
                                out=_ap(cc_in, off, [[D, 128], [1, 512]]),
                                in_=qev[:, (gidx % 4) * 512:(gidx % 4 + 1) * 512]
                            ).then_inc(sems["PJD%d" % (gidx % 4)], 16)
                            gidx += 1
                for q4 in range(4):
                    sy.wait_ge(sems["PJD%d" % q4], 16 * (gidx // 4))

    y1d = g["y1d"]
    y2d = g["y2d"]
    NSK = S_ALL // 128
    NTC = T_OWN // 512
    NG = NSK * NTC
    with ExitStack() as es3:
        u2t = es3.enter_context(nc.sbuf_tensor("u2t", [128, 2 * S_ALL], BF16))
        v2t = es3.enter_context(nc.sbuf_tensor("v2t", [128, 2 * S_ALL], BF16))
        utk = es3.enter_context(nc.sbuf_tensor("utk", [128, 2 * S_ALL], BF16))
        vtk = es3.enter_context(nc.sbuf_tensor("vtk", [128, 2 * S_ALL], BF16))
        qown = es3.enter_context(nc.sbuf_tensor("qown", [128, 2 * 2 * T_OWN], BF16))
        thsb = es3.enter_context(nc.sbuf_tensor("thsb", [128, 3072], BF16))
        yev = es3.enter_context(nc.sbuf_tensor("yev", [128, 2 * 2 * T_OWN], BF16))
        svc = es3.enter_context(nc.sbuf_tensor("svc", [128, 8], F32))
        if cast_jobs is not None:
            scrW = es3.enter_context(nc.sbuf_tensor("scrW", [128, 4 * 512], BF16))
        pF = es3.enter_context(nc.psum_tensor("pF", [128, 3072], F32))
        pY12 = es3.enter_context(nc.psum_tensor("pY12", [128, 1024], F32))

        with nc.Block() as blk:
            @blk.gpsimd
            def _(gp):
                npj_all = 2 * 8 * (T_OWN // 512) + 4 * NT
                gp.wait_ge(sems["CCS"], 1)
                for q4 in range(4):
                    gp.wait_ge(sems["PJD%d" % q4], 16 * (npj_all // 4))
                gp.collective_compute(
                    "AllGather", ALU.bypass,
                    replica_groups=[[0, 1, 2, 3], [4, 5, 6, 7]],
                    ins=[cc_in[2 * SEC:4 * SEC]], outs=[cc_out[8 * SEC:16 * SEC]],
                ).then_inc(sems["CCS"], 1)
                for hp in range(8):
                    o = (hp % 2) * S_ALL
                    if hp == 0:
                        gp.wait_ge(sems["CCS"], 1)
                    if hp >= 2:
                        gp.wait_ge(sems["FMM"], 2 * NG * (hp - 1) + 1)
                    gp.dma_start(out=_ap(u2t, o, [[2 * S_ALL, 128], [T_OWN, 4], [1, T_OWN]]),
                                 in_=_ap(cc_out, 128 * hp * T_OWN,
                                         [[T_OWN, 128], [2 * SEC, 4], [1, T_OWN]])
                                 ).then_inc(sems["PLDA%d" % (hp % 2)], 16)
                    gp.dma_start(out=_ap(v2t, o, [[2 * S_ALL, 128], [T_OWN, 4], [1, T_OWN]]),
                                 in_=_ap(cc_out, SEC + 128 * hp * T_OWN,
                                         [[T_OWN, 128], [2 * SEC, 4], [1, T_OWN]])
                                 ).then_inc(sems["PLDA%d" % (hp % 2)], 16)
                    if hp == 0:
                        gp.wait_ge(sems["CCS"], 2)
                    for r in range(4):
                        gp.dma_start(out=_ap(utk, o + r * 128 * NT, [[2 * S_ALL, 128], [128, NT], [1, 128]]),
                                     in_=_ap(cc_out, 8 * SEC + 128 * hp + r * 2 * SEC,
                                             [[D, 128], [128 * D, NT], [1, 128]])
                                     ).then_inc(sems["PLD%d" % (hp % 2)], 16)
                        gp.dma_start(out=_ap(vtk, o + r * 128 * NT, [[2 * S_ALL, 128], [128, NT], [1, 128]]),
                                     in_=_ap(cc_out, 8 * SEC + SEC + 128 * hp + r * 2 * SEC,
                                             [[D, 128], [128 * D, NT], [1, 128]])
                                     ).then_inc(sems["PLD%d" % (hp % 2)], 16)
                    for ch in cast_batch(hp):
                        src, dst, r0, c0 = cast_jobs[ch]
                        q = ch % 4
                        if ch >= 4:
                            gp.wait_ge(CWO[q], 16 * (ch // 4))
                        gp.dma_start(out=scrW[:, q * 512:(q + 1) * 512],
                                     in_=src[r0:r0 + 128, c0:c0 + 512]
                                     ).then_inc(CWI[q], 16)

            @blk.sync
            def _(sy):
                for hp in range(8):
                    oq = (hp % 2) * 2 * T_OWN
                    if hp >= 2:
                        sy.wait_ge(sems["FMM"], 2 * NG * (hp - 1))
                    sy.dma_start(out=_ap(qown, oq, [[4 * T_OWN, 128], [1, T_OWN]]),
                                 in_=_ap(cc_in, 128 * hp * T_OWN, [[T_OWN, 128], [1, T_OWN]])
                                 ).then_inc(sems["QLD%d" % (hp % 2)], 16)
                    sy.dma_start(out=_ap(qown, oq + T_OWN, [[4 * T_OWN, 128], [1, T_OWN]]),
                                 in_=_ap(cc_in, SEC + 128 * hp * T_OWN, [[T_OWN, 128], [1, T_OWN]])
                                 ).then_inc(sems["QLD%d" % (hp % 2)], 16)
                    for ch in cast_batch(hp):
                        src, dst, r0, c0 = cast_jobs[ch]
                        q = ch % 4
                        sy.wait_ge(CWI[q], 16 * (ch // 4 + 1))
                        sy.dma_start(out=dst[r0:r0 + 128, c0:c0 + 512],
                                     in_=scrW[:, q * 512:(q + 1) * 512]
                                     ).then_inc(CWO[q], 16)
                    ob = (hp % 2) * 2 * T_OWN
                    sy.wait_ge(sems["YEV"], 2 * NTC * hp + NTC)
                    sy.dma_start(out=_ap(y2d, 128 * hp * T_OWN, [[T_OWN, 128], [1, T_OWN]]),
                                 in_=yev[:, ob:ob + T_OWN]).then_inc(sems["YDM%d" % (hp % 2)], 16)
                    sy.wait_ge(sems["YEV"], 2 * NTC * hp + 2 * NTC)
                    sy.dma_start(out=_ap(y1d, 128 * hp * T_OWN, [[T_OWN, 128], [1, T_OWN]]),
                                 in_=yev[:, ob + T_OWN:ob + 2 * T_OWN]).then_inc(sems["YDM%d" % (hp % 2)], 16)
                sy.wait_ge(sems["YDM0"], 16 * 8)
                sy.wait_ge(sems["YDM1"], 16 * 8)
                if cast_jobs is not None:
                    for q in range(4):
                        sy.wait_ge(CWO[q], 16 * ((len(cast_jobs) + 3 - q) // 4))

            @blk.vector
            def _(v):
                for hp in range(8):
                    o = (hp % 2) * S_ALL
                    ob = (hp % 2) * 2 * T_OWN
                    sv = (hp % 2) * 4
                    v.wait_ge(sems["PLDA%d" % (hp % 2)], 32 * (hp // 2 + 1))
                    v.wait_ge(sems["PLD%d" % (hp % 2)], 128 * (hp // 2 + 1))
                    if hp >= 2:
                        v.wait_ge(sems["YMM"], 2 * (hp - 1))
                    v.reduce_sum(svc[:, sv:sv + 1], v2t[:, o:o + S_ALL], AX.X)
                    v.drain()
                    v.tensor_scalar(svc[:, sv + 1:sv + 2], svc[:, sv:sv + 1], 0.5, None, ALU.mult)
                    v.reduce_sum(svc[:, sv + 2:sv + 3], u2t[:, o:o + S_ALL], AX.X)
                    v.drain()
                    v.tensor_scalar(svc[:, sv + 3:sv + 4], svc[:, sv + 2:sv + 3],
                                    0.5, None, ALU.mult).then_inc(sems["SVC"], 1)
                    v.drain()
                    if hp >= 2:
                        v.wait_ge(sems["YDM%d" % (hp % 2)], 32 * (hp // 2))
                    for sl in range(2):
                        bias = svc[:, sv + 3:sv + 4] if sl == 0 else svc[:, sv + 1:sv + 2]
                        od = ob if sl == 0 else ob + T_OWN
                        for tc in range(NTC):
                            e = (2 * hp + sl) * NTC + tc
                            v.wait_ge(sems["YMM"], 2 * NTC * hp + sl * NTC + tc + 1)
                            v.tensor_scalar(yev[:, od + 512 * tc:od + 512 * (tc + 1)],
                                            pY12[:, (e % 2) * 512:(e % 2) * 512 + 512],
                                            0.5, bias, ALU.mult,
                                            ALU.add).then_inc(sems["YEV"], 1)

            @blk.tensor
            def _(tn):
                for hp in range(8):
                    o = (hp % 2) * S_ALL
                    oq = (hp % 2) * 2 * T_OWN
                    tn.wait_ge(sems["QLD%d" % (hp % 2)], 32 * (hp // 2 + 1))
                    GB = 2 * NG * hp

                    def gparts(gg):
                        gl = gg - GB
                        slab, rem = divmod(gl, NG)
                        tc, sk = divmod(rem, NSK)
                        return gl, slab, tc, sk

                    def emit_y(gg):
                        gl, slab, tc, sk = gparts(gg)
                        tok = utk if slab == 0 else vtk
                        ps = (gg % 3) * 1024
                        e = 2 * hp * NTC + slab * NTC + tc
                        if sk == 0:
                            if gl == 0:
                                tn.wait_ge(sems["PLD%d" % (hp % 2)], 128 * (hp // 2 + 1))
                            if e > 0:
                                tn.wait_ge(sems["YEV"], e)
                        tn.wait_ge(sems["THE"], gg + 1)
                        pyb = pY12[:, (e % 2) * 512:(e % 2) * 512 + 512]
                        tn.matmul(pyb[0:64, :],
                                  tok[:, o + 128 * sk:o + 128 * sk + 64],
                                  thsb[:, ps:ps + 512],
                                  start=(sk == 0), stop=(sk == NSK - 1),
                                  tile_position=(0, 0), skip_group_check=True)
                        ins = tn.matmul(pyb[64:128, :],
                                        tok[:, o + 128 * sk + 64:o + 128 * sk + 128],
                                        thsb[:, ps + 512:ps + 1024],
                                        start=(sk == 0), stop=(sk == NSK - 1),
                                        tile_position=(0, 64), skip_group_check=True)
                        if sk == NSK - 1:
                            ins.then_inc(sems["YMM"], 1)

                    for gl in range(2 * NG):
                        gg = GB + gl
                        _, slab, tc, sk = gparts(gg)
                        at = u2t if slab == 0 else v2t
                        boff = oq + (T_OWN if slab == 0 else 0)
                        ps = (gg % 3) * 1024
                        if gl == 0:
                            tn.wait_ge(sems["PLDA%d" % (hp % 2)], 32 * (hp // 2 + 1))
                        if gg >= 3:
                            tn.wait_ge(sems["THE"], gg - 2)
                        tn.matmul(pF[:, ps:ps + 512],
                                  at[0:64, o + 128 * sk:o + 128 * (sk + 1)],
                                  qown[0:64, boff + 512 * tc:boff + 512 * (tc + 1)],
                                  start=True, stop=True, tile_position=(0, 0))
                        tn.matmul(pF[:, ps + 512:ps + 1024],
                                  at[64:128, o + 128 * sk:o + 128 * (sk + 1)],
                                  qown[64:128, boff + 512 * tc:boff + 512 * (tc + 1)],
                                  start=True, stop=True,
                                  tile_position=(64, 0)).then_inc(sems["FMM"], 1)
                        if gl >= 2:
                            emit_y(gg - 2)
                    emit_y(GB + 2 * NG - 2)
                    emit_y(GB + 2 * NG - 1)

            @blk.scalar
            def _(s):
                for hp in range(8):
                    s.wait_ge(sems["SVC"], hp + 1)
                    for gl in range(2 * NG):
                        gg = 2 * NG * hp + gl
                        ps = (gg % 3) * 1024
                        s.wait_ge(sems["FMM"], gg + 1)
                        s.activation(thsb[:, ps:ps + 1024],
                                     pF[:, ps:ps + 1024],
                                     ACT.Tanh, scale=0.5).then_inc(sems["THE"], 1)

    with ExitStack() as es4:
        y1s = es4.enter_context(nc.sbuf_tensor("y1s", [128, 8 * T_OWN], BF16))
        y2s = es4.enter_context(nc.sbuf_tensor("y2s", [128, 8 * T_OWN], BF16))
        un_sb = es4.enter_context(nc.sbuf_tensor("un_sb", [128, 8 * D], BF16))
        vn_sb = es4.enter_context(nc.sbuf_tensor("vn_sb", [128, 8 * D], BF16))
        epiC = es4.enter_context(nc.sbuf_tensor("epiC", [128, 4 * D], F32))
        lnstC = es4.enter_context(nc.sbuf_tensor("lnstC", [128, 64], F32))
        pGC = es4.enter_context(nc.psum_tensor("pGC", [128, 2048], F32))

        with nc.Block() as blk:
            @blk.gpsimd
            def _(gp):
                gp.dma_start(out=un_sb[:, :].rearrange("p (b e) -> p b e", b=8),
                             in_=u_nat.rearrange("(b p) e -> p b e", p=128)).then_inc(sems["GLD"], 16)
                gp.dma_start(out=vn_sb[:, :].rearrange("p (b e) -> p b e", b=8),
                             in_=v_nat.rearrange("(b p) e -> p b e", p=128)).then_inc(sems["GLD"], 16)
                gp.dma_start(out=y1s[:, :].rearrange("p (k t) -> p k t", k=8),
                             in_=_ap(y1d, 0, [[T_OWN, 128], [128 * T_OWN, 8], [1, T_OWN]])
                             ).then_inc(sems["GLD"], 16)
                gp.dma_start(out=y2s[:, :].rearrange("p (k t) -> p k t", k=8),
                             in_=_ap(y2d, 0, [[T_OWN, 128], [128 * T_OWN, 8], [1, T_OWN]])
                             ).then_inc(sems["GLD"], 16)

            @blk.tensor
            def _(tn):
                tn.wait_ge(sems["GLD"], 64)
                for i in range(NT):
                    for eh in range(2):
                        gidx = 2 * i + eh
                        if gidx >= 4:
                            tn.wait_ge(sems["GEV"], gidx - 3)
                        ps = pGC[:, (gidx % 4) * 512:(gidx % 4 + 1) * 512]
                        for k in range(8):
                            tn.matmul(ps, y1s[:, k * T_OWN + 128 * i: k * T_OWN + 128 * (i + 1)],
                                      un_sb[:, k * D + 512 * eh: k * D + 512 * (eh + 1)],
                                      start=(k == 0), stop=False)
                        for k in range(8):
                            ins = tn.matmul(ps, y2s[:, k * T_OWN + 128 * i: k * T_OWN + 128 * (i + 1)],
                                            vn_sb[:, k * D + 512 * eh: k * D + 512 * (eh + 1)],
                                            start=False, stop=(k == 7))
                        ins.then_inc(sems["GMM"], 1)

            @blk.scalar
            def _(s):
                for i in range(NT):
                    ep = (i % 2) * 2 * D
                    for eh in range(2):
                        gidx = 2 * i + eh
                        s.wait_ge(sems["GMM"], gidx + 1)
                        if i >= 2:
                            s.wait_ge(sems["CEPV"], i - 1)
                        ps = pGC[:, (gidx % 4) * 512:(gidx % 4 + 1) * 512]
                        s.activation(epiC[:, ep + 512 * eh:ep + 512 * (eh + 1)], ps,
                                     ACT.Copy, scale=SCALE_C).then_inc(sems["GEV"], 1)
                    if i > 0:
                        q = i - 1
                        eq = (q % 2) * 2 * D
                        s.wait_ge(sems["CLNV"], i)
                        s.drain()
                        s.activation(epiC[:, eq + D:eq + 2 * D], epiC[:, eq:eq + D],
                                     ACT.Square, bias=lnstC[:, 4 * q:4 * q + 1],
                                     accum_out=lnstC[:, 4 * q + 1:4 * q + 2]).then_inc(sems["CSQS"], 1)
                        s.wait_ge(sems["CVAE"], i)
                        s.activation(lnstC[:, 4 * q + 3:4 * q + 4], lnstC[:, 4 * q + 2:4 * q + 3],
                                     ACT.Sqrt).then_inc(sems["CSQT"], 1)
                q = NT - 1
                eq = (q % 2) * 2 * D
                s.wait_ge(sems["CLNV"], NT)
                s.drain()
                s.activation(epiC[:, eq + D:eq + 2 * D], epiC[:, eq:eq + D],
                             ACT.Square, bias=lnstC[:, 4 * q:4 * q + 1],
                             accum_out=lnstC[:, 4 * q + 1:4 * q + 2]).then_inc(sems["CSQS"], 1)
                s.wait_ge(sems["CVAE"], NT)
                s.activation(lnstC[:, 4 * q + 3:4 * q + 4], lnstC[:, 4 * q + 2:4 * q + 3],
                             ACT.Sqrt).then_inc(sems["CSQT"], 1)

            @blk.vector
            def _(v):
                for i in range(NT):
                    ep = (i % 2) * 2 * D
                    v.wait_ge(sems["GEV"], 2 * i + 2)
                    v.tensor_add(epiC[:, ep:ep + D], stream[:, i * D:(i + 1) * D],
                                 epiC[:, ep:ep + D])
                    v.drain()
                    v.reduce_sum(lnstC[:, 4 * i + 1:4 * i + 2], epiC[:, ep:ep + D], AX.X)
                    v.drain()
                    v.tensor_scalar(lnstC[:, 4 * i:4 * i + 1], lnstC[:, 4 * i + 1:4 * i + 2],
                                    -1.0 / D, None, ALU.mult).then_inc(sems["CLNV"], 1)
                    v.wait_ge(sems["CSQS"], i + 1)
                    v.tensor_scalar(lnstC[:, 4 * i + 2:4 * i + 3], lnstC[:, 4 * i + 1:4 * i + 2],
                                    1.0 / D, EPS, ALU.mult, ALU.add).then_inc(sems["CVAE"], 1)
                    v.wait_ge(sems["CSQT"], i + 1)
                    v.reciprocal(lnstC[:, 4 * i + 1:4 * i + 2], lnstC[:, 4 * i + 3:4 * i + 4])
                    v.drain()
                    v.tensor_mul(lnstC[:, 4 * i + 2:4 * i + 3], lnstC[:, 4 * i:4 * i + 1],
                                 lnstC[:, 4 * i + 1:4 * i + 2])
                    v.drain()
                    v.tensor_scalar(stream[:, i * D:(i + 1) * D], epiC[:, ep:ep + D],
                                    lnstC[:, 4 * i + 1:4 * i + 2],
                                    lnstC[:, 4 * i + 2:4 * i + 3],
                                    ALU.mult, ALU.add).then_inc(sems["CEPV"], 1)
    _clear_sems(nc, sems)



def emit_phase_d(nc, es, g):
    W1B, W2B, CWO = g["W1B"], g["W2B"], g["CWO"]
    b1col, b2rep, wrcol = g["b1col"], g["b2rep"], g["wrcol"]
    cc2_in, cc2_out = g["cc2_in"], g["cc2_out"]
    stream, ident_b, onesr_f, out_hbm = g["stream"], g["ident_b"], g["onesr_f"], g["out_hbm"]
    NBIS = 45
    KSEL = 2 * T_OWN

    sems = {}
    for n in ["LD", "TT", "TE", "SMM", "SEV", "SDM", "CCS", "BLD", "BV", "BT",
              "W1P0", "W1P1", "W2L", "HMM", "HEV", "YMM", "YEV", "GTV",
              "DLNV", "DSQS", "DVAE", "DSQT", "DEPV", "DFIN", "FDM0", "FDM1"]:
        sems[n] = es.enter_context(nc.semaphore("d_" + n))

    x4T = es.enter_context(nc.sbuf_tensor("x4T", [128, 8 * T_OWN], BF16))
    scS = es.enter_context(nc.sbuf_tensor("scS", [128, NT], F32))
    wr_sb = es.enter_context(nc.sbuf_tensor("wr_sb", [128, 8], BF16))
    onesc_f = es.enter_context(nc.sbuf_tensor("onesc_f", [128, 1], F32))
    b1_sb = es.enter_context(nc.sbuf_tensor("b1_sb", [128, 32], F32))
    b2_sb = es.enter_context(nc.sbuf_tensor("b2_sb", [128, D], F32))

    with ExitStack() as es1:
        ptrD = es1.enter_context(nc.psum_tensor("ptrD", [128, 1024], BF16))
        pSC = es1.enter_context(nc.psum_tensor("pSC", [128, 1024], F32))
        with nc.Block() as blk:
            @blk.gpsimd
            def _(gp):
                gp.dma_start(out=wr_sb[:, :], in_=wrcol).then_inc(sems["LD"], 16)
                gp.dma_start(out=b1_sb[:, :], in_=b1col).then_inc(sems["LD"], 16)
                gp.dma_start(out=b2_sb[:, :], in_=b2rep).then_inc(sems["LD"], 16)
                gp.wait_ge(sems["SDM"], 16)
                gp.collective_compute(
                    "AllGather", ALU.bypass,
                    replica_groups=[[0, 1, 2, 3], [4, 5, 6, 7]],
                    ins=[cc2_in[:]], outs=[cc2_out[:]],
                ).then_inc(sems["CCS"], 1)

            @blk.tensor
            def _(tn):
                for i in range(NT):
                    tn.wait_ge(sems["TE"], i)
                    for jb in range(8):
                        ins = tn.transpose(ptrD[:, jb * 128:(jb + 1) * 128],
                                           stream[:, i * D + jb * 128:i * D + (jb + 1) * 128],
                                           ident_b[:, :])
                    ins.then_inc(sems["TT"], 1)
                tn.wait_ge(sems["LD"], 48)
                for i in range(NT):
                    tn.wait_ge(sems["TE"], i + 1)
                    if i >= 2:
                        tn.wait_ge(sems["SEV"], i - 1)
                    ps = pSC[:, (i % 2) * 512:(i % 2) * 512 + 1]
                    for jb in range(8):
                        ins = tn.matmul(ps,
                                        x4T[:, jb * T_OWN + 128 * i: jb * T_OWN + 128 * (i + 1)],
                                        wr_sb[:, jb:jb + 1],
                                        start=(jb == 0), stop=(jb == 7))
                    ins.then_inc(sems["SMM"], 1)

            @blk.scalar
            def _(s):
                for i in range(NT):
                    s.wait_ge(sems["TT"], i + 1)
                    s.activation(
                        _ap(x4T, i * 128, [[8 * T_OWN, 128], [T_OWN, 8], [1, 128]]),
                        _ap(ptrD, 0, [[1024, 128], [128, 8], [1, 128]]),
                        ACT.Copy).then_inc(sems["TE"], 1)

            @blk.vector
            def _(v):
                v.memset(onesc_f[:, 0:1], 1.0)
                for i in range(NT):
                    v.wait_ge(sems["SMM"], i + 1)
                    v.tensor_copy(scS[:, i:i + 1], pSC[:, (i % 2) * 512:(i % 2) * 512 + 1]
                                  ).then_inc(sems["SEV"], 1)

            @blk.sync
            def _(sy):
                sy.wait_ge(sems["SEV"], NT)
                sy.dma_start(out=_ap(cc2_in, 0, [[NT, 128], [1, NT]]),
                             in_=scS[:, :]).then_inc(sems["SDM"], 16)
                sy.wait_ge(sems["SDM"], 16)

    GB_T = min(1024, T_OWN)
    NGRP = T_OWN // GB_T
    NPC = GB_T // 512
    NPTL = GB_T // 128
    NHG = 16 * NPC
    NYG = NPTL * 2

    w1p = es.enter_context(nc.sbuf_tensor("dw1p", [128, 2 * 4096], BF16))
    w2sb = es.enter_context(nc.sbuf_tensor("dw2sb", [128, 16 * 1024], BF16))
    hT = es.enter_context(nc.sbuf_tensor("dhT", [128, 16 * GB_T], BF16))
    acc_bf = es.enter_context(nc.sbuf_tensor("daccb", [128, NT * 1024], BF16))
    ytmp = es.enter_context(nc.sbuf_tensor("dytmp", [128, 512], BF16))
    gat = es.enter_context(nc.sbuf_tensor("gat", [128, NT], F32))
    epiD = es.enter_context(nc.sbuf_tensor("epiD", [128, 4 * D], F32))
    lnstD = es.enter_context(nc.sbuf_tensor("lnstD", [128, 64], F32))
    fin = es.enter_context(nc.sbuf_tensor("fin", [128, 2 * D], F32))
    pH = es.enter_context(nc.psum_tensor("dpH", [128, 1024], F32))
    pY = es.enter_context(nc.psum_tensor("dpY", [128, 1024], F32))

    thr_b = es.enter_context(nc.sbuf_tensor("thr_b", [128, 2], F32))
    scA = es.enter_context(nc.sbuf_tensor("scA", [128, 4 * NT], F32))
    cmp64 = es.enter_context(nc.sbuf_tensor("cmp64", [128, 4 * NT], F32))
    pc = es.enter_context(nc.sbuf_tensor("pc", [128, 2], F32))
    bis = es.enter_context(nc.sbuf_tensor("bis", [1, 8], F32))
    pB = es.enter_context(nc.psum_tensor("pB", [128, 1024], F32))

    bis_ops = [(w, j) for j in range(NBIS) for w in ("cnt", "bct")] + [("fin", 0), ("fin", 1)]
    N_Y0A = max(0, NYG - 8)
    N_H1A = NHG
    N_Y1A = NYG

    def bisPE(tn, op):
        w, j = op
        if w == "cnt":
            tn.wait_ge(sems["BV"], 2 * j + 1)
            tn.matmul(pB[0:1, 0:1], pc[:, 0:1], onesc_f[:, 0:1],
                      start=True, stop=True).then_inc(sems["BT"], 1)
        elif w == "bct":
            tn.wait_ge(sems["BV"], 2 * j + 2)
            tn.matmul(pB[:, 512:513], onesr_f[0:1, :], bis[0:1, 2:3],
                      start=True, stop=True).then_inc(sems["BT"], 1)
        elif j == 0:
            tn.wait_ge(sems["BV"], 2 * NBIS + 1)
            tn.matmul(pB[:, 512:513], onesr_f[0:1, :], bis[0:1, 0:1],
                      start=True, stop=True).then_inc(sems["BT"], 1)

    def bisV(v, op):
        w, j = op
        if w == "cnt":
            if j == 0:
                v.wait_ge(sems["BLD"], 16)
            v.wait_ge(sems["BT"], 2 * j)
            if j > 0:
                v.tensor_copy(thr_b[:, 0:1], pB[:, 512:513])
                v.drain()
            v.tensor_scalar(cmp64[:, :], scA[:, :], thr_b[:, 0:1], None, ALU.is_gt)
            v.drain()
            v.reduce_sum(pc[:, 0:1], cmp64[:, :], AX.X).then_inc(sems["BV"], 1)
        elif w == "bct":
            v.wait_ge(sems["BT"], 2 * j + 1)
            v.tensor_scalar(bis[0:1, 3:4], pB[0:1, 0:1], float(KSEL), None, ALU.is_ge)
            v.drain()
            v.tensor_sub(bis[0:1, 5:6], bis[0:1, 2:3], bis[0:1, 0:1])
            v.tensor_sub(bis[0:1, 6:7], bis[0:1, 1:2], bis[0:1, 2:3])
            v.drain()
            v.scalar_tensor_tensor(bis[0:1, 0:1], bis[0:1, 5:6], bis[0:1, 3:4],
                                   bis[0:1, 0:1], ALU.mult, ALU.add)
            v.scalar_tensor_tensor(bis[0:1, 1:2], bis[0:1, 6:7], bis[0:1, 3:4],
                                   bis[0:1, 2:3], ALU.mult, ALU.add)
            v.drain()
            v.tensor_add(bis[0:1, 4:5], bis[0:1, 0:1], bis[0:1, 1:2])
            v.drain()
            v.tensor_scalar(bis[0:1, 2:3], bis[0:1, 4:5], 0.5, None,
                            ALU.mult).then_inc(sems["BV"], 1)
        elif j == 0:
            v.drain()
            v.tensor_copy(bis[0:1, 5:6], bis[0:1, 0:1]).then_inc(sems["BV"], 1)
        else:
            v.wait_ge(sems["BT"], 2 * NBIS + 1)
            v.tensor_copy(thr_b[:, 0:1], pB[:, 512:513]).then_inc(sems["BV"], 1)

    C = dict(W1P0=0, W1P1=0, W2L=0, HMM=0, HEV=0, YMM=0, YEV=0)

    for grp in range(NGRP):
        base = dict(C)
        last = grp == NGRP - 1
        with nc.Block() as blk:
            @blk.gpsimd
            def _(gp):
                if grp == 0:
                    for q in range(4):
                        gp.wait_ge(CWO[q], 16 * 32)
                for fh in range(2):
                    if fh > 0:
                        gp.wait_ge(sems["YMM"], base["YMM"] + NYG * fh)
                    gp.dma_start(
                        out=w2sb[:, :].rearrange("p (fb c) -> p fb c", fb=16),
                        in_=W2B[2048 * fh:2048 * (fh + 1), :].rearrange("(fb p) c -> p fb c", p=128)
                    ).then_inc(sems["W2L"], 16)
                    for k in range(4):
                        gidx = 4 * fh + k
                        if gidx >= 2:
                            pe, pk = (gidx - 2) // 4, (gidx - 2) % 4
                            gp.wait_ge(sems["HMM"],
                                       base["HMM"] + NHG * pe + 4 * NPC * (pk + 1))
                        buf = gidx % 2
                        gp.dma_start(
                            out=w1p[:, buf * 4096:(buf + 1) * 4096].rearrange(
                                "p (cb f) -> p cb f", cb=8),
                            in_=W1B[:, 2048 * fh + 512 * k:2048 * fh + 512 * (k + 1)].rearrange(
                                "(cb p) f -> p cb f", p=128)
                        ).then_inc(sems["W1P0" if buf == 0 else "W1P1"], 16)

            @blk.tensor
            def _(tn):
                bi = [0]

                def banchor(cond):
                    if last and cond and bi[0] < len(bis_ops):
                        bisPE(tn, bis_ops[bi[0]])
                        bi[0] += 1

                for fh in range(2):
                    tn.wait_ge(sems["W2L"], base["W2L"] + 16 * (fh + 1))
                    for k in range(4):
                        gidx = 4 * fh + k
                        buf = gidx % 2
                        tn.wait_ge(sems["W1P0" if buf == 0 else "W1P1"],
                                   base["W1P0" if buf == 0 else "W1P1"] + 16 * (gidx // 2 + 1))
                        for ft2 in range(4):
                            ft = 4 * k + ft2
                            for pcc in range(NPC):
                                hg = NPC * ft + pcc
                                if NHG * fh + hg >= 2:
                                    tn.wait_ge(sems["HEV"], base["HEV"] + NHG * fh + hg - 1)
                                ph = pH[:, (hg % 2) * 512:(hg % 2 + 1) * 512]
                                for cb in range(8):
                                    ins = tn.matmul(
                                        ph,
                                        w1p[:, buf * 4096 + cb * 512 + 128 * ft2:
                                            buf * 4096 + cb * 512 + 128 * (ft2 + 1)],
                                        x4T[:, cb * T_OWN + GB_T * grp + 512 * pcc:
                                            cb * T_OWN + GB_T * grp + 512 * (pcc + 1)],
                                        start=(cb == 0), stop=(cb == 7))
                                ins.then_inc(sems["HMM"], 1)
                                banchor(fh == 1)
                    tn.wait_ge(sems["HEV"], base["HEV"] + NHG * (fh + 1))
                    for yg in range(NYG):
                        ptl, ch = yg // 2, yg % 2
                        if NYG * fh + yg >= 2:
                            tn.wait_ge(sems["YEV"], base["YEV"] + NYG * fh + yg - 1)
                        py = pY[:, (yg % 2) * 512:(yg % 2 + 1) * 512]
                        for fk in range(16):
                            ins = tn.matmul(
                                py,
                                hT[:, GB_T * fk + 128 * ptl: GB_T * fk + 128 * (ptl + 1)],
                                w2sb[:, 1024 * fk + 512 * ch: 1024 * fk + 512 * (ch + 1)],
                                start=(fk == 0), stop=(fk == 15))
                        ins.then_inc(sems["YMM"], 1)
                        banchor(fh == 1 or yg >= 8)
                if last:
                    while bi[0] < len(bis_ops):
                        bisPE(tn, bis_ops[bi[0]])
                        bi[0] += 1
                    tn.wait_ge(sems["BV"], 2 * NBIS + 2)

            @blk.scalar
            def _(s):
                if grp == 0:
                    s.wait_ge(sems["LD"], 48)
                for fh in range(2):
                    if fh > 0:
                        s.wait_ge(sems["YMM"], base["YMM"] + NYG * fh)
                    for hg in range(NHG):
                        ft, pcc = hg // NPC, hg % NPC
                        s.wait_ge(sems["HMM"], base["HMM"] + NHG * fh + hg + 1)
                        ph = pH[:, (hg % 2) * 512:(hg % 2 + 1) * 512]
                        s.activation(hT[:, GB_T * ft + 512 * pcc: GB_T * ft + 512 * (pcc + 1)],
                                     ph, ACT.Gelu_apprx_tanh,
                                     bias=b1_sb[:, 16 * fh + ft:16 * fh + ft + 1]
                                     ).then_inc(sems["HEV"], 1)

            @blk.vector
            def _(v):
                bi = [0]

                def vanchor(cond):
                    if last and cond:
                        n = 2 if bi[0] == 0 else 1
                        for _ in range(n):
                            if bi[0] < len(bis_ops):
                                bisV(v, bis_ops[bi[0]])
                                bi[0] += 1

                if last:
                    v.memset(bis[0:1, 0:1], -10000.0)
                    v.memset(bis[0:1, 1:2], 10000.0)
                    v.memset(bis[0:1, 2:3], 0.0)
                    v.memset(thr_b[:, 0:1], 0.0)
                    v.drain()
                for fh in range(2):
                    if fh == 1:
                        for _ in range(NHG):
                            vanchor(True)
                    for yg in range(NYG):
                        ptl, ch = yg // 2, yg % 2
                        v.wait_ge(sems["YMM"], base["YMM"] + NYG * fh + yg + 1)
                        py = pY[:, (yg % 2) * 512:(yg % 2 + 1) * 512]
                        a = acc_bf[:, 1024 * (NPTL * grp + ptl) + 512 * ch:
                                   1024 * (NPTL * grp + ptl) + 512 * (ch + 1)]
                        if fh == 0:
                            v.tensor_copy(a, py).then_inc(sems["YEV"], 1)
                        else:
                            v.tensor_copy(ytmp[:, 0:512], py)
                            v.drain()
                            v.tensor_add(a, a, ytmp[:, 0:512]
                                         ).then_inc(sems["YEV"], 1)
                            v.drain()
                        vanchor(fh == 1 or yg >= 8)
                if last:
                    while bi[0] < len(bis_ops):
                        bisV(v, bis_ops[bi[0]])
                        bi[0] += 1

            if last:
                @blk.sync
                def _(sy):
                    sy.wait_ge(sems["CCS"], 1)
                    sy.dma_start(out=scA[:, :].rearrange("p (r i) -> p r i", r=4),
                                 in_=_ap(cc2_out, 0, [[NT, 128], [T_OWN, 4], [1, NT]])
                                 ).then_inc(sems["BLD"], 16)

        C["W2L"] += 32
        C["W1P0"] += 16 * 4
        C["W1P1"] += 16 * 4
        C["HMM"] += NHG * 2
        C["HEV"] += NHG * 2
        C["YMM"] += NYG * 2
        C["YEV"] += NYG * 2

    with nc.Block() as blk:
        @blk.scalar
        def _(s):
            s.activation(epiD[:, 0:NT], scS[:, :], ACT.Tanh, scale=0.5).then_inc(sems["GTV"], 1)

        @blk.vector
        def _(v):
            v.tensor_scalar(gat[:, :], scS[:, :], thr_b[:, 0:1], None, ALU.is_gt)
            v.drain()
            v.wait_ge(sems["GTV"], 1)
            v.tensor_scalar(epiD[:, 0:NT], epiD[:, 0:NT], 0.5, 0.5, ALU.mult, ALU.add)
            v.drain()
            v.tensor_mul(gat[:, :], gat[:, :], epiD[:, 0:NT])
    with nc.Block() as blk:
        @blk.scalar
        def _(s):
            for i in range(NT):
                s.wait_ge(sems["DLNV"], i + 1)
                s.drain()
                eq = (i % 2) * 2 * D
                s.activation(epiD[:, eq + D:eq + 2 * D], epiD[:, eq:eq + D],
                             ACT.Square, bias=lnstD[:, 4 * i:4 * i + 1],
                             accum_out=lnstD[:, 4 * i + 1:4 * i + 2]).then_inc(sems["DSQS"], 1)
                s.wait_ge(sems["DVAE"], i + 1)
                s.activation(lnstD[:, 4 * i + 3:4 * i + 4],
                             lnstD[:, 4 * i + 2:4 * i + 3],
                             ACT.Sqrt).then_inc(sems["DSQT"], 1)

        @blk.gpsimd
        def _(gp):
            for i in range(NT):
                eq = (i % 2) * 2 * D
                gp.wait_ge(sems["DEPV"], i + 1)
                if i >= 2:
                    gp.wait_ge(sems["FDM0" if i % 2 == 0 else "FDM1"],
                               16 * (i // 2))
                gp.tensor_scalar(fin[:, (i % 2) * D:(i % 2 + 1) * D], epiD[:, eq:eq + D],
                                 lnstD[:, 4 * i + 1:4 * i + 2],
                                 lnstD[:, 4 * i + 2:4 * i + 3],
                                 ALU.mult, ALU.add).then_inc(sems["DFIN"], 1)

        @blk.vector
        def _(v):
            for i in range(NT):
                ep = (i % 2) * 2 * D
                if i >= 2:
                    v.wait_ge(sems["DFIN"], i - 1)
                v.tensor_add(epiD[:, ep:ep + D], acc_bf[:, i * 1024:(i + 1) * 1024],
                             b2_sb[:, :])
                v.drain()
                v.tensor_scalar(epiD[:, ep:ep + D], epiD[:, ep:ep + D],
                                gat[:, i:i + 1], None, ALU.mult)
                v.drain()
                v.scalar_tensor_tensor(epiD[:, ep:ep + D], stream[:, i * D:(i + 1) * D],
                                       2.0, epiD[:, ep:ep + D], ALU.mult, ALU.add)
                v.drain()
                v.reduce_sum(lnstD[:, 4 * i + 1:4 * i + 2], epiD[:, ep:ep + D], AX.X)
                v.drain()
                v.tensor_scalar(lnstD[:, 4 * i:4 * i + 1], lnstD[:, 4 * i + 1:4 * i + 2],
                                -1.0 / D, None, ALU.mult).then_inc(sems["DLNV"], 1)
                v.wait_ge(sems["DSQS"], i + 1)
                v.tensor_scalar(lnstD[:, 4 * i + 2:4 * i + 3],
                                lnstD[:, 4 * i + 1:4 * i + 2],
                                1.0 / D, EPS, ALU.mult, ALU.add).then_inc(sems["DVAE"], 1)
                v.wait_ge(sems["DSQT"], i + 1)
                v.reciprocal(lnstD[:, 4 * i + 1:4 * i + 2], lnstD[:, 4 * i + 3:4 * i + 4])
                v.drain()
                v.tensor_mul(lnstD[:, 4 * i + 2:4 * i + 3], lnstD[:, 4 * i:4 * i + 1],
                             lnstD[:, 4 * i + 1:4 * i + 2]).then_inc(sems["DEPV"], 1)

        @blk.sync
        def _(sy):
            for i in range(NT):
                sy.wait_ge(sems["DFIN"], i + 1)
                sy.dma_start(out=out_hbm[i * 128:(i + 1) * 128, :],
                             in_=fin[:, (i % 2) * D:(i % 2 + 1) * D]).then_inc(
                                 sems["FDM0" if i % 2 == 0 else "FDM1"], 16)

    with nc.Block() as blk:
        @blk.gpsimd
        def _(gp):
            gp.wait_ge(sems["FDM0"], 16 * (NT // 2))
            gp.wait_ge(sems["FDM1"], 16 * (NT // 2))



def build_program(n_phases=N_PHASES):
    nc = bass.Bass("TRN2", target_bir_lowering=False)

    x_ext = nc.dram_tensor("x_ext", [1 + T_PREF + T_OWN, D], F32, kind="ExternalInput").ap()
    wwT = nc.dram_tensor("wwT", [D, D], F32, kind="ExternalInput").ap()
    wrT = nc.dram_tensor("wrT", [D, D], F32, kind="ExternalInput").ap()
    decay = nc.dram_tensor("decay", [1, 1], F32, kind="ExternalInput").ap()
    log_alpha = nc.dram_tensor("log_alpha", [1, 1], F32, kind="ExternalInput").ap()
    iota4 = nc.dram_tensor("iota4", [128, 4], F32, kind="ExternalInput").ap()
    gwex4 = nc.dram_tensor("gwex4", [128, 4], F32, kind="ExternalInput").ap()
    diffsT = nc.dram_tensor("diffsT", [SCL, SCL], F32, kind="ExternalInput").ap()
    causalT = nc.dram_tensor("causalT", [SCL, SCL], F32, kind="ExternalInput").ap()
    identity = nc.dram_tensor("identity", [128, 128], F32, kind="ExternalInput").ap()
    onesr = nc.dram_tensor("onesr", [1, 128], F32, kind="ExternalInput").ap()
    if n_phases >= 2:
        gate_io = nc.dram_tensor("gate_io", [D, 8], F32, kind="ExternalInput").ap()
        gammacol = nc.dram_tensor("gammacol", [128, 8], F32, kind="ExternalInput").ap()
        betacol = nc.dram_tensor("betacol", [128, 8], F32, kind="ExternalInput").ap()
        w1_io = nc.dram_tensor("w1_io", [8, D, 2048], F32, kind="ExternalInput").ap()
        w2_io = nc.dram_tensor("w2_io", [8, 2048, D], F32, kind="ExternalInput").ap()
    if n_phases >= 3:
        u_io = nc.dram_tensor("u_io", [D, D], F32, kind="ExternalInput").ap()
        v_io = nc.dram_tensor("v_io", [D, D], F32, kind="ExternalInput").ap()
        u_nat = nc.dram_tensor("u_nat", [D, D], F32, kind="ExternalInput").ap()
        v_nat = nc.dram_tensor("v_nat", [D, D], F32, kind="ExternalInput").ap()
        cc_in = nc.dram_tensor("cc_in", [CCIN_N], BF16)
        cc_out = nc.dram_tensor("cc_out", [4 * CCIN_N], BF16)
        y1d = nc.dram_tensor("y1d", [8 * 128 * T_OWN], BF16)
        y2d = nc.dram_tensor("y2d", [8 * 128 * T_OWN], BF16)
    if n_phases >= 4:
        W1d = nc.dram_tensor("W1d", [D, 4096], F32, kind="ExternalInput").ap()
        W2d = nc.dram_tensor("W2d", [4096, D], F32, kind="ExternalInput").ap()
        b1col = nc.dram_tensor("b1col", [128, 32], F32, kind="ExternalInput").ap()
        b2rep = nc.dram_tensor("b2rep", [128, D], F32, kind="ExternalInput").ap()
        wrcol = nc.dram_tensor("wrcol", [128, 8], F32, kind="ExternalInput").ap()
        cc2_in = nc.dram_tensor("cc2_in", [T_OWN], F32)
        cc2_out = nc.dram_tensor("cc2_out", [4 * T_OWN], F32)
        W1B = nc.dram_tensor("w1bf", [D, 4096], BF16).ap()
        W2B = nc.dram_tensor("w2bf", [4096, D], BF16).ap()
    out_hbm = nc.dram_tensor("out", [T_OWN, D], F32, kind="ExternalOutput").ap()

    with ExitStack() as es:
        stream = es.enter_context(nc.sbuf_tensor("stream", [128, NT * D], BF16))
        ident_b = es.enter_context(nc.sbuf_tensor("ident_b", [128, 128], BF16))
        onesr_f = es.enter_context(nc.sbuf_tensor("onesr_f", [1, 128], F32))
        scal = es.enter_context(nc.sbuf_tensor("scal", [128, 16], F32))
        s0_ld = es.enter_context(nc.semaphore("s0_ld"))
        s0_mm = es.enter_context(nc.semaphore("s0_mm"))

        with nc.Block() as blk:
            c_iota4 = nc.sbuf_tensor("c_iota4", [128, 4], F32)
            c_gwex4 = nc.sbuf_tensor("c_gwex4", [128, 4], F32)
            c_two = nc.sbuf_tensor("c_two", [1, 2], F32)
            p_a0 = nc.psum_tensor("p_a0", [128, 8], F32)
            with c_iota4 as ci, c_gwex4 as cg, c_two as ct, p_a0 as pa:
                @blk.gpsimd
                def _(gp):
                    gp.dma_start(out=ident_b[:, :], in_=identity).then_inc(s0_ld, 16)
                    gp.dma_start(out=onesr_f[:, :], in_=onesr).then_inc(s0_ld, 16)
                    gp.dma_start(out=ci[:, :], in_=iota4).then_inc(s0_ld, 16)
                    gp.dma_start(out=cg[:, :], in_=gwex4).then_inc(s0_ld, 16)
                    gp.dma_start(out=ct[0:1, 0:1], in_=decay).then_inc(s0_ld, 16)
                    gp.dma_start(out=ct[0:1, 1:2], in_=log_alpha).then_inc(s0_ld, 16)

                @blk.tensor
                def _(tn):
                    tn.wait_ge(s0_ld, 96)
                    tn.matmul(pa[:, 0:1], onesr_f[0:1, :], ct[0:1, 0:1],
                              start=True, stop=True)
                    tn.matmul(pa[:, 1:2], onesr_f[0:1, :], ct[0:1, 1:2],
                              start=True, stop=True).then_inc(s0_mm, 1)

                @blk.scalar
                def _(s):
                    s.wait_ge(s0_mm, 1)
                    s.copy(scal[:, 0:1], pa[:, 0:1])
                    s.activation(scal[:, 1:2], pa[:, 0:1], ACT.Tanh, scale=0.5)
                    s.activation(scal[:, 3:4], pa[:, 1:2], ACT.Exp)
                    s.drain()
                    s.activation(scal[:, 1:2], scal[:, 1:2], ACT.Copy, scale=0.5, bias=0.5)
                    s.drain()
                    s.activation(scal[:, 2:3], scal[:, 1:2], ACT.Ln)
                    s.drain()
                    s.activation(scal[:, 4:5], scal[:, 2:3], ACT.Exp, scale=float(SCL))
                    for t in range(4):
                        s.activation(scal[:, 5 + t:6 + t], ci[:, t:t + 1], ACT.Exp,
                                     scale=scal[:, 2:3])
                        s.activation(scal[:, 9 + t:10 + t], cg[:, t:t + 1], ACT.Exp,
                                     scale=scal[:, 2:3])

        with ExitStack() as es_a:
            emit_phase_a(nc, es_a, dict(
                x_ext=x_ext, wwT=wwT, wrT=wrT, diffsT=diffsT, causalT=causalT,
                stream=stream, ident_b=ident_b, scal=scal))

        if n_phases >= 2:
            with ExitStack() as es_b:
                emit_phase_b(nc, es_b, dict(
                    gate_io=gate_io, gammacol=gammacol, betacol=betacol,
                    w1_io=w1_io, w2_io=w2_io, stream=stream,
                    ident_b=ident_b, onesr_f=onesr_f))

        if n_phases >= 3:
            cast_g = {}
            if n_phases >= 4:
                cast_g = dict(
                    W1E=W1d, W2E=W2d, W1B=W1B, W2B=W2B,
                    CWI=[es.enter_context(nc.semaphore("cwi%d" % q)) for q in range(4)],
                    CWO=[es.enter_context(nc.semaphore("cwo%d" % q)) for q in range(4)])
            with ExitStack() as es_c:
                emit_phase_c(nc, es_c, dict(
                    u_io=u_io, v_io=v_io, u_nat=u_nat, v_nat=v_nat,
                    cc_in=cc_in, cc_out=cc_out, y1d=y1d, y2d=y2d,
                    stream=stream, ident_b=ident_b, **cast_g))

        if n_phases >= 4:
            with ExitStack() as es_d:
                emit_phase_d(nc, es_d, dict(
                    W1B=W1B, W2B=W2B, CWO=cast_g["CWO"],
                    b1col=b1col, b2rep=b2rep, wrcol=wrcol,
                    cc2_in=cc2_in, cc2_out=cc2_out, stream=stream,
                    ident_b=ident_b, onesr_f=onesr_f, out_hbm=out_hbm))


        if n_phases >= 4:
            return nc
        s_fcp = es.enter_context(nc.semaphore("s_fcp"))
        s_fdm0 = es.enter_context(nc.semaphore("s_fdm0"))
        s_fdm1 = es.enter_context(nc.semaphore("s_fdm1"))
        with nc.Block() as blk:
            fin = nc.sbuf_tensor("fin", [128, 2 * D], F32)
            with fin as fb:
                @blk.vector
                def _(v):
                    for i in range(NT):
                        if i >= 2:
                            v.wait_ge(s_fdm0 if i % 2 == 0 else s_fdm1,
                                      16 * (i // 2))
                        fo = (i % 2) * D
                        v.tensor_copy(fb[:, fo:fo + D],
                                      stream[:, i * D:(i + 1) * D]).then_inc(s_fcp, 1)

                @blk.sync
                def _(sy):
                    for i in range(NT):
                        sy.wait_ge(s_fcp, i + 1)
                        fo = (i % 2) * D
                        sy.dma_start(out=out_hbm[i * 128:(i + 1) * 128, :],
                                     in_=fb[:, fo:fo + D]).then_inc(
                                         s_fdm0 if i % 2 == 0 else s_fdm1, 16)

                @blk.gpsimd
                def _(gp):
                    gp.wait_ge(s_fdm0, 16 * (NT // 2))
                    gp.wait_ge(s_fdm1, 16 * (NT // 2))

    return nc



_CONSTS = None


def _consts():
    global _CONSTS
    if _CONSTS is None:
        p = np.arange(128, dtype=np.float32)[:, None]
        iota4 = np.concatenate([p + 128 * t for t in range(4)], axis=1)
        gwex4 = np.concatenate([(SCL - 1) - (128 * t + p) for t in range(4)], axis=1)
        k = np.arange(SCL, dtype=np.float32)
        diffsT = (k[None, :] - 1.0 - k[:, None]).astype(np.float32)
        causalT = (k[None, :] > k[:, None]).astype(np.float32)
        identity = np.eye(128, dtype=np.float32)
        onesr = np.ones((1, 128), np.float32)
        _CONSTS = dict(iota4=np.ascontiguousarray(iota4),
                       gwex4=np.ascontiguousarray(gwex4),
                       diffsT=np.ascontiguousarray(diffsT),
                       causalT=np.ascontiguousarray(causalT),
                       identity=identity, onesr=onesr)
    return _CONSTS


def make_in_maps(inputs):
    x = np.ascontiguousarray(np.asarray(inputs["L0_attn_out"], dtype=np.float32))
    wwT = np.ascontiguousarray(np.asarray(inputs["L0_attn_W_write"], np.float32).T)
    wrT = np.ascontiguousarray(np.asarray(inputs["L0_attn_W_read"], np.float32).T)
    dec = np.asarray(inputs["L0_attn_decay"], np.float32).reshape(1, 1)
    la = np.asarray(inputs["L0_attn_log_alpha"], np.float32).reshape(1, 1)
    cst = _consts()
    extra = {}
    if N_PHASES >= 2:
        extra["gate_io"] = np.ascontiguousarray(np.asarray(inputs["L0_ffn_gate_w"], np.float32).T)
        extra["gammacol"] = np.ascontiguousarray(np.asarray(inputs["L0_ffn_ln_gamma"], np.float32).reshape(8, 128).T)
        extra["betacol"] = np.ascontiguousarray(np.asarray(inputs["L0_ffn_ln_beta"], np.float32).reshape(8, 128).T)
        extra["w1_io"] = np.ascontiguousarray(np.asarray(inputs["L0_ffn_w1"], np.float32).transpose(0, 2, 1))
        extra["w2_io"] = np.ascontiguousarray(np.asarray(inputs["L0_ffn_w2"], np.float32).transpose(0, 2, 1))
    if N_PHASES >= 3:
        tu = np.asarray(inputs["L1_attn_ternary_factor_u"], np.float32)
        tv = np.asarray(inputs["L1_attn_ternary_factor_v"], np.float32)
        extra["u_io"] = np.ascontiguousarray(tu.T)
        extra["v_io"] = np.ascontiguousarray(tv.T)
        extra["u_nat"] = np.ascontiguousarray(tu)
        extra["v_nat"] = np.ascontiguousarray(tv)
    if N_PHASES >= 4:
        extra["W1d"] = np.ascontiguousarray(np.asarray(inputs["L1_ffn_W1"], np.float32))
        extra["W2d"] = np.ascontiguousarray(np.asarray(inputs["L1_ffn_W2"], np.float32))
        extra["b1col"] = np.ascontiguousarray(np.asarray(inputs["L1_ffn_b1"], np.float32).reshape(32, 128).T)
        extra["b2rep"] = np.ascontiguousarray(np.broadcast_to(
            np.asarray(inputs["L1_ffn_b2"], np.float32), (128, D)))
        extra["wrcol"] = np.ascontiguousarray(np.asarray(inputs["L1_ffn_w_router"], np.float32).reshape(8, 128).T)
    in_maps = []
    for c in range(8):
        b, j = c // 4, c % 4
        xe = np.zeros((1 + T_PREF + T_OWN, D), np.float32)
        if j:
            xe[1 + T_PREF - T_OWN * j: 1 + T_PREF] = x[b, :T_OWN * j]
        xe[1 + T_PREF:] = x[b, T_OWN * j: T_OWN * (j + 1)]
        im = dict(x_ext=xe, wwT=wwT, wrT=wrT, decay=dec, log_alpha=la, **cst, **extra)
        in_maps.append(im)
    return in_maps


def kernel(**inputs):
    nc = build_program()
    in_maps = make_in_maps(inputs)
    res = run_bass_kernel_spmd(nc, in_maps, core_ids=list(range(8)))
    out = np.empty((2, 8192, D), np.float32)
    for c in range(8):
        b, j = c // 4, c % 4
        out[b, T_OWN * j: T_OWN * (j + 1)] = res.results[c]["out"]
    return out
